# version 16
# baseline (speedup 1.0000x reference)
import math
import numpy as np
import ml_dtypes
from contextlib import ExitStack

import concourse.bass as bass
import concourse.mybir as mybir
from concourse.bass_utils import run_bass_kernel_spmd

F32 = mybir.dt.float32
BF16 = mybir.dt.bfloat16
AF = mybir.ActivationFunctionType
ALU = mybir.AluOpType
ds = bass.ds

PARANOID = False
S = 8192
DM = 2048
NOC = 16
TG = 512
NTG = S // TG
OWN = 1024
OTG = 2
CSH = 12.0
SCL = 128.0 ** -0.5
LI = 0.8 - 0.6 * math.exp(-0.3)
EPS = 1e-5

ENGS = ("tensor", "scalar", "vector", "sync", "gpsimd")
ESEM = {"tensor": "PE", "scalar": "SC", "vector": "VE", "sync": "dW", "gpsimd": "dG"}


def _f32(a):
    return np.ascontiguousarray(a, dtype=np.float32)


def _tiles(wT, n):
    K = wT.shape[0]
    out = np.empty((n, 128, K // 128, 128), np.float32)
    for j in range(n):
        out[j] = wT[:, j * 128:(j + 1) * 128].reshape(K // 128, 128, 128).transpose(1, 0, 2)
    return _f32(out)


def _swz(a):
    return _f32(np.asarray(a).reshape(NOC, 128, -1).transpose(1, 0, 2))


def host_in_maps(inputs):
    x = np.asarray(inputs["L0_attn_hidden_states"], dtype=np.float32)[0]
    xT = _f32(x.T)
    wqkv = np.asarray(inputs["L0_attn_Wqkv_w"])
    wqkv_b = np.asarray(inputs["L0_attn_Wqkv_b"])
    outw_t = _tiles(_f32(np.asarray(inputs["L0_attn_out_w"]).T), NOC)
    outb = _f32(np.asarray(inputs["L0_attn_out_b"]).reshape(NOC, 128).T)
    sublnw = _f32(np.asarray(inputs["L0_attn_subln_w"]).reshape(2, 128).T)
    lam = {k: _f32(np.asarray(inputs[f"L0_attn_lambda_{k}"]).reshape(128, 1))
           for k in ("q1", "k1", "q2", "k2")}
    wq = np.asarray(inputs["L0_ffn_wq"]); wk = np.asarray(inputs["L0_ffn_wk"])
    wv = np.asarray(inputs["L0_ffn_wv"])
    wo_t = _tiles(_f32(np.asarray(inputs["L0_ffn_wo"]).T), NOC)
    wr_s = _swz(np.asarray(inputs["L0_ffn_w_router"]).T)
    wd_s = _swz(np.asarray(inputs["L0_ffn_w_down"]).T)
    wuT = _f32(np.asarray(inputs["L0_ffn_w_up"]).T)
    ff1_t = _tiles(_f32(np.asarray(inputs["L0_ffn_w_ff1"]).T), 64)
    ff2_t = _tiles(_f32(np.asarray(inputs["L0_ffn_w_ff2"]).T), NOC)
    pv = {k: _f32(np.asarray(inputs["L0_ffn_" + k]).reshape(NOC, 128).T)
          for k in ("ln1_scale", "ln1_bias", "ln2_scale", "ln2_bias", "simple_bias")}

    km = np.arange(128)[:, None]
    tm = np.arange(TG)[None, :]
    masks = np.concatenate(
        [((km + 128 * o) <= tm).astype(np.float32) for o in range(4)],
        axis=1).astype(ml_dtypes.bfloat16)

    maps = []
    for i in range(8):
        h = i; g = h // 2
        rows = np.concatenate([
            np.arange(h * 256, h * 256 + 256),
            np.arange(2048 + g * 256, 2048 + g * 256 + 256),
            np.arange(3072 + g * 256, 3072 + g * 256 + 256)])
        wqkvT = wqkv[rows].T
        bsl = wqkv_b[rows]
        jr = np.arange(2 * i * 128, (2 * i + 2) * 128)
        ffnqkvT = np.concatenate([wq[jr], wk[jr], wv[jr]], axis=0).T
        maps.append({
            "xT": xT, "x_ownT": _f32(xT[:, i * OWN:(i + 1) * OWN]),
            "wqkv_s": _swz(wqkvT), "qkvb": _f32(bsl[:512].reshape(4, 128).T),
            "vbias": _f32(bsl[512:].reshape(1, 256)),
            "outw_t": outw_t, "outb": outb, "sublnw": sublnw,
            "lamq1": lam["q1"], "lamk1": lam["k1"],
            "lamq2": lam["q2"], "lamk2": lam["k2"],
            "ffnqkv_s": _swz(ffnqkvT), "wo_t": wo_t,
            "wr_s": wr_s, "wd_s": wd_s, "wuT": wuT,
            "ff1_t": ff1_t, "ff2_t": ff2_t,
            "ln1s": pv["ln1_scale"], "ln1b": pv["ln1_bias"],
            "ln2s": pv["ln2_scale"], "ln2b": pv["ln2_bias"],
            "sbias": pv["simple_bias"], "masks": np.ascontiguousarray(masks),
        })
    return maps


class Prog:

    def __init__(self, nc, ctx, aliases):
        self.nc = nc
        self.ctx = ctx
        self.q = {e: [] for e in ENGS}
        self.sem = {}
        self.cnt = {}
        for s in ("PE", "SC", "VE", "dW", "dG", "CC"):
            self.sem[s] = ctx.enter_context(nc.semaphore(s))
            self.cnt[s] = 0
        self.writer = {}
        self.readers = {}
        self.waited = {e: {} for e in ENGS}
        self.alias = aliases

    def canon(self, res):
        return self.alias.get(res, (res,))

    def _wait(self, eng, sem, val, skip_eng):
        if skip_eng == eng:
            return
        if self.waited[eng].get(sem, -1) >= val:
            return
        self.waited[eng][sem] = val
        h = self.sem[sem]
        self.q[eng].append(lambda e, h=h, val=val: e.wait_ge(h, val))

    def emit(self, eng, fns, reads=(), writes=(), sem=None, amt=1):
        sem = sem or ESEM[eng]
        if PARANOID:
            for s2, v2 in list(self.cnt.items()):
                if v2 > 0:
                    self._wait(eng, s2, v2, None)
        creads = [c for r in reads for c in self.canon(r)]
        cwrites = [c for r in writes for c in self.canon(r)]
        for r in creads:
            w = self.writer.get(r)
            if w:
                self._wait(eng, w[0], w[1], w[2])
        for r in cwrites:
            w = self.writer.get(r)
            if w:
                self._wait(eng, w[0], w[1], w[2])
            for s2, (v2, sk) in self.readers.get(r, {}).items():
                self._wait(eng, s2, v2, sk)
        self.cnt[sem] += amt
        val = self.cnt[sem]
        h = self.sem[sem]
        assert fns
        for f in fns[:-1]:
            self.q[eng].append(f)
        last = fns[-1]
        self.q[eng].append(lambda e, last=last, h=h, amt=amt: last(e).then_inc(h, amt))
        skip = None
        for r in cwrites:
            self.writer[r] = (sem, val, skip)
            self.readers[r] = {}
        for r in creads:
            d = self.readers.setdefault(r, {})
            old = d.get(sem)
            if old is None or old[0] < val:
                d[sem] = (val, skip)
        return val

    def chan(self, name):
        if name not in self.sem:
            self.sem[name] = self.ctx.enter_context(self.nc.semaphore(name[:31]))
            self.cnt[name] = 0
        return name

    def dma(self, eng, out, in_, reads=(), writes=()):
        key = self.canon(writes[0])[0] if writes else "misc"
        sem = self.chan("D_" + key)
        return self.emit(eng, [lambda e: e.dma_start(out=out, in_=in_)],
                         reads=reads, writes=writes, sem=sem, amt=16)


def build_nc(dbg=()):
    nc = bass.Bass("TRN2", target_bir_lowering=False)
    A = {}
    def din(name, shape, dt=F32):
        A[name] = nc.dram_tensor(name, shape, dt, kind="ExternalInput").ap()
    din("xT", [DM, S]); din("x_ownT", [DM, OWN])
    din("wqkv_s", [128, NOC, 768]); din("qkvb", [128, 4]); din("vbias", [1, 256])
    din("outw_t", [NOC, 128, NOC, 128]); din("outb", [128, NOC]); din("sublnw", [128, 2])
    for k in ("lamq1", "lamk1", "lamq2", "lamk2"):
        din(k, [128, 1])
    din("ffnqkv_s", [128, NOC, 768]); din("wo_t", [NOC, 128, NOC, 128])
    din("wr_s", [128, NOC, 3]); din("wd_s", [128, NOC, 32]); din("wuT", [32, DM])
    din("ff1_t", [64, 128, NOC, 128]); din("ff2_t", [NOC, 128, 64, 128])
    for k in ("ln1s", "ln1b", "ln2s", "ln2b", "sbias"):
        din(k, [128, NOC])
    din("masks", [128, 4 * TG], BF16)
    out_d = nc.dram_tensor("out", [DM, OWN], F32, kind="ExternalOutput").ap()

    ag_o_in = nc.dram_tensor("ag_o_in", [16, 256, TG], BF16, kind="Internal").ap()
    ag_o_out = nc.dram_tensor("ag_o_out", [16, DM, TG], BF16, kind="Internal",
                              addr_space="Shared").ap()
    ag_xn_in = nc.dram_tensor("ag_xn_in", [2, DM, TG], BF16, kind="Internal").ap()
    ag_xn_out = nc.dram_tensor("ag_xn_out", [2, 8 * DM, TG], BF16, kind="Internal",
                               addr_space="Shared").ap()
    ag_cx_in = nc.dram_tensor("ag_cx_in", [16, 256, TG], BF16, kind="Internal").ap()
    ag_cx_out = nc.dram_tensor("ag_cx_out", [16, DM, TG], BF16, kind="Internal",
                               addr_space="Shared").ap()
    y_spill = nc.dram_tensor("y_spill", [DM, OWN], F32, kind="Internal").ap()
    ff1_bf = nc.dram_tensor("ff1_bf", [64, 128, NOC * 128], BF16, kind="Internal").ap()
    ff2_bf = nc.dram_tensor("ff2_bf", [NOC, 128, 64 * 128], BF16, kind="Internal").ap()
    outw_bf = nc.dram_tensor("outw_bf", [NOC, 128, NOC * 128], BF16, kind="Internal").ap()
    wo_bf = nc.dram_tensor("wo_bf", [NOC, 128, NOC * 128], BF16, kind="Internal").ap()

    dbg_d = {}
    for name, shape, dt in (
        ("o", [256, S], BF16), ("y", [DM, OWN], F32), ("xn", [DM, OWN], BF16),
        ("cx", [256, S], BF16), ("k1", [128, 2 * S], BF16),
        ("v1", [128, 64 * 256], BF16), ("q1", [128, 4 * TG], BF16),
        ("ng", [128, 1], F32), ("lg", [128, 4], F32), ("l2", [1, 8], F32),
        ("pt", [128, 2048], BF16), ("c1", [128, 2 * TG], F32),
        ("d1", [1, 2 * TG], F32), ("c2", [128, 2 * TG], F32),
        ("of", [128, 2 * TG], F32), ("os", [128, 2 * TG], BF16),
        ("qg", [128, 4 * TG], BF16),
    ):
        if name in dbg:
            dbg_d[name] = nc.dram_tensor("dbg_" + name, shape, dt,
                                         kind="ExternalOutput").ap()

    ctx = ExitStack()
    e = ctx.enter_context
    R1 = e(nc.sbuf_tensor("R1", [128, 16384], BF16))
    R2 = e(nc.sbuf_tensor("R2", [128, 16384], BF16))
    R3 = e(nc.sbuf_tensor("R3", [128, 16384], BF16))
    R4 = e(nc.sbuf_tensor("R4", [128, 8192], BF16))
    R5 = e(nc.sbuf_tensor("R5", [128, 8192], BF16))
    R7 = e(nc.sbuf_tensor("R7", [128, 1024], BF16))
    R8 = e(nc.sbuf_tensor("R8", [128, 1024], F32))
    R9 = e(nc.sbuf_tensor("R9", [128, 2048], BF16))

    def rv(R, a, b, pat=None, **kw):
        ap = R[:, a:b]
        return ap.rearrange(pat, **kw) if pat else ap

    K_sb = rv(R1, 0, 16384, "p (c s) -> p c s", c=2)
    h_lo = rv(R1, 0, 16384, "p (c t) -> p c t", c=32)
    V_sb = rv(R2, 0, 16384, "p (c s) -> p c s", c=64)
    h_hi = rv(R2, 0, 16384, "p (c t) -> p c t", c=32)
    wtVa = rv(R2, 0, 2048, "p (c k) -> p c k", c=NOC)
    wtVb = rv(R2, 8192, 8192 + 2048, "p (c k) -> p c k", c=NOC)
    wqkv_sb = rv(R3, 0, 12288, "p (c o) -> p c o", c=NOC)
    q_sb = rv(R3, 12288, 14336, "p (c t) -> p c t", c=4)
    PT_sb = rv(R3, 14336, 16384, "p (c t) -> p c t", c=2)
    w2a = rv(R3, 0, 8192, "p (c k) -> p c k", c=64)
    w2b = rv(R3, 8192, 16384, "p (c k) -> p c k", c=64)
    z_sb = R3[0:32, 14336:14848]
    panA = rv(R4, 0, 8192, "p (c t) -> p c t", c=NOC)
    wtPa = rv(R4, 0, 2048, "p (c k) -> p c k", c=NOC)
    wtPb = rv(R4, 4096, 4096 + 2048, "p (c k) -> p c k", c=NOC)
    panB = rv(R5, 0, 8192, "p (c t) -> p c t", c=NOC)
    xn2_sb = rv(R5, 0, 8192, "p (c t) -> p c t", c=NOC)
    ostg = rv(R7, 0, 1024, "p (c t) -> p c t", c=2)
    Mb_sb = rv(R7, 0, 1024, "p (c t) -> p c t", c=2)
    ctx2_sb = rv(R8, 0, 1024, "p (c t) -> p c t", c=2)
    B2_sb = rv(R8, 0, 1024, "p (c t) -> p c t", c=2)
    masks_sb = rv(R9, 0, 2048, "p (c t) -> p c t", c=4)
    wu_sb = R9[0:32, 0:2048]

    F1 = e(nc.sbuf_tensor("F1", [128, NOC, TG], F32))
    F1_bf_view = F1.bitcast(BF16)[:, :, :].rearrange("p c t -> p (c t)")
    ctx1_sb = e(nc.sbuf_tensor("ctx1_sb", [128, 2, TG], F32))
    o_f32 = e(nc.sbuf_tensor("o_f32", [128, 2, TG], F32))
    cast_sb = e(nc.sbuf_tensor("cast_sb", [128, 2, TG], BF16))
    yoc = e(nc.sbuf_tensor("yoc", [128, 2, TG], F32))
    PTf_sb = yoc.bitcast(BF16)
    wr_b = e(nc.sbuf_tensor("wr_b", [128, NOC, 3], BF16))
    wd_b = e(nc.sbuf_tensor("wd_b", [128, NOC, 32], BF16))
    rowf = e(nc.sbuf_tensor("rowf", [1, 8 * TG], F32))
    rowb = e(nc.sbuf_tensor("rowb", [1, 4 * TG], BF16))
    qkvb_sb = e(nc.sbuf_tensor("qkvb_sb", [128, 4], F32))
    vbias_sb = e(nc.sbuf_tensor("vbias_sb", [1, 256], BF16))
    outb_sb = e(nc.sbuf_tensor("outb_sb", [128, NOC], F32))
    parm_sb = e(nc.sbuf_tensor("parm_sb", [128, 5, NOC], F32))
    sublnw2 = e(nc.sbuf_tensor("sublnw2", [128, 2], F32))
    lam_sb = e(nc.sbuf_tensor("lam_sb", [128, 4], F32))
    negl_pp = e(nc.sbuf_tensor("negl_pp", [128, 1], F32))
    ones_bf = e(nc.sbuf_tensor("ones_bf", [128, 1], BF16))
    ones1r_bf = e(nc.sbuf_tensor("ones1r_bf", [1, 128], BF16))
    ones1r_f = e(nc.sbuf_tensor("ones1r_f", [1, 128], F32))
    cexp = e(nc.sbuf_tensor("cexp", [128, 1], F32))
    czero = e(nc.sbuf_tensor("czero", [128, 1], F32))
    zrow = e(nc.sbuf_tensor("zrow", [1, 1], F32))
    epsrow = e(nc.sbuf_tensor("epsrow", [1, 1], F32))

    ALIASES = {
        "V": ("Va", "Vb"),
        "wqkv": ("wqkvA", "wqkvB"),
        "panA": ("panAa", "panAb"),
        "h": ("K", "Va", "Vb"),
        "h_lo": ("K",), "h_hi": ("Va", "Vb"),
        "w2a_": ("wqkvA",), "w2b_": ("wqkvB", "q", "PT0", "PT1"),
        "wtVa_": ("Va",), "wtVb_": ("Vb",),
        "wtPa_": ("panAa",), "wtPb_": ("panAb",),
        "xn2": ("panB",),
        "Mb": ("ostg",),
        "PTf0": ("yoc0",), "PTf1": ("yoc1",),
        "rows": ("rowsA", "rowsB"),
        "rowb": ("rowbA", "rowbB"),
        "B2": ("ctx2",),
        "z": ("PT0",),
        "wu": ("masks",),
    }

    pairA = e(nc.psum_tensor("pairA", [128, 1024], F32))
    pairB = e(nc.psum_tensor("pairB", [128, 1024], F32))
    ctxA = e(nc.psum_tensor("ctxA", [128, 512], F32))
    ctxB = e(nc.psum_tensor("ctxB", [128, 512], F32))
    auxP = e(nc.psum_tensor("auxP", [128, 512], F32))
    bcst = e(nc.psum_tensor("bcst", [128, 512], F32))

    p = Prog(nc, ctx, ALIASES)

    def frow(i):
        return rowf[0:1, i * TG:(i + 1) * TG]
    def brow(i):
        return rowb[0:1, i * TG:(i + 1) * TG]

    p.dma("sync", qkvb_sb[:, :], A["qkvb"], writes=["params"])
    p.dma("gpsimd", vbias_sb[:, :], A["vbias"], writes=["vbias_bf"])
    p.dma("sync", outb_sb[:, :], A["outb"], writes=["params"])
    for j, k in enumerate(("ln1s", "ln1b", "ln2s", "ln2b", "sbias")):
        p.dma("sync", parm_sb[:, j, :], A[k], writes=["params"])
    p.dma("sync", sublnw2[:, :], A["sublnw"], writes=["sublnw_raw"])
    for j, k in enumerate(("lamq1", "lamk1", "lamq2", "lamk2")):
        p.dma("sync", lam_sb[:, j:j + 1], A[k], writes=["lam"])
    p.dma("sync", masks_sb[:, :, :], A["masks"].rearrange("p (o t) -> p o t", o=4),
          writes=["masks"])
    p.dma("gpsimd", wqkv_sb[:, :, :], A["wqkv_s"], writes=["wqkv"])
    p.dma("gpsimd", wr_b[:, :, :], A["wr_s"], writes=["wr"])
    p.dma("gpsimd", wd_b[:, :, :], A["wd_s"], writes=["wd"])

    def memsets(v):
        v.memset(ones_bf[:, :], 1.0)
        v.memset(ones1r_bf[:, :], 1.0)
        v.memset(ones1r_f[:, :], 1.0)
        v.memset(cexp[:, :], -CSH)
        v.memset(czero[:, :], 0.0)
        v.memset(zrow[:, :], 0.0)
        return v.memset(epsrow[:, :], EPS)
    p.emit("vector", [memsets], writes=["consts"])
    p.emit("vector",
           [lambda v: v.tensor_scalar_mul(sublnw2[:, :], sublnw2[:, :], 1.0 - LI)],
           reads=["sublnw_raw"], writes=["sublnw_raw"])

    p.emit("tensor",
           [lambda t: t.matmul(auxP[0:1, 0:1], lam_sb[:, 0:1], lam_sb[:, 1:2],
                               start=True, stop=True)],
           reads=["lam"], writes=["aux"])
    p.emit("tensor",
           [lambda t: t.matmul(bcst[0:1, 0:1], lam_sb[:, 2:3], lam_sb[:, 3:4],
                               start=True, stop=True)],
           reads=["lam"], writes=["bcst"])
    p.emit("scalar",
           [lambda s: s.activation(rowf[0:1, 0:1], auxP[0:1, 0:1], AF.Exp,
                                   bias=zrow[0:1, 0:1], scale=1.0)],
           reads=["aux", "consts"], writes=["rows"])
    p.emit("scalar",
           [lambda s: s.activation(rowf[0:1, 1:2], bcst[0:1, 0:1], AF.Exp,
                                   bias=zrow[0:1, 0:1], scale=1.0)],
           reads=["bcst", "consts"], writes=["rows"])
    p.emit("vector",
           [lambda v: v.tensor_sub(rowf[0:1, 2:3], rowf[0:1, 1:2], rowf[0:1, 0:1])],
           reads=["rows"], writes=["rows"])
    p.emit("vector",
           [lambda v: v.tensor_scalar_add(rowf[0:1, 2:3], rowf[0:1, 2:3], -LI)],
           reads=["rows"], writes=["rows"])
    p.emit("tensor",
           [lambda t: t.matmul(bcst[:, 0:1], ones1r_f[0:1, :], rowf[0:1, 2:3],
                               start=True, stop=True)],
           reads=["rows", "consts"], writes=["bcst"])
    p.emit("vector", [lambda v: v.tensor_copy(negl_pp[:, :], bcst[:, 0:1])],
           reads=["bcst"], writes=["negl"])

    def bcast_row(src_row, hs, lsl, bank_ap, bank_res):
        hi, lo = brow(hs), brow(lsl)
        p.emit("vector", [lambda v: v.tensor_copy(hi, src_row)],
               reads=["rows"], writes=["rowb"])
        p.emit("vector", [lambda v: v.tensor_sub(lo, src_row, hi)],
               reads=["rows", "rowb"], writes=["rowb"])
        p.emit("tensor",
               [lambda t: t.matmul(bank_ap, ones1r_bf[0:1, :], hi, start=True, stop=False),
                lambda t: t.matmul(bank_ap, ones1r_bf[0:1, :], lo, start=False, stop=True)],
               reads=["rowb", "consts"], writes=[bank_res])

    def vres(tile):
        return "Va" if tile < 32 else "Vb"

    stage_jobs = []
    for c in range(16):
        stage_jobs.append((A["ff1_t"].rearrange("n p c k -> n p (c k)")[4 * c:4 * c + 4],
                           ff1_bf[4 * c:4 * c + 4], 4))
    for c in range(NOC):
        stage_jobs.append((A["ff2_t"].rearrange("n p c k -> n p (c k)")[c:c + 1],
                           ff2_bf[c:c + 1], 1))
    for c in range(4):
        stage_jobs.append((A["outw_t"].rearrange("n p c k -> n p (c k)")[4 * c:4 * c + 4],
                           outw_bf[4 * c:4 * c + 4], 4))
        stage_jobs.append((A["wo_t"].rearrange("n p c k -> n p (c k)")[4 * c:4 * c + 4],
                           wo_bf[4 * c:4 * c + 4], 4))
    stage_state = {"i": 0, "par": 0}

    def emit_stage(njobs):
        for _ in range(njobs):
            i = stage_state["i"]
            if i >= len(stage_jobs):
                return
            stage_state["i"] += 1
            srcap, dstap, ntile = stage_jobs[i]
            par = stage_state["par"]; stage_state["par"] ^= 1
            bv = F1_bf_view[:, par * 8192:(par + 1) * 8192]
            p.dma("gpsimd", bv.rearrange("p (n e) -> p n e", n=ntile),
                  srcap.rearrange("n p e -> p n e"),
                  reads=["F1"], writes=["F1", "stgb%d" % par])
            p.dma("sync", dstap.rearrange("n p e -> p n e"),
                  bv.rearrange("p (n e) -> p n e", n=ntile),
                  reads=["F1", "stgb%d" % par], writes=["wstaged"])

    PAIR_RES = [("pairA0", "pairA1"), ("pairB0", "pairB1")]
    PAIR_AP = [pairA, pairB]
    gp = {"par": 0}

    def x_panel_load(tg, buf, bufres):
        for qq in range(4):
            src = A["xT"][qq * 512:(qq + 1) * 512, tg * TG:(tg + 1) * TG]
            p.dma("gpsimd", buf[:, qq * 4:(qq + 1) * 4, :],
                  src.rearrange("(c p) t -> p c t", p=128), writes=[bufres])

    def xn_panel_load(tg, buf, bufres):
        rk, ch = tg // 2, tg % 2
        srcap = ag_xn_out[ch, rk * DM:(rk + 1) * DM, :]
        for qq in range(4):
            p.dma("sync", buf[:, qq * 4:(qq + 1) * 4, :],
                  srcap[qq * 512:(qq + 1) * 512, :].rearrange("(c p) t -> p c t", p=128),
                  reads=[f"ag_xn_out{ch}"], writes=[bufres])

    pidbox = {}

    def dyn_panel_load(tg_own, buf, bufres, src_dram, tag):
        reads = [f"ag_{tag}_out{j}" for j in range(16)]
        for qq in range(4):
            def f(e_, qq=qq, tg_own=tg_own, buf=buf, src_dram=src_dram):
                if "pid" not in pidbox:
                    pidbox["pid"] = e_.partition_id()
                src = src_dram[ds(pidbox["pid"] * 2 + tg_own, 1),
                               qq * 512:(qq + 1) * 512, :]
                return e_.dma_start(out=buf[:, qq * 4:(qq + 1) * 4, :],
                                    in_=src.rearrange("o (c p) t -> p (o c) t", p=128))
            p.emit("gpsimd", [f], reads=reads, writes=[bufres],
                   sem=p.chan("D_" + p.canon(bufres)[0]), amt=16)

    def proj_qk(tg, buf, bufres, with_bias):
        banks = [("pairB0", pairB[:, 0:512]), ("pairB1", pairB[:, 512:1024]),
                 ("pairB0", pairB[:, 0:512]), ("pairB1", pairB[:, 512:1024])]
        for j in range(4):
            res, bank = banks[j]
            fns = [(lambda t, fc=fc, j=j, bank=bank, buf=buf:
                    t.matmul(bank, wqkv_sb[:, fc, j * 128:(j + 1) * 128],
                             buf[:, fc, :], start=(fc == 0), stop=(fc == NOC - 1)))
                   for fc in range(NOC)]
            p.emit("tensor", fns, reads=["wqkv", bufres], writes=[res])
            if j < 2:
                dst, dres = q_sb[:, j, :], "q"
            else:
                dst, dres = K_sb[:, j - 2, tg * TG:(tg + 1) * TG], "K"
            if with_bias:
                p.emit("vector",
                       [lambda v, dst=dst, bank=bank, j=j:
                        v.tensor_scalar_add(dst, bank, qkvb_sb[:, j:j + 1])],
                       reads=[res, "params"], writes=[dres])
            else:
                p.emit("vector",
                       [lambda v, dst=dst, bank=bank: v.tensor_copy(dst, bank)],
                       reads=[res], writes=[dres])

    def proj_v(tg, buf, bufres, with_bias):
        for q4 in range(4):
            res = "pairA0" if q4 % 2 == 0 else "pairA1"
            bank_ap = pairA[:, 0:256] if q4 % 2 == 0 else pairA[:, 512:768]
            fns = [(lambda t, fc=fc, q4=q4, bank_ap=bank_ap, buf=buf:
                    t.matmul(bank_ap, buf[:, fc, q4 * 128:(q4 + 1) * 128],
                             wqkv_sb[:, fc, 512:768], start=(fc == 0),
                             stop=(fc == NOC - 1 and not with_bias)))
                   for fc in range(NOC)]
            if with_bias:
                fns.append(lambda t, bank_ap=bank_ap:
                           t.matmul(bank_ap, ones1r_bf[0:1, :], vbias_sb[0:1, :],
                                    start=False, stop=True))
            p.emit("tensor", fns, reads=["wqkv", bufres, "vbias_bf", "consts"],
                   writes=[res])
            tile = tg * 4 + q4
            p.emit("vector",
                   [lambda v, tile=tile, bank_ap=bank_ap:
                    v.tensor_copy(V_sb[:, tile, :], bank_ap)],
                   reads=[res], writes=[vres(tile)])

    def attn_matrix(tg, m, is_ffn):
        npair = (tg + 1) * 2

        def emit_scores(j):
            par = gp["par"]; gp["par"] ^= 1
            prA = PAIR_AP[par]
            r0, r1 = PAIR_RES[par]
            kb_a, kb_b = 2 * j, 2 * j + 1
            p.emit("tensor",
                   [lambda t, kb=kb_a, m=m, prA=prA:
                    t.matmul(prA[:, 0:512], K_sb[:, m, kb * 128:(kb + 1) * 128],
                             q_sb[:, m, :], start=True, stop=True),
                    lambda t, kb=kb_b, m=m, prA=prA:
                    t.matmul(prA[:, 512:1024], K_sb[:, m, kb * 128:(kb + 1) * 128],
                             q_sb[:, m, :], start=True, stop=True)],
                   reads=["K", "q"], writes=[r0, r1])
            ptres = f"PT{par}"
            p.emit("scalar",
                   [lambda s_, par=par, prA=prA:
                    s_.activation(PT_sb[:, par, :], prA[:, :], AF.Exp,
                                  bias=cexp[:, 0:1], scale=SCL)],
                   reads=[r0, r1, "consts"], writes=[ptres])
            if "pt" in dbg_d and not is_ffn and tg == 3 and m == 0 and j < 2:
                p.dma("sync", dbg_d["pt"][:, j * 1024:(j + 1) * 1024],
                      PT_sb[:, par, :], reads=[ptres], writes=["dbg"])
            if j >= npair - 2:
                off = kb_a - tg * 4
                p.emit("gpsimd",
                       [lambda g, par=par, off=off:
                        g.tensor_mul(PT_sb[:, par, 0:512], PT_sb[:, par, 0:512],
                                     masks_sb[:, off, :]),
                        lambda g, par=par, off=off:
                        g.tensor_mul(PT_sb[:, par, 512:1024], PT_sb[:, par, 512:1024],
                                     masks_sb[:, off + 1, :])],
                       reads=[ptres, "masks"], writes=[ptres],
                       sem=p.chan("PF"), amt=1)
            p.emit("vector",
                   [lambda v, par=par: v.tensor_add(PTf_sb[:, par, 0:512],
                                                    PT_sb[:, par, 0:512],
                                                    PT_sb[:, par, 512:1024])],
                   reads=[ptres], writes=[f"PTf{par}"])
            return par

        def emit_ctx(j, par):
            kb_a, kb_b = 2 * j, 2 * j + 1
            ptres = f"PT{par}"
            first = (j == 0)
            lastp = (j == npair - 1)
            fns = []
            for half, kb in ((0, kb_a), (1, kb_b)):
                st = first and half == 0
                sp = lastp and half == 1
                if is_ffn:
                    fns.append(lambda t, kb=kb, m=m, par=par, half=half, st=st, sp=sp:
                               t.matmul(ctxA[:, :], V_sb[:, kb, m * 128:(m + 1) * 128],
                                        PT_sb[:, par, half * 512:(half + 1) * 512],
                                        start=st, stop=sp))
                else:
                    for ech in range(2):
                        bank = ctxA if ech == 0 else ctxB
                        fns.append(lambda t, kb=kb, ech=ech, par=par, half=half,
                                   st=st, sp=sp, bank=bank:
                                   t.matmul(bank[:, :],
                                            V_sb[:, kb, ech * 128:(ech + 1) * 128],
                                            PT_sb[:, par, half * 512:(half + 1) * 512],
                                            start=st, stop=sp))
            fns.append(lambda t, par=par, first=first, lastp=lastp:
                       t.matmul(auxP[0:1, :], ones_bf[:, 0:1],
                                PTf_sb[:, par, 0:512],
                                start=first, stop=lastp))
            wr = (["ctxA", "aux"] if is_ffn else ["ctxA", "ctxB", "aux"])
            rd = [ptres, f"PTf{par}", "consts", vres(kb_a), vres(kb_b)]
            p.emit("tensor", fns, reads=rd, writes=wr)

        return emit_scores, emit_ctx, npair

    def fr(P, k):
        return rowf[0:1, (4 * P + k) * TG:(4 * P + k + 1) * TG]

    def bw(P, k):
        return rowb[0:1, (2 * P + k) * TG:(2 * P + k + 1) * TG]

    def rres(P):
        return "rowsA" if P == 0 else "rowsB"

    def bres_(P):
        return "rowbA" if P == 0 else "rowbB"

    def seg_rden(P, slots):
        def go():
            p.emit("scalar",
                   [(lambda s_, d=d, r=r: s_.activation(fr(P, r), fr(P, d), AF.Ln,
                                                        bias=zrow[0:1, 0:1], scale=1.0))
                    for d, r in slots],
                   reads=[rres(P), "consts"], writes=[rres(P)])
            p.emit("scalar",
                   [(lambda s_, r=r: s_.activation(fr(P, r), fr(P, r), AF.Exp,
                                                   bias=zrow[0:1, 0:1], scale=-1.0))
                    for d, r in slots],
                   reads=[rres(P), "consts"], writes=[rres(P)])
        return go

    def seg_bcast(P, src_slot):
        def go():
            p.emit("vector", [lambda v: v.tensor_copy(bw(P, 0), fr(P, src_slot))],
                   reads=[rres(P)], writes=[bres_(P)])
            p.emit("vector", [lambda v: v.tensor_sub(bw(P, 1), fr(P, src_slot), bw(P, 0))],
                   reads=[rres(P), bres_(P)], writes=[bres_(P)])
            p.emit("tensor",
                   [lambda t: t.matmul(bcst[:, :], ones1r_bf[0:1, :], bw(P, 0),
                                       start=True, stop=False),
                    lambda t: t.matmul(bcst[:, :], ones1r_bf[0:1, :], bw(P, 1),
                                       start=False, stop=True)],
                   reads=[bres_(P), "consts"], writes=["bcst"])
        return go

    def seg_row_bcast_b(P, bslot):
        def go():
            p.emit("tensor",
                   [lambda t: t.matmul(bcst[:, :], ones1r_bf[0:1, :], bw(P, bslot),
                                       start=True, stop=True)],
                   reads=[bres_(P), "consts"], writes=["bcst"])
        return go

    def evac_ffn_head(tg, m):
        P = tg % 2
        p.emit("vector",
               [lambda v, m=m: v.tensor_copy(o_f32[:, m, :], ctxA[:, :])],
               reads=["ctxA"], writes=["o32"])
        p.emit("vector",
               [lambda v, P=P, m=m: v.tensor_copy(fr(P, 2 * m), auxP[0:1, :])],
               reads=["aux"], writes=[rres(P)])

    def ffn_tail_segs(tg, m, dstD, dstO, tag):
        P = tg % 2
        segs = [seg_rden(P, [(2 * m, 2 * m + 1)]),
                seg_bcast(P, 2 * m + 1)]

        def s_mult():
            p.emit("vector",
                   [lambda v, m=m: v.tensor_mul(ostg[:, m, :], o_f32[:, m, :],
                                                bcst[:, :])],
                   reads=["o32", "bcst"], writes=["ostg"])
        segs.append(s_mult)
        if m == 1:
            def s_store():
                p.dma("sync", dstD[tg].rearrange("(c p) t -> p c t", p=128),
                      ostg[:, :, :], reads=["ostg"], writes=[f"ag_{tag}_in{tg}"])
                p.emit("gpsimd",
                       [lambda g: g.collective_compute(
                           "AllGather", ALU.bypass,
                           replica_groups=[list(range(8))],
                           ins=[dstD[tg]], outs=[dstO[tg]])],
                       reads=[f"ag_{tag}_in{tg}"], writes=[f"ag_{tag}_out{tg}"],
                       sem=p.chan(f"CC{tag}{tg}"), amt=1)
            segs.append(s_store)
        return segs

    def evac_attn1_p1(tg):
        P = tg % 2
        p.emit("vector",
               [lambda v: v.tensor_copy(ctx1_sb[:, 0, :], ctxA[:, :]),
                lambda v: v.tensor_copy(ctx1_sb[:, 1, :], ctxB[:, :])],
               reads=["ctxA", "ctxB"], writes=["ctx1"])
        p.emit("vector",
               [lambda v, P=P: v.tensor_copy(fr(P, 0), auxP[0:1, :])],
               reads=["aux"], writes=[rres(P)])

    def evac_attn1_p2_head(tg):
        P = tg % 2
        p.emit("vector",
               [lambda v: v.tensor_copy(ctx2_sb[:, 0, :], ctxA[:, :]),
                lambda v: v.tensor_copy(ctx2_sb[:, 1, :], ctxB[:, :])],
               reads=["ctxA", "ctxB"], writes=["ctx2"])
        p.emit("vector",
               [lambda v, P=P: v.tensor_copy(fr(P, 1), auxP[0:1, :])],
               reads=["aux"], writes=[rres(P)])

    def attn1_tail_segs(tg):
        P = tg % 2
        segs = [seg_rden(P, [(0, 2), (1, 3)]),
                seg_bcast(P, 2)]

        def s_t1():
            for ech in range(2):
                p.emit("vector",
                       [lambda v, ech=ech: v.tensor_mul(o_f32[:, ech, :],
                                                        ctx1_sb[:, ech, :], bcst[:, :])],
                       reads=["ctx1", "bcst"], writes=["o32"])
        segs.append(s_t1)
        segs.append(seg_bcast(P, 3))

        def s_u2():
            for ech in range(2):
                p.emit("vector",
                       [lambda v, ech=ech: v.tensor_mul(ctx2_sb[:, ech, :],
                                                        ctx2_sb[:, ech, :], bcst[:, :])],
                       reads=["ctx2", "bcst"], writes=["ctx2"])
                p.emit("vector",
                       [lambda v, ech=ech: v.scalar_tensor_tensor(
                           o_f32[:, ech, :], ctx2_sb[:, ech, :], negl_pp[:, 0:1],
                           o_f32[:, ech, :], ALU.mult, ALU.add)],
                       reads=["ctx2", "negl", "o32"], writes=["o32"])
        segs.append(s_u2)

        def s_sq():
            p.emit("vector",
                   [lambda v: v.tensor_copy(cast_sb[:, 0, :], o_f32[:, 0, :]),
                    lambda v: v.tensor_copy(cast_sb[:, 1, :], o_f32[:, 1, :])],
                   reads=["o32"], writes=["cast"])
            p.emit("vector",
                   [lambda v: v.tensor_mul(cast_sb[:, 0, :], cast_sb[:, 0, :],
                                           cast_sb[:, 0, :]),
                    lambda v: v.tensor_mul(cast_sb[:, 1, :], cast_sb[:, 1, :],
                                           cast_sb[:, 1, :])],
                   reads=["cast"], writes=["cast"])
        segs.append(s_sq)

        def s_ssq():
            p.emit("tensor",
                   [lambda t: t.matmul(bcst[0:1, :], ones_bf[:, 0:1], cast_sb[:, 0, :],
                                       start=True, stop=False),
                    lambda t: t.matmul(bcst[0:1, :], ones_bf[:, 0:1], cast_sb[:, 1, :],
                                       start=False, stop=True)],
                   reads=["cast", "consts"], writes=["bcst"])
        segs.append(s_ssq)

        def s_rstd():
            p.emit("scalar",
                   [lambda s_, P=P: s_.activation(fr(P, 2), bcst[0:1, :], AF.Ln,
                                                  bias=epsrow[0:1, 0:1],
                                                  scale=1.0 / 256.0)],
                   reads=["bcst", "consts"], writes=[rres(P)])
            p.emit("scalar",
                   [lambda s_, P=P: s_.activation(fr(P, 2), fr(P, 2), AF.Exp,
                                                  bias=zrow[0:1, 0:1], scale=-0.5)],
                   reads=[rres(P), "consts"], writes=[rres(P)])
        segs.append(s_rstd)
        segs.append(seg_bcast(P, 2))

        def s_subln():
            p.emit("vector",
                   [(lambda v, ech=ech: v.scalar_tensor_tensor(
                       ostg[:, ech, :], o_f32[:, ech, :], sublnw2[:, ech:ech + 1],
                       bcst[:, :], ALU.mult, ALU.mult)) for ech in range(2)],
                   reads=["o32", "bcst", "sublnw_raw"], writes=["ostg"])
        segs.append(s_subln)

        def s_store():
            p.dma("sync", ag_o_in[tg].rearrange("(c p) t -> p c t", p=128),
                  ostg[:, :, :], reads=["ostg"], writes=[f"ag_o_in{tg}"])
            p.emit("gpsimd",
                   [lambda g: g.collective_compute(
                       "AllGather", ALU.bypass,
                       replica_groups=[list(range(8))],
                       ins=[ag_o_in[tg]], outs=[ag_o_out[tg]])],
                   reads=[f"ag_o_in{tg}"], writes=[f"ag_o_out{tg}"],
                   sem=p.chan(f"CCo{tg}"), amt=1)
        segs.append(s_store)
        return segs

    def attn_phase(is_ffn):
        from collections import deque
        pend = deque()

        def hook():
            if pend:
                pend.popleft()()

        for tg in range(2):
            buf, bufres = (panA, "panA") if tg % 2 == 0 else (panB, "panB")
            if is_ffn:
                xn_panel_load(tg, buf, bufres)
            else:
                x_panel_load(tg, buf, bufres)
        for tg in range(NTG):
            buf, bufres = (panA, "panA") if tg % 2 == 0 else (panB, "panB")
            proj_qk(tg, buf, bufres, with_bias=not is_ffn)
            proj_v(tg, buf, bufres, with_bias=not is_ffn)
            if tg + 2 < NTG:
                nbuf, nbres = (panA, "panA") if tg % 2 == 0 else (panB, "panB")
                if is_ffn:
                    xn_panel_load(tg + 2, nbuf, nbres)
                else:
                    x_panel_load(tg + 2, nbuf, nbres)
            if not is_ffn:
                emit_stage(2 if tg < 4 else 5)
            es0, ec0, np0 = attn_matrix(tg, 0, is_ffn)
            es1, ec1, np1 = attn_matrix(tg, 1, is_ffn)
            stream = [(es0, ec0, j) for j in range(np0)] + \
                     [(es1, ec1, j) for j in range(np1)]
            par_prev = stream[0][0](stream[0][2])
            for idx in range(len(stream)):
                par_j = par_prev
                if idx + 1 < len(stream):
                    par_prev = stream[idx + 1][0](stream[idx + 1][2])
                stream[idx][1](stream[idx][2], par_j)
                hook()
                if idx == np0 - 1:
                    while pend:
                        pend.popleft()()
                    if is_ffn:
                        evac_ffn_head(tg, 0)
                        pend.extend(ffn_tail_segs(tg, 0, ag_cx_in, ag_cx_out, "cx"))
                    else:
                        evac_attn1_p1(tg)
            while pend:
                pend.popleft()()
            if is_ffn:
                evac_ffn_head(tg, 1)
                pend.extend(ffn_tail_segs(tg, 1, ag_cx_in, ag_cx_out, "cx"))
            else:
                evac_attn1_p2_head(tg)
                pend.extend(attn1_tail_segs(tg))
        while pend:
            pend.popleft()()

    attn_phase(is_ffn=False)
    emit_stage(len(stage_jobs))

    if "k1" in dbg_d:
        p.dma("gpsimd", dbg_d["k1"], R1[:, :], reads=["K"], writes=["dbg"])
    if "v1" in dbg_d:
        p.dma("gpsimd", dbg_d["v1"], R2[:, :], reads=["V"], writes=["dbg"])
    if "q1" in dbg_d:
        p.dma("gpsimd", dbg_d["q1"], R3[:, 12288:14336], reads=["q"], writes=["dbg"])

    def ln_rows(sum_res, sum_ap, ssq_res, ssq_ap, n):
        p.emit("vector",
               [lambda v: v.tensor_scalar_mul(frow(0), sum_ap, 1.0 / n)],
               reads=[sum_res], writes=["rows"])
        p.emit("vector",
               [lambda v: v.tensor_mul(frow(1), frow(0), frow(0))],
               reads=["rows"], writes=["rows"])
        p.emit("vector",
               [lambda v: v.scalar_tensor_tensor(frow(1), ssq_ap, 1.0 / n, frow(1),
                                                 ALU.mult, ALU.subtract)],
               reads=[ssq_res, "rows"], writes=["rows"])
        p.emit("scalar",
               [lambda s_: s_.activation(frow(2), frow(1), AF.Ln,
                                         bias=epsrow[0:1, 0:1], scale=1.0)],
               reads=["rows", "consts"], writes=["rows"])
        p.emit("scalar",
               [lambda s_: s_.activation(frow(2), frow(2), AF.Exp,
                                         bias=zrow[0:1, 0:1], scale=-0.5)],
               reads=["rows", "consts"], writes=["rows"])

    def mean_rstd_bcast():
        bcast_row(frow(0), 0, 1, bcst[:, :], "bcst")
        p.emit("vector", [lambda v: v.tensor_copy(B2_sb[:, 0, :], bcst[:, :])],
               reads=["bcst"], writes=["B2"])
        bcast_row(frow(2), 2, 3, bcst[:, :], "bcst")
        p.emit("vector", [lambda v: v.tensor_copy(B2_sb[:, 1, :], bcst[:, :])],
               reads=["bcst"], writes=["B2"])

    for tgo in range(OTG):
        dyn_panel_load(tgo, panA, "panA", ag_o_out, "o")
        def c_mm(oc):
            wbuf = wtVa if oc % 2 == 0 else wtVb
            wres = "wtVa_" if oc % 2 == 0 else "wtVb_"
            p.dma("sync", wbuf[:, :, :],
                  outw_bf[oc].rearrange("p (c k) -> p c k", c=NOC),
                  reads=["wstaged"], writes=[wres])
            bank, bres = (ctxA, "ctxA") if oc % 2 == 0 else (ctxB, "ctxB")
            fns = [(lambda t, fc=fc, wbuf=wbuf, bank=bank:
                    t.matmul(bank[:, :], wbuf[:, fc, :], panA[:, fc, :],
                             start=(fc == 0), stop=(fc == NOC - 1)))
                   for fc in range(NOC)]
            p.emit("tensor", fns, reads=[wres, "panA"], writes=[bres])
            xb = yoc[:, oc % 2, :]
            xres = "yoc0" if oc % 2 == 0 else "yoc1"
            p.dma("sync", xb, A["x_ownT"][oc * 128:(oc + 1) * 128,
                                          tgo * TG:(tgo + 1) * TG], writes=[xres])
            return bank, bres, xb, xres

        def c_evac(oc, bank, bres, xb, xres):
            p.emit("vector",
                   [lambda v, oc=oc, bank=bank, xb=xb:
                    v.scalar_tensor_tensor(F1[:, oc, :], bank[:, :],
                                           outb_sb[:, oc:oc + 1], xb,
                                           ALU.add, ALU.add)],
                   reads=[bres, xres, "params"], writes=["F1"])
            p.emit("vector",
                   [lambda v, oc=oc: v.tensor_copy(cast_sb[:, 0, :], F1[:, oc, :])],
                   reads=["F1"], writes=["cast"])
            p.emit("vector",
                   [lambda v: v.tensor_mul(cast_sb[:, 1, :], cast_sb[:, 0, :],
                                           cast_sb[:, 0, :])],
                   reads=["cast"], writes=["cast"])
            p.emit("tensor",
                   [lambda t, oc=oc: t.matmul(auxP[0:1, :], ones_bf[:, 0:1],
                                              cast_sb[:, 0, :], start=(oc == 0),
                                              stop=(oc == NOC - 1)),
                    lambda t, oc=oc: t.matmul(bcst[0:1, :], ones_bf[:, 0:1],
                                              cast_sb[:, 1, :], start=(oc == 0),
                                              stop=(oc == NOC - 1))],
                   reads=["cast", "consts"], writes=["aux", "bcst"])

        prevc = c_mm(0)
        for oc in range(NOC):
            curc = prevc
            if oc + 1 < NOC:
                prevc = c_mm(oc + 1)
            c_evac(oc, *curc)
        ln_rows("aux", auxP[0:1, :], "bcst", bcst[0:1, :], DM)
        mean_rstd_bcast()
        for oc in range(NOC):
            p.emit("vector",
                   [lambda v, oc=oc: v.tensor_sub(F1[:, oc, :], F1[:, oc, :],
                                                  B2_sb[:, 0, :])],
                   reads=["B2", "F1"], writes=["F1"])
            p.emit("vector",
                   [lambda v, oc=oc: v.tensor_mul(F1[:, oc, :], F1[:, oc, :],
                                                  B2_sb[:, 1, :])],
                   reads=["B2", "F1"], writes=["F1"])
            p.emit("vector",
                   [lambda v, oc=oc: v.tensor_scalar(panB[:, oc, :], F1[:, oc, :],
                                                     parm_sb[:, 0, oc:oc + 1],
                                                     parm_sb[:, 1, oc:oc + 1],
                                                     ALU.mult, ALU.add)],
                   reads=["F1", "params"], writes=["panB"])
        p.dma("sync", y_spill[:, tgo * TG:(tgo + 1) * TG].rearrange(
                  "(c p) t -> p c t", p=128), F1[:, :, :],
              reads=["F1"], writes=["y_spill"])
        p.dma("sync", ag_xn_in[tgo].rearrange("(c p) t -> p c t", p=128),
              panB[:, :, :], reads=["panB"], writes=[f"ag_xn_in{tgo}"])
        p.emit("gpsimd",
               [lambda g, tgo=tgo: g.collective_compute(
                   "AllGather", ALU.bypass, replica_groups=[list(range(8))],
                   ins=[ag_xn_in[tgo]], outs=[ag_xn_out[tgo]])],
               reads=[f"ag_xn_in{tgo}"], writes=[f"ag_xn_out{tgo}"],
               sem=p.chan(f"CCxn{tgo}"), amt=1)


    p.dma("gpsimd", wqkv_sb[:, :, :], A["ffnqkv_s"], writes=["wqkv"])
    attn_phase(is_ffn=True)


    p.dma("gpsimd", wu_sb[0:32, :], A["wuT"], writes=["wu"])
    for tgo in range(OTG):
        dyn_panel_load(tgo, panA, "panA", ag_cx_out, "cx")
        for qq in range(4):
            p.dma("gpsimd", panB[:, qq * 4:(qq + 1) * 4, :],
                  y_spill[qq * 512:(qq + 1) * 512, tgo * TG:(tgo + 1) * TG]
                  .rearrange("(c p) t -> p c t", p=128),
                  reads=["y_spill"], writes=["panB"])
        rbank = [("ctxA", ctxA[0:1, :]), ("ctxB", ctxB[0:1, :]), ("aux", auxP[0:1, :])]
        for rj in range(3):
            res, ap_ = rbank[rj]
            fns = [(lambda t, fc=fc, rj=rj, ap_=ap_:
                    t.matmul(ap_, wr_b[:, fc, rj:rj + 1], panB[:, fc, :],
                             start=(fc == 0), stop=(fc == NOC - 1)))
                   for fc in range(NOC)]
            p.emit("tensor", fns, reads=["wr", "panB"], writes=[res])
        p.emit("vector",
               [lambda v: v.tensor_copy(frow(0), ctxA[0:1, :]),
                lambda v: v.tensor_copy(frow(1), ctxB[0:1, :]),
                lambda v: v.tensor_copy(frow(2), auxP[0:1, :])],
               reads=["ctxA", "ctxB", "aux"], writes=["rows"])
        p.emit("vector",
               [lambda v: v.tensor_tensor(frow(3), frow(0), frow(1), ALU.is_ge),
                lambda v: v.tensor_tensor(frow(4), frow(0), frow(2), ALU.is_ge)],
               reads=["rows"], writes=["rows"])
        p.emit("vector", [lambda v: v.tensor_mul(frow(3), frow(3), frow(4))],
               reads=["rows"], writes=["rows"])
        p.emit("vector",
               [lambda v: v.tensor_tensor(frow(4), frow(1), frow(0), ALU.is_gt),
                lambda v: v.tensor_tensor(frow(5), frow(1), frow(2), ALU.is_ge)],
               reads=["rows"], writes=["rows"])
        p.emit("vector", [lambda v: v.tensor_mul(frow(4), frow(4), frow(5))],
               reads=["rows"], writes=["rows"])
        p.emit("vector",
               [lambda v: v.tensor_copy(brow(0), frow(3)),
                lambda v: v.tensor_copy(brow(1), frow(4))],
               reads=["rows"], writes=["rowb"])
        for mj in range(2):
            p.emit("tensor",
                   [lambda t, mj=mj: t.matmul(bcst[:, :], ones1r_bf[0:1, :],
                                              brow(mj), start=True, stop=True)],
                   reads=["rowb", "consts"], writes=["bcst"])
            p.emit("vector",
                   [lambda v, mj=mj: v.tensor_copy(Mb_sb[:, mj, :], bcst[:, :])],
                   reads=["bcst"], writes=["Mb"])
        fns = [(lambda t, fc=fc: t.matmul(auxP[0:32, :], wd_b[:, fc, :],
                                          panB[:, fc, :], start=(fc == 0),
                                          stop=(fc == NOC - 1)))
               for fc in range(NOC)]
        p.emit("tensor", fns, reads=["wd", "panB"], writes=["aux"])
        p.emit("vector", [lambda v: v.tensor_copy(z_sb[0:32, :], auxP[0:32, :])],
               reads=["aux"], writes=["z"])
        def w_mm(oc):
            wbuf = wtVa if oc % 2 == 0 else wtVb
            wres = "wtVa_" if oc % 2 == 0 else "wtVb_"
            p.dma("sync", wbuf[:, :, :],
                  wo_bf[oc].rearrange("p (c k) -> p c k", c=NOC),
                  reads=["wstaged"], writes=[wres])
            bank, bres = (ctxA, "ctxA") if oc % 2 == 0 else (ctxB, "ctxB")
            fns = [(lambda t, fc=fc, wbuf=wbuf, bank=bank:
                    t.matmul(bank[:, :], wbuf[:, fc, :], panA[:, fc, :],
                             start=(fc == 0), stop=(fc == NOC - 1)))
                   for fc in range(NOC)]
            p.emit("tensor", fns, reads=[wres, "panA"], writes=[bres])
            xb = yoc[:, oc % 2, :]
            xres = "yoc0" if oc % 2 == 0 else "yoc1"
            p.dma("sync", xb, y_spill[oc * 128:(oc + 1) * 128,
                                      tgo * TG:(tgo + 1) * TG],
                  reads=["y_spill"], writes=[xres])
            return bank, bres, xb, xres

        def w_evac(oc, bank, bres, xb, xres):
            p.emit("vector",
                   [lambda v, oc=oc, bank=bank, xb=xb:
                    v.tensor_add(F1[:, oc, :], bank[:, :], xb)],
                   reads=[bres, xres], writes=["F1"])
            p.emit("vector",
                   [lambda v, oc=oc: v.tensor_copy(cast_sb[:, 0, :], F1[:, oc, :])],
                   reads=["F1"], writes=["cast"])
            p.emit("vector",
                   [lambda v: v.tensor_mul(cast_sb[:, 1, :], cast_sb[:, 0, :],
                                           cast_sb[:, 0, :])],
                   reads=["cast"], writes=["cast"])
            p.emit("tensor",
                   [lambda t, oc=oc: t.matmul(pairB[0:1, 512:1024], ones_bf[:, 0:1],
                                              cast_sb[:, 0, :], start=(oc == 0),
                                              stop=(oc == NOC - 1)),
                    lambda t, oc=oc: t.matmul(pairB[0:1, 0:512], ones_bf[:, 0:1],
                                              cast_sb[:, 1, :], start=(oc == 0),
                                              stop=(oc == NOC - 1))],
                   reads=["cast", "consts"], writes=["pairB1", "pairB0"])

        prevw = w_mm(0)
        for oc in range(NOC):
            curw = prevw
            if oc + 1 < NOC:
                prevw = w_mm(oc + 1)
            w_evac(oc, *curw)
        ln_rows("pairB1", pairB[0:1, 512:1024], "pairB0", pairB[0:1, 0:512], DM)
        mean_rstd_bcast()
        for oc in range(NOC):
            p.emit("vector",
                   [lambda v, oc=oc: v.tensor_sub(o_f32[:, 0, :], F1[:, oc, :],
                                                  B2_sb[:, 0, :])],
                   reads=["F1", "B2"], writes=["o32"])
            p.emit("vector",
                   [lambda v: v.tensor_mul(o_f32[:, 0, :], o_f32[:, 0, :],
                                           B2_sb[:, 1, :])],
                   reads=["o32", "B2"], writes=["o32"])
            p.emit("vector",
                   [lambda v, oc=oc: v.tensor_scalar(xn2_sb[:, oc, :], o_f32[:, 0, :],
                                                     parm_sb[:, 2, oc:oc + 1],
                                                     parm_sb[:, 3, oc:oc + 1],
                                                     ALU.mult, ALU.add)],
                   reads=["o32", "params"], writes=["xn2"])
        for hc in range(64):
            wbuf = wtPa if hc % 2 == 0 else wtPb
            wres = "wtPa_" if hc % 2 == 0 else "wtPb_"
            p.dma("sync", wbuf[:, :, :],
                  ff1_bf[hc].rearrange("p (c k) -> p c k", c=NOC),
                  reads=["wstaged"], writes=[wres])
            bank, bres = (ctxA, "ctxA") if hc % 2 == 0 else (ctxB, "ctxB")
            fns = [(lambda t, fc=fc, wbuf=wbuf, bank=bank:
                    t.matmul(bank[:, :], wbuf[:, fc, :], xn2_sb[:, fc, :],
                             start=(fc == 0), stop=(fc == NOC - 1)))
                   for fc in range(NOC)]
            p.emit("tensor", fns, reads=[wres, "xn2"], writes=[bres])
            hdst = h_lo[:, hc, :] if hc < 32 else h_hi[:, hc - 32, :]
            hres = "h_lo" if hc < 32 else "h_hi"
            p.emit("scalar",
                   [lambda s_, hdst=hdst, bank=bank:
                    s_.activation(hdst, bank[:, :], AF.Gelu_apprx_tanh,
                                  bias=czero[:, 0:1], scale=1.0)],
                   reads=[bres, "consts"], writes=[hres])
        def f_mm(oc):
            wbuf2 = w2a if oc % 2 == 0 else w2b
            wres2 = "w2a_" if oc % 2 == 0 else "w2b_"
            p.dma("sync", wbuf2[:, :, :],
                  ff2_bf[oc].rearrange("p (c k) -> p c k", c=64),
                  reads=["wstaged"], writes=[wres2])
            bank_ap = [pairA[:, 0:512], pairA[:, 512:1024], ctxA[:, :], ctxB[:, :]][oc % 4]
            bres = ["pairA0", "pairA1", "ctxA", "ctxB"][oc % 4]
            fns = [(lambda t, hc=hc, wbuf2=wbuf2, bank_ap=bank_ap:
                    t.matmul(bank_ap, wbuf2[:, hc, :],
                             (h_lo[:, hc, :] if hc < 32 else h_hi[:, hc - 32, :]),
                             start=(hc == 0), stop=(hc == 63)))
                   for hc in range(64)]
            p.emit("tensor", fns, reads=[wres2, "h_lo", "h_hi"], writes=[bres])
            up_ap = pairB[:, 0:512] if oc % 2 == 0 else pairB[:, 512:1024]
            upres = "pairB0" if oc % 2 == 0 else "pairB1"
            p.emit("tensor",
                   [lambda t, oc=oc, up_ap=up_ap:
                    t.matmul(up_ap, wu_sb[0:32, oc * 128:(oc + 1) * 128],
                             z_sb[0:32, :], start=True, stop=True)],
                   reads=["wu", "z"], writes=[upres])
            xb = yoc[:, oc % 2, :]
            xres = "yoc0" if oc % 2 == 0 else "yoc1"
            p.dma("sync", xb, y_spill[oc * 128:(oc + 1) * 128,
                                      tgo * TG:(tgo + 1) * TG],
                  reads=["y_spill"], writes=[xres])
            return bank_ap, bres, up_ap, upres, xb, xres

        def f_evac(oc, bank_ap, bres, up_ap, upres, xb, xres):
            p.emit("vector",
                   [lambda v, oc=oc, bank_ap=bank_ap:
                    v.tensor_add(F1[:, oc, :], bank_ap, F1[:, oc, :])],
                   reads=[bres, "F1"], writes=["F1"])
            p.emit("vector",
                   [lambda v, oc=oc, xb=xb: v.scalar_tensor_tensor(
                       ctx1_sb[:, 0, :], xb, parm_sb[:, 4, oc:oc + 1], F1[:, oc, :],
                       ALU.add, ALU.subtract)],
                   reads=[xres, "params", "F1"], writes=["ctx1"])
            p.emit("vector",
                   [lambda v: v.tensor_mul(ctx1_sb[:, 0, :], ctx1_sb[:, 0, :],
                                           Mb_sb[:, 0, :])],
                   reads=["ctx1", "Mb"], writes=["ctx1"])
            p.emit("vector",
                   [lambda v, xb=xb, up_ap=up_ap:
                    v.tensor_add(ctx1_sb[:, 1, :], up_ap, xb)],
                   reads=[upres, xres], writes=["ctx1"])
            p.emit("vector",
                   [lambda v, oc=oc: v.tensor_sub(ctx1_sb[:, 1, :], ctx1_sb[:, 1, :],
                                                  F1[:, oc, :])],
                   reads=["ctx1", "F1"], writes=["ctx1"])
            p.emit("vector",
                   [lambda v: v.tensor_mul(ctx1_sb[:, 1, :], ctx1_sb[:, 1, :],
                                           Mb_sb[:, 1, :])],
                   reads=["ctx1", "Mb"], writes=["ctx1"])
            p.emit("vector",
                   [lambda v, oc=oc, xb=xb: v.tensor_add(F1[:, oc, :], F1[:, oc, :], xb)],
                   reads=["F1", xres], writes=["F1"])
            p.emit("vector",
                   [lambda v: v.tensor_add(ctx1_sb[:, 0, :], ctx1_sb[:, 0, :],
                                           ctx1_sb[:, 1, :])],
                   reads=["ctx1"], writes=["ctx1"])
            p.emit("vector",
                   [lambda v, oc=oc: v.tensor_add(F1[:, oc, :], F1[:, oc, :],
                                                  ctx1_sb[:, 0, :])],
                   reads=["F1", "ctx1"], writes=["F1"])
            p.emit("vector",
                   [lambda v, oc=oc: v.tensor_copy(cast_sb[:, 0, :], F1[:, oc, :])],
                   reads=["F1"], writes=["cast"])
            p.emit("vector",
                   [lambda v: v.tensor_mul(cast_sb[:, 1, :], cast_sb[:, 0, :],
                                           cast_sb[:, 0, :])],
                   reads=["cast"], writes=["cast"])
            p.emit("tensor",
                   [lambda t, oc=oc: t.matmul(auxP[0:1, :], ones_bf[:, 0:1],
                                              cast_sb[:, 0, :], start=(oc == 0),
                                              stop=(oc == NOC - 1)),
                    lambda t, oc=oc: t.matmul(bcst[0:1, :], ones_bf[:, 0:1],
                                              cast_sb[:, 1, :], start=(oc == 0),
                                              stop=(oc == NOC - 1))],
                   reads=["cast", "consts"], writes=["aux", "bcst"])

        prevf = f_mm(0)
        for oc in range(NOC):
            curf = prevf
            if oc + 1 < NOC:
                prevf = f_mm(oc + 1)
            f_evac(oc, *curf)
        ln_rows("aux", auxP[0:1, :], "bcst", bcst[0:1, :], DM)
        mean_rstd_bcast()
        for oc in range(NOC):
            p.emit("vector",
                   [lambda v, oc=oc: v.tensor_sub(F1[:, oc, :], F1[:, oc, :],
                                                  B2_sb[:, 0, :])],
                   reads=["B2", "F1"], writes=["F1"])
            p.emit("vector",
                   [lambda v, oc=oc: v.tensor_mul(F1[:, oc, :], F1[:, oc, :],
                                                  B2_sb[:, 1, :])],
                   reads=["B2", "F1"], writes=["F1"])
        p.dma("sync", out_d[:, tgo * TG:(tgo + 1) * TG].rearrange(
                  "(c p) t -> p c t", p=128), F1[:, :, :],
              reads=["F1"], writes=["outd"])

    if "lg" in dbg_d:
        p.dma("sync", dbg_d["lg"], lam_sb[:, :], reads=["lam"], writes=["dbg"])
    if "ng" in dbg_d:
        p.dma("sync", dbg_d["ng"], negl_pp[:, :], reads=["negl"], writes=["dbg"])
    if "y" in dbg_d:
        p.dma("gpsimd", dbg_d["y"], y_spill, reads=["y_spill"], writes=["dbg"])

    fin = {s: v for s, v in p.cnt.items()
           if (s.startswith("D_") or s.startswith("CC")) and v > 0}
    def final(e_):
        for s, v in fin.items():
            if v:
                e_.wait_ge(p.sem[s], v)
    p.q["sync"].append(final)

    with nc.Block() as block:
        @block.tensor
        def _(t):
            for f in p.q["tensor"]:
                f(t)

        @block.scalar
        def _(s_):
            for f in p.q["scalar"]:
                f(s_)

        @block.vector
        def _(v):
            for f in p.q["vector"]:
                f(v)

        @block.sync
        def _(s_):
            for f in p.q["sync"]:
                f(s_)

        @block.gpsimd
        def _(g):
            for f in p.q["gpsimd"]:
                f(g)

    ctx.close()
    return nc


_CACHE = {}


def kernel(**inputs):
    maps = host_in_maps(inputs)
    if "nc" not in _CACHE:
        _CACHE["nc"] = build_nc()
    nc = _CACHE["nc"]
    res = run_bass_kernel_spmd(nc, maps, core_ids=list(range(8)))
    outs = [np.asarray(res.results[i]["out"]) for i in range(8)]
    full = np.concatenate(outs, axis=1)
    return np.ascontiguousarray(full.T).reshape(1, S, DM)


# version 17
# speedup vs baseline: 1.1139x; 1.1139x over previous
import math
import numpy as np
import ml_dtypes
from contextlib import ExitStack

import concourse.bass as bass
import concourse.mybir as mybir
from concourse.bass_utils import run_bass_kernel_spmd

F32 = mybir.dt.float32
BF16 = mybir.dt.bfloat16
AF = mybir.ActivationFunctionType
ALU = mybir.AluOpType
ds = bass.ds

PARANOID = False
S = 8192
DM = 2048
NOC = 16
TG = 512
NTG = S // TG
OWN = 1024
OTG = 2
CSH = 12.0
SCL = 128.0 ** -0.5
LI = 0.8 - 0.6 * math.exp(-0.3)
EPS = 1e-5

ENGS = ("tensor", "scalar", "vector", "sync", "gpsimd")
ESEM = {"tensor": "PE", "scalar": "SC", "vector": "VE", "sync": "dW", "gpsimd": "dG"}


def _f32(a):
    return np.ascontiguousarray(a, dtype=np.float32)


def _tiles(wT, n):
    K = wT.shape[0]
    out = np.empty((n, 128, K // 128, 128), np.float32)
    for j in range(n):
        out[j] = wT[:, j * 128:(j + 1) * 128].reshape(K // 128, 128, 128).transpose(1, 0, 2)
    return _f32(out)


def _swz(a):
    return _f32(np.asarray(a).reshape(NOC, 128, -1).transpose(1, 0, 2))


def host_in_maps(inputs):
    x = np.asarray(inputs["L0_attn_hidden_states"], dtype=np.float32)[0]
    xT = _f32(x.T)
    wqkv = np.asarray(inputs["L0_attn_Wqkv_w"])
    wqkv_b = np.asarray(inputs["L0_attn_Wqkv_b"])
    outw_t = _tiles(_f32(np.asarray(inputs["L0_attn_out_w"]).T), NOC)
    outb = _f32(np.asarray(inputs["L0_attn_out_b"]).reshape(NOC, 128).T)
    sublnw = _f32(np.asarray(inputs["L0_attn_subln_w"]).reshape(2, 128).T)
    lam = {k: _f32(np.asarray(inputs[f"L0_attn_lambda_{k}"]).reshape(128, 1))
           for k in ("q1", "k1", "q2", "k2")}
    wq = np.asarray(inputs["L0_ffn_wq"]); wk = np.asarray(inputs["L0_ffn_wk"])
    wv = np.asarray(inputs["L0_ffn_wv"])
    wo_t = _tiles(_f32(np.asarray(inputs["L0_ffn_wo"]).T), NOC)
    wr_s = _swz(np.asarray(inputs["L0_ffn_w_router"]).T)
    wd_s = _swz(np.asarray(inputs["L0_ffn_w_down"]).T)
    wuT = _f32(np.asarray(inputs["L0_ffn_w_up"]).T)
    ff1_t = _tiles(_f32(np.asarray(inputs["L0_ffn_w_ff1"]).T), 64)
    ff2_t = _tiles(_f32(np.asarray(inputs["L0_ffn_w_ff2"]).T), NOC)
    pv = {k: _f32(np.asarray(inputs["L0_ffn_" + k]).reshape(NOC, 128).T)
          for k in ("ln1_scale", "ln1_bias", "ln2_scale", "ln2_bias", "simple_bias")}

    km = np.arange(128)[:, None]
    tm = np.arange(TG)[None, :]
    masks = np.concatenate(
        [((km + 128 * o) <= tm).astype(np.float32) for o in range(4)],
        axis=1).astype(ml_dtypes.bfloat16)

    maps = []
    for i in range(8):
        h = i; g = h // 2
        rows = np.concatenate([
            np.arange(h * 256, h * 256 + 256),
            np.arange(2048 + g * 256, 2048 + g * 256 + 256),
            np.arange(3072 + g * 256, 3072 + g * 256 + 256)])
        wqkvT = wqkv[rows].T
        bsl = wqkv_b[rows]
        jr = np.arange(2 * i * 128, (2 * i + 2) * 128)
        ffnqkvT = np.concatenate([wq[jr], wk[jr], wv[jr]], axis=0).T
        maps.append({
            "xT": xT, "x_ownT": _f32(xT[:, i * OWN:(i + 1) * OWN]),
            "wqkv_s": _swz(wqkvT), "qkvb": _f32(bsl[:512].reshape(4, 128).T),
            "vbias": _f32(bsl[512:].reshape(1, 256)),
            "outw_t": outw_t, "outb": outb, "sublnw": sublnw,
            "lamq1": lam["q1"], "lamk1": lam["k1"],
            "lamq2": lam["q2"], "lamk2": lam["k2"],
            "ffnqkv_s": _swz(ffnqkvT), "wo_t": wo_t,
            "wr_s": wr_s, "wd_s": wd_s, "wuT": wuT,
            "ff1_t": ff1_t, "ff2_t": ff2_t,
            "ln1s": pv["ln1_scale"], "ln1b": pv["ln1_bias"],
            "ln2s": pv["ln2_scale"], "ln2b": pv["ln2_bias"],
            "sbias": pv["simple_bias"], "masks": np.ascontiguousarray(masks),
        })
    return maps


class Prog:

    def __init__(self, nc, ctx, aliases):
        self.nc = nc
        self.ctx = ctx
        self.q = {e: [] for e in ENGS}
        self.sem = {}
        self.cnt = {}
        for s in ("PE", "SC", "VE", "dW", "dG", "CC"):
            self.sem[s] = ctx.enter_context(nc.semaphore(s))
            self.cnt[s] = 0
        self.writer = {}
        self.readers = {}
        self.waited = {e: {} for e in ENGS}
        self.alias = aliases

    def canon(self, res):
        return self.alias.get(res, (res,))

    def _wait(self, eng, sem, val, skip_eng):
        if skip_eng == eng:
            return
        if self.waited[eng].get(sem, -1) >= val:
            return
        self.waited[eng][sem] = val
        h = self.sem[sem]
        self.q[eng].append(lambda e, h=h, val=val: e.wait_ge(h, val))

    def emit(self, eng, fns, reads=(), writes=(), sem=None, amt=1):
        sem = sem or ESEM[eng]
        if PARANOID:
            for s2, v2 in list(self.cnt.items()):
                if v2 > 0:
                    self._wait(eng, s2, v2, None)
        creads = [c for r in reads for c in self.canon(r)]
        cwrites = [c for r in writes for c in self.canon(r)]
        for r in creads:
            w = self.writer.get(r)
            if w:
                self._wait(eng, w[0], w[1], w[2])
        for r in cwrites:
            w = self.writer.get(r)
            if w:
                self._wait(eng, w[0], w[1], w[2])
            for s2, (v2, sk) in self.readers.get(r, {}).items():
                self._wait(eng, s2, v2, sk)
        self.cnt[sem] += amt
        val = self.cnt[sem]
        h = self.sem[sem]
        assert fns
        for f in fns[:-1]:
            self.q[eng].append(f)
        last = fns[-1]
        self.q[eng].append(lambda e, last=last, h=h, amt=amt: last(e).then_inc(h, amt))
        skip = None
        for r in cwrites:
            self.writer[r] = (sem, val, skip)
            self.readers[r] = {}
        for r in creads:
            d = self.readers.setdefault(r, {})
            old = d.get(sem)
            if old is None or old[0] < val:
                d[sem] = (val, skip)
        return val

    def chan(self, name):
        if name not in self.sem:
            self.sem[name] = self.ctx.enter_context(self.nc.semaphore(name[:31]))
            self.cnt[name] = 0
        return name

    def dma(self, eng, out, in_, reads=(), writes=()):
        key = self.canon(writes[0])[0] if writes else "misc"
        sem = self.chan("D" + eng[0] + "_" + key)
        return self.emit(eng, [lambda e: e.dma_start(out=out, in_=in_)],
                         reads=reads, writes=writes, sem=sem, amt=16)


def build_nc(dbg=()):
    nc = bass.Bass("TRN2", target_bir_lowering=False)
    A = {}
    def din(name, shape, dt=F32):
        A[name] = nc.dram_tensor(name, shape, dt, kind="ExternalInput").ap()
    din("xT", [DM, S]); din("x_ownT", [DM, OWN])
    din("wqkv_s", [128, NOC, 768]); din("qkvb", [128, 4]); din("vbias", [1, 256])
    din("outw_t", [NOC, 128, NOC, 128]); din("outb", [128, NOC]); din("sublnw", [128, 2])
    for k in ("lamq1", "lamk1", "lamq2", "lamk2"):
        din(k, [128, 1])
    din("ffnqkv_s", [128, NOC, 768]); din("wo_t", [NOC, 128, NOC, 128])
    din("wr_s", [128, NOC, 3]); din("wd_s", [128, NOC, 32]); din("wuT", [32, DM])
    din("ff1_t", [64, 128, NOC, 128]); din("ff2_t", [NOC, 128, 64, 128])
    for k in ("ln1s", "ln1b", "ln2s", "ln2b", "sbias"):
        din(k, [128, NOC])
    din("masks", [128, 4 * TG], BF16)
    out_d = nc.dram_tensor("out", [DM, OWN], F32, kind="ExternalOutput").ap()

    ag_o_in = nc.dram_tensor("ag_o_in", [16, 256, TG], BF16, kind="Internal").ap()
    ag_o_out = nc.dram_tensor("ag_o_out", [16, DM, TG], BF16, kind="Internal",
                              addr_space="Shared").ap()
    ag_xn_in = nc.dram_tensor("ag_xn_in", [2, DM, TG], BF16, kind="Internal").ap()
    ag_xn_out = nc.dram_tensor("ag_xn_out", [2, 8 * DM, TG], BF16, kind="Internal",
                               addr_space="Shared").ap()
    ag_cx_in = nc.dram_tensor("ag_cx_in", [16, 256, TG], BF16, kind="Internal").ap()
    ag_cx_out = nc.dram_tensor("ag_cx_out", [16, DM, TG], BF16, kind="Internal",
                               addr_space="Shared").ap()
    y_spill = nc.dram_tensor("y_spill", [DM, OWN], F32, kind="Internal").ap()
    ff1_bf = nc.dram_tensor("ff1_bf", [64, 128, NOC * 128], BF16, kind="Internal").ap()
    ff2_bf = nc.dram_tensor("ff2_bf", [NOC, 128, 64 * 128], BF16, kind="Internal").ap()
    outw_bf = nc.dram_tensor("outw_bf", [NOC, 128, NOC * 128], BF16, kind="Internal").ap()
    wo_bf = nc.dram_tensor("wo_bf", [NOC, 128, NOC * 128], BF16, kind="Internal").ap()

    dbg_d = {}
    for name, shape, dt in (
        ("o", [256, S], BF16), ("y", [DM, OWN], F32), ("xn", [DM, OWN], BF16),
        ("cx", [256, S], BF16), ("k1", [128, 2 * S], BF16),
        ("v1", [128, 64 * 256], BF16), ("q1", [128, 4 * TG], BF16),
        ("ng", [128, 1], F32), ("lg", [128, 4], F32), ("l2", [1, 8], F32),
        ("pt", [128, 2048], BF16), ("c1", [128, 2 * TG], F32),
        ("d1", [1, 2 * TG], F32), ("c2", [128, 2 * TG], F32),
        ("of", [128, 2 * TG], F32), ("os", [128, 2 * TG], BF16),
        ("qg", [128, 4 * TG], BF16),
    ):
        if name in dbg:
            dbg_d[name] = nc.dram_tensor("dbg_" + name, shape, dt,
                                         kind="ExternalOutput").ap()

    ctx = ExitStack()
    e = ctx.enter_context
    R1 = e(nc.sbuf_tensor("R1", [128, 16384], BF16))
    R2 = e(nc.sbuf_tensor("R2", [128, 16384], BF16))
    R3 = e(nc.sbuf_tensor("R3", [128, 16384], BF16))
    R4 = e(nc.sbuf_tensor("R4", [128, 8192], BF16))
    R5 = e(nc.sbuf_tensor("R5", [128, 8192], BF16))
    R7 = e(nc.sbuf_tensor("R7", [128, 1024], BF16))
    R8 = e(nc.sbuf_tensor("R8", [128, 1024], F32))
    R9 = e(nc.sbuf_tensor("R9", [128, 2048], BF16))

    def rv(R, a, b, pat=None, **kw):
        ap = R[:, a:b]
        return ap.rearrange(pat, **kw) if pat else ap

    K_sb = rv(R1, 0, 16384, "p (c s) -> p c s", c=2)
    h_lo = rv(R1, 0, 16384, "p (c t) -> p c t", c=32)
    V_sb = rv(R2, 0, 16384, "p (c s) -> p c s", c=64)
    h_hi = rv(R2, 0, 16384, "p (c t) -> p c t", c=32)
    wtVa = rv(R2, 0, 2048, "p (c k) -> p c k", c=NOC)
    wtVb = rv(R2, 8192, 8192 + 2048, "p (c k) -> p c k", c=NOC)
    wqkv_sb = rv(R3, 0, 12288, "p (c o) -> p c o", c=NOC)
    q_sb = rv(R3, 12288, 14336, "p (c t) -> p c t", c=4)
    PT_sb = rv(R3, 14336, 16384, "p (c t) -> p c t", c=2)
    w2a = rv(R3, 0, 8192, "p (c k) -> p c k", c=64)
    w2b = rv(R3, 8192, 16384, "p (c k) -> p c k", c=64)
    z_sb = R3[0:32, 14336:14848]
    panA = rv(R4, 0, 8192, "p (c t) -> p c t", c=NOC)
    wtPa = rv(R4, 0, 2048, "p (c k) -> p c k", c=NOC)
    wtPb = rv(R4, 4096, 4096 + 2048, "p (c k) -> p c k", c=NOC)
    panB = rv(R5, 0, 8192, "p (c t) -> p c t", c=NOC)
    xn2_sb = rv(R5, 0, 8192, "p (c t) -> p c t", c=NOC)
    ostg = rv(R7, 0, 1024, "p (c t) -> p c t", c=2)
    Mb_sb = rv(R7, 0, 1024, "p (c t) -> p c t", c=2)
    ctx2_sb = rv(R8, 0, 1024, "p (c t) -> p c t", c=2)
    B2_sb = rv(R8, 0, 1024, "p (c t) -> p c t", c=2)
    masks_sb = rv(R9, 0, 2048, "p (c t) -> p c t", c=4)
    wu_sb = R9[0:32, 0:2048]

    F1 = e(nc.sbuf_tensor("F1", [128, NOC, TG], F32))
    F1_bf_view = F1.bitcast(BF16)[:, :, :].rearrange("p c t -> p (c t)")
    ctx1_sb = e(nc.sbuf_tensor("ctx1_sb", [128, 2, TG], F32))
    o_f32 = e(nc.sbuf_tensor("o_f32", [128, 2, TG], F32))
    cast_sb = e(nc.sbuf_tensor("cast_sb", [128, 2, TG], BF16))
    yoc = e(nc.sbuf_tensor("yoc", [128, 2, TG], F32))
    PTf_sb = yoc.bitcast(BF16)
    wr_b = e(nc.sbuf_tensor("wr_b", [128, NOC, 3], BF16))
    wd_b = e(nc.sbuf_tensor("wd_b", [128, NOC, 32], BF16))
    rowf = e(nc.sbuf_tensor("rowf", [1, 8 * TG], F32))
    rowb = e(nc.sbuf_tensor("rowb", [1, 4 * TG], BF16))
    qkvb_sb = e(nc.sbuf_tensor("qkvb_sb", [128, 4], F32))
    vbias_sb = e(nc.sbuf_tensor("vbias_sb", [1, 256], BF16))
    outb_sb = e(nc.sbuf_tensor("outb_sb", [128, NOC], F32))
    parm_sb = e(nc.sbuf_tensor("parm_sb", [128, 5, NOC], F32))
    sublnw2 = e(nc.sbuf_tensor("sublnw2", [128, 2], F32))
    lam_sb = e(nc.sbuf_tensor("lam_sb", [128, 4], F32))
    negl_pp = e(nc.sbuf_tensor("negl_pp", [128, 1], F32))
    ones_bf = e(nc.sbuf_tensor("ones_bf", [128, 1], BF16))
    ones1r_bf = e(nc.sbuf_tensor("ones1r_bf", [1, 128], BF16))
    ones1r_f = e(nc.sbuf_tensor("ones1r_f", [1, 128], F32))
    cexp = e(nc.sbuf_tensor("cexp", [128, 1], F32))
    czero = e(nc.sbuf_tensor("czero", [128, 1], F32))
    zrow = e(nc.sbuf_tensor("zrow", [1, 1], F32))
    epsrow = e(nc.sbuf_tensor("epsrow", [1, 1], F32))

    ALIASES = {
        "V": ("Va", "Vb"),
        "wqkv": ("wqkvA", "wqkvB"),
        "panA": ("panAa", "panAb"),
        "h": ("K", "Va", "Vb"),
        "h_lo": ("K",), "h_hi": ("Va", "Vb"),
        "w2a_": ("wqkvA",), "w2b_": ("wqkvB", "q", "PT0", "PT1"),
        "wtVa_": ("Va",), "wtVb_": ("Vb",),
        "wtPa_": ("panAa",), "wtPb_": ("panAb",),
        "xn2": ("panB",),
        "Mb": ("ostg",),
        "PTf0": ("yoc0",), "PTf1": ("yoc1",),
        "rows": ("rowsA", "rowsB"),
        "rowb": ("rowbA", "rowbB"),
        "B2": ("ctx2",),
        "z": ("PT0",),
        "wu": ("masks",),
    }

    pairA = e(nc.psum_tensor("pairA", [128, 1024], F32))
    pairB = e(nc.psum_tensor("pairB", [128, 1024], F32))
    ctxA = e(nc.psum_tensor("ctxA", [128, 512], F32))
    ctxB = e(nc.psum_tensor("ctxB", [128, 512], F32))
    auxP = e(nc.psum_tensor("auxP", [128, 512], F32))
    bcst = e(nc.psum_tensor("bcst", [128, 512], F32))

    p = Prog(nc, ctx, ALIASES)

    def frow(i):
        return rowf[0:1, i * TG:(i + 1) * TG]
    def brow(i):
        return rowb[0:1, i * TG:(i + 1) * TG]

    p.dma("sync", qkvb_sb[:, :], A["qkvb"], writes=["params"])
    p.dma("gpsimd", vbias_sb[:, :], A["vbias"], writes=["vbias_bf"])
    p.dma("sync", outb_sb[:, :], A["outb"], writes=["params"])
    for j, k in enumerate(("ln1s", "ln1b", "ln2s", "ln2b", "sbias")):
        p.dma("sync", parm_sb[:, j, :], A[k], writes=["params"])
    p.dma("sync", sublnw2[:, :], A["sublnw"], writes=["sublnw_raw"])
    for j, k in enumerate(("lamq1", "lamk1", "lamq2", "lamk2")):
        p.dma("sync", lam_sb[:, j:j + 1], A[k], writes=["lam"])
    p.dma("sync", masks_sb[:, :, :], A["masks"].rearrange("p (o t) -> p o t", o=4),
          writes=["masks"])
    p.dma("gpsimd", wqkv_sb[:, :, :], A["wqkv_s"], writes=["wqkv"])
    p.dma("gpsimd", wr_b[:, :, :], A["wr_s"], writes=["wr"])
    p.dma("gpsimd", wd_b[:, :, :], A["wd_s"], writes=["wd"])

    def memsets(v):
        v.memset(ones_bf[:, :], 1.0)
        v.memset(ones1r_bf[:, :], 1.0)
        v.memset(ones1r_f[:, :], 1.0)
        v.memset(cexp[:, :], -CSH)
        v.memset(czero[:, :], 0.0)
        v.memset(zrow[:, :], 0.0)
        return v.memset(epsrow[:, :], EPS)
    p.emit("vector", [memsets], writes=["consts"])
    p.emit("vector",
           [lambda v: v.tensor_scalar_mul(sublnw2[:, :], sublnw2[:, :], 1.0 - LI)],
           reads=["sublnw_raw"], writes=["sublnw_raw"])

    p.emit("tensor",
           [lambda t: t.matmul(auxP[0:1, 0:1], lam_sb[:, 0:1], lam_sb[:, 1:2],
                               start=True, stop=True)],
           reads=["lam"], writes=["aux"])
    p.emit("tensor",
           [lambda t: t.matmul(bcst[0:1, 0:1], lam_sb[:, 2:3], lam_sb[:, 3:4],
                               start=True, stop=True)],
           reads=["lam"], writes=["bcst"])
    p.emit("scalar",
           [lambda s: s.activation(rowf[0:1, 0:1], auxP[0:1, 0:1], AF.Exp,
                                   bias=zrow[0:1, 0:1], scale=1.0)],
           reads=["aux", "consts"], writes=["rows"])
    p.emit("scalar",
           [lambda s: s.activation(rowf[0:1, 1:2], bcst[0:1, 0:1], AF.Exp,
                                   bias=zrow[0:1, 0:1], scale=1.0)],
           reads=["bcst", "consts"], writes=["rows"])
    p.emit("vector",
           [lambda v: v.tensor_sub(rowf[0:1, 2:3], rowf[0:1, 1:2], rowf[0:1, 0:1])],
           reads=["rows"], writes=["rows"])
    p.emit("vector",
           [lambda v: v.tensor_scalar_add(rowf[0:1, 2:3], rowf[0:1, 2:3], -LI)],
           reads=["rows"], writes=["rows"])
    p.emit("tensor",
           [lambda t: t.matmul(bcst[:, 0:1], ones1r_f[0:1, :], rowf[0:1, 2:3],
                               start=True, stop=True)],
           reads=["rows", "consts"], writes=["bcst"])
    p.emit("vector", [lambda v: v.tensor_copy(negl_pp[:, :], bcst[:, 0:1])],
           reads=["bcst"], writes=["negl"])

    def bcast_row(src_row, hs, lsl, bank_ap, bank_res):
        hi, lo = brow(hs), brow(lsl)
        p.emit("vector", [lambda v: v.tensor_copy(hi, src_row)],
               reads=["rows"], writes=["rowb"])
        p.emit("vector", [lambda v: v.tensor_sub(lo, src_row, hi)],
               reads=["rows", "rowb"], writes=["rowb"])
        p.emit("tensor",
               [lambda t: t.matmul(bank_ap, ones1r_bf[0:1, :], hi, start=True, stop=False),
                lambda t: t.matmul(bank_ap, ones1r_bf[0:1, :], lo, start=False, stop=True)],
               reads=["rowb", "consts"], writes=[bank_res])

    def vres(tile):
        return "Va" if tile < 32 else "Vb"

    stage_jobs = []
    for c in range(16):
        stage_jobs.append((A["ff1_t"].rearrange("n p c k -> n p (c k)")[4 * c:4 * c + 4],
                           ff1_bf[4 * c:4 * c + 4], 4))
    for c in range(NOC):
        stage_jobs.append((A["ff2_t"].rearrange("n p c k -> n p (c k)")[c:c + 1],
                           ff2_bf[c:c + 1], 1))
    for c in range(4):
        stage_jobs.append((A["outw_t"].rearrange("n p c k -> n p (c k)")[4 * c:4 * c + 4],
                           outw_bf[4 * c:4 * c + 4], 4))
        stage_jobs.append((A["wo_t"].rearrange("n p c k -> n p (c k)")[4 * c:4 * c + 4],
                           wo_bf[4 * c:4 * c + 4], 4))
    stage_state = {"i": 0, "par": 0}

    def emit_stage(njobs):
        for _ in range(njobs):
            i = stage_state["i"]
            if i >= len(stage_jobs):
                return
            stage_state["i"] += 1
            srcap, dstap, ntile = stage_jobs[i]
            par = stage_state["par"]; stage_state["par"] ^= 1
            bv = F1_bf_view[:, par * 8192:(par + 1) * 8192]
            p.dma("gpsimd", bv.rearrange("p (n e) -> p n e", n=ntile),
                  srcap.rearrange("n p e -> p n e"),
                  reads=["F1"], writes=["F1", "stgb%d" % par])
            p.dma("sync", dstap.rearrange("n p e -> p n e"),
                  bv.rearrange("p (n e) -> p n e", n=ntile),
                  reads=["F1", "stgb%d" % par], writes=["wstaged"])

    PAIR_RES = [("pairA0", "pairA1"), ("pairB0", "pairB1")]
    PAIR_AP = [pairA, pairB]
    gp = {"par": 0}

    def x_panel_load(tg, buf, bufres):
        for qq in range(4):
            src = A["xT"][qq * 512:(qq + 1) * 512, tg * TG:(tg + 1) * TG]
            p.dma("gpsimd", buf[:, qq * 4:(qq + 1) * 4, :],
                  src.rearrange("(c p) t -> p c t", p=128), writes=[bufres])

    def xn_panel_load(tg, buf, bufres):
        rk, ch = tg // 2, tg % 2
        srcap = ag_xn_out[ch, rk * DM:(rk + 1) * DM, :]
        for qq in range(4):
            p.dma("sync", buf[:, qq * 4:(qq + 1) * 4, :],
                  srcap[qq * 512:(qq + 1) * 512, :].rearrange("(c p) t -> p c t", p=128),
                  reads=[f"ag_xn_out{ch}"], writes=[bufres])

    pidbox = {}

    def dyn_panel_load(tg_own, buf, bufres, src_dram, tag):
        reads = [f"ag_{tag}_out{j}" for j in range(16)]
        for qq in range(4):
            def f(e_, qq=qq, tg_own=tg_own, buf=buf, src_dram=src_dram):
                if "pid" not in pidbox:
                    pidbox["pid"] = e_.partition_id()
                src = src_dram[ds(pidbox["pid"] * 2 + tg_own, 1),
                               qq * 512:(qq + 1) * 512, :]
                return e_.dma_start(out=buf[:, qq * 4:(qq + 1) * 4, :],
                                    in_=src.rearrange("o (c p) t -> p (o c) t", p=128))
            p.emit("gpsimd", [f], reads=reads, writes=[bufres],
                   sem=p.chan("Dg_" + p.canon(bufres)[0]), amt=16)

    def proj_qk(tg, buf, bufres, with_bias):
        banks = [("pairB0", pairB[:, 0:512]), ("pairB1", pairB[:, 512:1024]),
                 ("pairB0", pairB[:, 0:512]), ("pairB1", pairB[:, 512:1024])]
        for j in range(4):
            res, bank = banks[j]
            fns = [(lambda t, fc=fc, j=j, bank=bank, buf=buf:
                    t.matmul(bank, wqkv_sb[:, fc, j * 128:(j + 1) * 128],
                             buf[:, fc, :], start=(fc == 0), stop=(fc == NOC - 1)))
                   for fc in range(NOC)]
            p.emit("tensor", fns, reads=["wqkv", bufres], writes=[res])
            if j < 2:
                dst, dres = q_sb[:, j, :], "q"
            else:
                dst, dres = K_sb[:, j - 2, tg * TG:(tg + 1) * TG], "K"
            if with_bias:
                p.emit("vector",
                       [lambda v, dst=dst, bank=bank, j=j:
                        v.tensor_scalar_add(dst, bank, qkvb_sb[:, j:j + 1])],
                       reads=[res, "params"], writes=[dres])
            else:
                p.emit("vector",
                       [lambda v, dst=dst, bank=bank: v.tensor_copy(dst, bank)],
                       reads=[res], writes=[dres])

    def proj_v(tg, buf, bufres, with_bias):
        for q4 in range(4):
            res = "pairA0" if q4 % 2 == 0 else "pairA1"
            bank_ap = pairA[:, 0:256] if q4 % 2 == 0 else pairA[:, 512:768]
            fns = [(lambda t, fc=fc, q4=q4, bank_ap=bank_ap, buf=buf:
                    t.matmul(bank_ap, buf[:, fc, q4 * 128:(q4 + 1) * 128],
                             wqkv_sb[:, fc, 512:768], start=(fc == 0),
                             stop=(fc == NOC - 1 and not with_bias)))
                   for fc in range(NOC)]
            if with_bias:
                fns.append(lambda t, bank_ap=bank_ap:
                           t.matmul(bank_ap, ones1r_bf[0:1, :], vbias_sb[0:1, :],
                                    start=False, stop=True))
            p.emit("tensor", fns, reads=["wqkv", bufres, "vbias_bf", "consts"],
                   writes=[res])
            tile = tg * 4 + q4
            p.emit("vector",
                   [lambda v, tile=tile, bank_ap=bank_ap:
                    v.tensor_copy(V_sb[:, tile, :], bank_ap)],
                   reads=[res], writes=[vres(tile)])

    def attn_matrix(tg, m, is_ffn):
        npair = (tg + 1) * 2

        def emit_scores(j):
            par = gp["par"]; gp["par"] ^= 1
            prA = PAIR_AP[par]
            r0, r1 = PAIR_RES[par]
            kb_a, kb_b = 2 * j, 2 * j + 1
            p.emit("tensor",
                   [lambda t, kb=kb_a, m=m, prA=prA:
                    t.matmul(prA[:, 0:512], K_sb[:, m, kb * 128:(kb + 1) * 128],
                             q_sb[:, m, :], start=True, stop=True),
                    lambda t, kb=kb_b, m=m, prA=prA:
                    t.matmul(prA[:, 512:1024], K_sb[:, m, kb * 128:(kb + 1) * 128],
                             q_sb[:, m, :], start=True, stop=True)],
                   reads=["K", "q"], writes=[r0, r1])
            ptres = f"PT{par}"
            p.emit("scalar",
                   [lambda s_, par=par, prA=prA:
                    s_.activation(PT_sb[:, par, :], prA[:, :], AF.Exp,
                                  bias=cexp[:, 0:1], scale=SCL)],
                   reads=[r0, r1, "consts"], writes=[ptres])
            if "pt" in dbg_d and not is_ffn and tg == 3 and m == 0 and j < 2:
                p.dma("sync", dbg_d["pt"][:, j * 1024:(j + 1) * 1024],
                      PT_sb[:, par, :], reads=[ptres], writes=["dbg"])
            if j >= npair - 2:
                off = kb_a - tg * 4
                p.emit("vector",
                       [lambda v, par=par, off=off:
                        v.tensor_mul(PT_sb[:, par, 0:512], PT_sb[:, par, 0:512],
                                     masks_sb[:, off, :]),
                        lambda v, par=par, off=off:
                        v.tensor_mul(PT_sb[:, par, 512:1024], PT_sb[:, par, 512:1024],
                                     masks_sb[:, off + 1, :])],
                       reads=[ptres, "masks"], writes=[ptres])
            p.emit("vector",
                   [lambda v, par=par: v.tensor_add(PTf_sb[:, par, 0:512],
                                                    PT_sb[:, par, 0:512],
                                                    PT_sb[:, par, 512:1024])],
                   reads=[ptres], writes=[f"PTf{par}"])
            return par

        def emit_ctx(j, par):
            kb_a, kb_b = 2 * j, 2 * j + 1
            ptres = f"PT{par}"
            first = (j == 0)
            lastp = (j == npair - 1)
            fns = []
            for half, kb in ((0, kb_a), (1, kb_b)):
                st = first and half == 0
                sp = lastp and half == 1
                if is_ffn:
                    fns.append(lambda t, kb=kb, m=m, par=par, half=half, st=st, sp=sp:
                               t.matmul(ctxA[:, :], V_sb[:, kb, m * 128:(m + 1) * 128],
                                        PT_sb[:, par, half * 512:(half + 1) * 512],
                                        start=st, stop=sp))
                else:
                    for ech in range(2):
                        bank = ctxA if ech == 0 else ctxB
                        fns.append(lambda t, kb=kb, ech=ech, par=par, half=half,
                                   st=st, sp=sp, bank=bank:
                                   t.matmul(bank[:, :],
                                            V_sb[:, kb, ech * 128:(ech + 1) * 128],
                                            PT_sb[:, par, half * 512:(half + 1) * 512],
                                            start=st, stop=sp))
            fns.append(lambda t, par=par, first=first, lastp=lastp:
                       t.matmul(auxP[0:1, :], ones_bf[:, 0:1],
                                PTf_sb[:, par, 0:512],
                                start=first, stop=lastp))
            wr = (["ctxA", "aux"] if is_ffn else ["ctxA", "ctxB", "aux"])
            rd = [ptres, f"PTf{par}", "consts", vres(kb_a), vres(kb_b)]
            p.emit("tensor", fns, reads=rd, writes=wr)

        return emit_scores, emit_ctx, npair

    def fr(P, k):
        return rowf[0:1, (4 * P + k) * TG:(4 * P + k + 1) * TG]

    def bw(P, k):
        return rowb[0:1, (2 * P + k) * TG:(2 * P + k + 1) * TG]

    def rres(P):
        return "rowsA" if P == 0 else "rowsB"

    def bres_(P):
        return "rowbA" if P == 0 else "rowbB"

    def seg_rden(P, slots):
        def go():
            p.emit("scalar",
                   [(lambda s_, d=d, r=r: s_.activation(fr(P, r), fr(P, d), AF.Ln,
                                                        bias=zrow[0:1, 0:1], scale=1.0))
                    for d, r in slots],
                   reads=[rres(P), "consts"], writes=[rres(P)])
            p.emit("scalar",
                   [(lambda s_, r=r: s_.activation(fr(P, r), fr(P, r), AF.Exp,
                                                   bias=zrow[0:1, 0:1], scale=-1.0))
                    for d, r in slots],
                   reads=[rres(P), "consts"], writes=[rres(P)])
        return go

    def seg_bcast(P, src_slot):
        def go():
            p.emit("vector", [lambda v: v.tensor_copy(bw(P, 0), fr(P, src_slot))],
                   reads=[rres(P)], writes=[bres_(P)])
            p.emit("vector", [lambda v: v.tensor_sub(bw(P, 1), fr(P, src_slot), bw(P, 0))],
                   reads=[rres(P), bres_(P)], writes=[bres_(P)])
            p.emit("tensor",
                   [lambda t: t.matmul(bcst[:, :], ones1r_bf[0:1, :], bw(P, 0),
                                       start=True, stop=False),
                    lambda t: t.matmul(bcst[:, :], ones1r_bf[0:1, :], bw(P, 1),
                                       start=False, stop=True)],
                   reads=[bres_(P), "consts"], writes=["bcst"])
        return go

    def seg_row_bcast_b(P, bslot):
        def go():
            p.emit("tensor",
                   [lambda t: t.matmul(bcst[:, :], ones1r_bf[0:1, :], bw(P, bslot),
                                       start=True, stop=True)],
                   reads=[bres_(P), "consts"], writes=["bcst"])
        return go

    def evac_ffn_head(tg, m):
        P = tg % 2
        p.emit("vector",
               [lambda v, m=m: v.tensor_copy(o_f32[:, m, :], ctxA[:, :])],
               reads=["ctxA"], writes=["o32"])
        p.emit("vector",
               [lambda v, P=P, m=m: v.tensor_copy(fr(P, 2 * m), auxP[0:1, :])],
               reads=["aux"], writes=[rres(P)])

    def ffn_tail_segs(tg, m, dstD, dstO, tag):
        P = tg % 2
        segs = [seg_rden(P, [(2 * m, 2 * m + 1)]),
                seg_bcast(P, 2 * m + 1)]

        def s_mult():
            p.emit("vector",
                   [lambda v, m=m: v.tensor_mul(ostg[:, m, :], o_f32[:, m, :],
                                                bcst[:, :])],
                   reads=["o32", "bcst"], writes=["ostg"])
        segs.append(s_mult)
        if m == 1:
            def s_store():
                p.dma("sync", dstD[tg].rearrange("(c p) t -> p c t", p=128),
                      ostg[:, :, :], reads=["ostg"], writes=[f"ag_{tag}_in{tg}"])
                p.emit("gpsimd",
                       [lambda g: g.collective_compute(
                           "AllGather", ALU.bypass,
                           replica_groups=[list(range(8))],
                           ins=[dstD[tg]], outs=[dstO[tg]])],
                       reads=[f"ag_{tag}_in{tg}"], writes=[f"ag_{tag}_out{tg}"],
                       sem=p.chan(f"CC{tag}{tg}"), amt=1)
            segs.append(s_store)
        return segs

    def evac_attn1_p1(tg):
        P = tg % 2
        p.emit("vector",
               [lambda v: v.tensor_copy(ctx1_sb[:, 0, :], ctxA[:, :]),
                lambda v: v.tensor_copy(ctx1_sb[:, 1, :], ctxB[:, :])],
               reads=["ctxA", "ctxB"], writes=["ctx1"])
        p.emit("vector",
               [lambda v, P=P: v.tensor_copy(fr(P, 0), auxP[0:1, :])],
               reads=["aux"], writes=[rres(P)])

    def evac_attn1_p2_head(tg):
        P = tg % 2
        p.emit("vector",
               [lambda v: v.tensor_copy(ctx2_sb[:, 0, :], ctxA[:, :]),
                lambda v: v.tensor_copy(ctx2_sb[:, 1, :], ctxB[:, :])],
               reads=["ctxA", "ctxB"], writes=["ctx2"])
        p.emit("vector",
               [lambda v, P=P: v.tensor_copy(fr(P, 1), auxP[0:1, :])],
               reads=["aux"], writes=[rres(P)])

    def attn1_tail_segs(tg):
        P = tg % 2
        segs = [seg_rden(P, [(0, 2), (1, 3)]),
                seg_bcast(P, 2)]

        def s_t1():
            for ech in range(2):
                p.emit("vector",
                       [lambda v, ech=ech: v.tensor_mul(o_f32[:, ech, :],
                                                        ctx1_sb[:, ech, :], bcst[:, :])],
                       reads=["ctx1", "bcst"], writes=["o32"])
        segs.append(s_t1)
        segs.append(seg_bcast(P, 3))

        def s_u2():
            for ech in range(2):
                p.emit("vector",
                       [lambda v, ech=ech: v.tensor_mul(ctx2_sb[:, ech, :],
                                                        ctx2_sb[:, ech, :], bcst[:, :])],
                       reads=["ctx2", "bcst"], writes=["ctx2"])
                p.emit("vector",
                       [lambda v, ech=ech: v.scalar_tensor_tensor(
                           o_f32[:, ech, :], ctx2_sb[:, ech, :], negl_pp[:, 0:1],
                           o_f32[:, ech, :], ALU.mult, ALU.add)],
                       reads=["ctx2", "negl", "o32"], writes=["o32"])
        segs.append(s_u2)

        def s_sq():
            p.emit("vector",
                   [lambda v: v.tensor_copy(cast_sb[:, 0, :], o_f32[:, 0, :]),
                    lambda v: v.tensor_copy(cast_sb[:, 1, :], o_f32[:, 1, :])],
                   reads=["o32"], writes=["cast"])
            p.emit("vector",
                   [lambda v: v.tensor_mul(cast_sb[:, 0, :], cast_sb[:, 0, :],
                                           cast_sb[:, 0, :]),
                    lambda v: v.tensor_mul(cast_sb[:, 1, :], cast_sb[:, 1, :],
                                           cast_sb[:, 1, :])],
                   reads=["cast"], writes=["cast"])
        segs.append(s_sq)

        def s_ssq():
            p.emit("tensor",
                   [lambda t: t.matmul(bcst[0:1, :], ones_bf[:, 0:1], cast_sb[:, 0, :],
                                       start=True, stop=False),
                    lambda t: t.matmul(bcst[0:1, :], ones_bf[:, 0:1], cast_sb[:, 1, :],
                                       start=False, stop=True)],
                   reads=["cast", "consts"], writes=["bcst"])
        segs.append(s_ssq)

        def s_rstd():
            p.emit("scalar",
                   [lambda s_, P=P: s_.activation(fr(P, 2), bcst[0:1, :], AF.Ln,
                                                  bias=epsrow[0:1, 0:1],
                                                  scale=1.0 / 256.0)],
                   reads=["bcst", "consts"], writes=[rres(P)])
            p.emit("scalar",
                   [lambda s_, P=P: s_.activation(fr(P, 2), fr(P, 2), AF.Exp,
                                                  bias=zrow[0:1, 0:1], scale=-0.5)],
                   reads=[rres(P), "consts"], writes=[rres(P)])
        segs.append(s_rstd)
        segs.append(seg_bcast(P, 2))

        def s_subln():
            p.emit("vector",
                   [(lambda v, ech=ech: v.scalar_tensor_tensor(
                       ostg[:, ech, :], o_f32[:, ech, :], sublnw2[:, ech:ech + 1],
                       bcst[:, :], ALU.mult, ALU.mult)) for ech in range(2)],
                   reads=["o32", "bcst", "sublnw_raw"], writes=["ostg"])
        segs.append(s_subln)

        def s_store():
            p.dma("sync", ag_o_in[tg].rearrange("(c p) t -> p c t", p=128),
                  ostg[:, :, :], reads=["ostg"], writes=[f"ag_o_in{tg}"])
            p.emit("gpsimd",
                   [lambda g: g.collective_compute(
                       "AllGather", ALU.bypass,
                       replica_groups=[list(range(8))],
                       ins=[ag_o_in[tg]], outs=[ag_o_out[tg]])],
                   reads=[f"ag_o_in{tg}"], writes=[f"ag_o_out{tg}"],
                   sem=p.chan(f"CCo{tg}"), amt=1)
        segs.append(s_store)
        return segs

    def attn_phase(is_ffn):
        from collections import deque
        pend = deque()

        def hook():
            if pend:
                pend.popleft()()

        for tg in range(2):
            buf, bufres = (panA, "panA") if tg % 2 == 0 else (panB, "panB")
            if is_ffn:
                xn_panel_load(tg, buf, bufres)
            else:
                x_panel_load(tg, buf, bufres)
        for tg in range(NTG):
            buf, bufres = (panA, "panA") if tg % 2 == 0 else (panB, "panB")
            proj_qk(tg, buf, bufres, with_bias=not is_ffn)
            proj_v(tg, buf, bufres, with_bias=not is_ffn)
            if tg + 2 < NTG:
                nbuf, nbres = (panA, "panA") if tg % 2 == 0 else (panB, "panB")
                if is_ffn:
                    xn_panel_load(tg + 2, nbuf, nbres)
                else:
                    x_panel_load(tg + 2, nbuf, nbres)
            if not is_ffn:
                emit_stage(2 if tg < 4 else 5)
            es0, ec0, np0 = attn_matrix(tg, 0, is_ffn)
            es1, ec1, np1 = attn_matrix(tg, 1, is_ffn)
            stream = [(es0, ec0, j) for j in range(np0)] + \
                     [(es1, ec1, j) for j in range(np1)]
            par_prev = stream[0][0](stream[0][2])
            for idx in range(len(stream)):
                par_j = par_prev
                if idx + 1 < len(stream):
                    par_prev = stream[idx + 1][0](stream[idx + 1][2])
                stream[idx][1](stream[idx][2], par_j)
                hook()
                if idx == np0 - 1:
                    while pend:
                        pend.popleft()()
                    if is_ffn:
                        evac_ffn_head(tg, 0)
                        pend.extend(ffn_tail_segs(tg, 0, ag_cx_in, ag_cx_out, "cx"))
                    else:
                        evac_attn1_p1(tg)
            while pend:
                pend.popleft()()
            if is_ffn:
                evac_ffn_head(tg, 1)
                pend.extend(ffn_tail_segs(tg, 1, ag_cx_in, ag_cx_out, "cx"))
            else:
                evac_attn1_p2_head(tg)
                pend.extend(attn1_tail_segs(tg))
        while pend:
            pend.popleft()()

    attn_phase(is_ffn=False)
    emit_stage(len(stage_jobs))

    if "k1" in dbg_d:
        p.dma("gpsimd", dbg_d["k1"], R1[:, :], reads=["K"], writes=["dbg"])
    if "v1" in dbg_d:
        p.dma("gpsimd", dbg_d["v1"], R2[:, :], reads=["V"], writes=["dbg"])
    if "q1" in dbg_d:
        p.dma("gpsimd", dbg_d["q1"], R3[:, 12288:14336], reads=["q"], writes=["dbg"])

    def ln_rows(sum_res, sum_ap, ssq_res, ssq_ap, n):
        p.emit("vector",
               [lambda v: v.tensor_scalar_mul(frow(0), sum_ap, 1.0 / n)],
               reads=[sum_res], writes=["rows"])
        p.emit("vector",
               [lambda v: v.tensor_mul(frow(1), frow(0), frow(0))],
               reads=["rows"], writes=["rows"])
        p.emit("vector",
               [lambda v: v.scalar_tensor_tensor(frow(1), ssq_ap, 1.0 / n, frow(1),
                                                 ALU.mult, ALU.subtract)],
               reads=[ssq_res, "rows"], writes=["rows"])
        p.emit("scalar",
               [lambda s_: s_.activation(frow(2), frow(1), AF.Ln,
                                         bias=epsrow[0:1, 0:1], scale=1.0)],
               reads=["rows", "consts"], writes=["rows"])
        p.emit("scalar",
               [lambda s_: s_.activation(frow(2), frow(2), AF.Exp,
                                         bias=zrow[0:1, 0:1], scale=-0.5)],
               reads=["rows", "consts"], writes=["rows"])

    def mean_rstd_bcast():
        bcast_row(frow(0), 0, 1, bcst[:, :], "bcst")
        p.emit("vector", [lambda v: v.tensor_copy(B2_sb[:, 0, :], bcst[:, :])],
               reads=["bcst"], writes=["B2"])
        bcast_row(frow(2), 2, 3, bcst[:, :], "bcst")
        p.emit("vector", [lambda v: v.tensor_copy(B2_sb[:, 1, :], bcst[:, :])],
               reads=["bcst"], writes=["B2"])

    for tgo in range(OTG):
        dyn_panel_load(tgo, panA, "panA", ag_o_out, "o")
        def c_mm(oc):
            wbuf = wtVa if oc % 2 == 0 else wtVb
            wres = "wtVa_" if oc % 2 == 0 else "wtVb_"
            p.dma("sync", wbuf[:, :, :],
                  outw_bf[oc].rearrange("p (c k) -> p c k", c=NOC),
                  reads=["wstaged"], writes=[wres])
            bank, bres = (ctxA, "ctxA") if oc % 2 == 0 else (ctxB, "ctxB")
            fns = [(lambda t, fc=fc, wbuf=wbuf, bank=bank:
                    t.matmul(bank[:, :], wbuf[:, fc, :], panA[:, fc, :],
                             start=(fc == 0), stop=(fc == NOC - 1)))
                   for fc in range(NOC)]
            p.emit("tensor", fns, reads=[wres, "panA"], writes=[bres])
            xb = yoc[:, oc % 2, :]
            xres = "yoc0" if oc % 2 == 0 else "yoc1"
            p.dma("sync", xb, A["x_ownT"][oc * 128:(oc + 1) * 128,
                                          tgo * TG:(tgo + 1) * TG], writes=[xres])
            return bank, bres, xb, xres

        def c_evac(oc, bank, bres, xb, xres):
            p.emit("vector",
                   [lambda v, oc=oc, bank=bank, xb=xb:
                    v.scalar_tensor_tensor(F1[:, oc, :], bank[:, :],
                                           outb_sb[:, oc:oc + 1], xb,
                                           ALU.add, ALU.add)],
                   reads=[bres, xres, "params"], writes=["F1"])
            p.emit("vector",
                   [lambda v, oc=oc: v.tensor_copy(cast_sb[:, 0, :], F1[:, oc, :])],
                   reads=["F1"], writes=["cast"])
            p.emit("vector",
                   [lambda v: v.tensor_mul(cast_sb[:, 1, :], cast_sb[:, 0, :],
                                           cast_sb[:, 0, :])],
                   reads=["cast"], writes=["cast"])
            p.emit("tensor",
                   [lambda t, oc=oc: t.matmul(auxP[0:1, :], ones_bf[:, 0:1],
                                              cast_sb[:, 0, :], start=(oc == 0),
                                              stop=(oc == NOC - 1)),
                    lambda t, oc=oc: t.matmul(bcst[0:1, :], ones_bf[:, 0:1],
                                              cast_sb[:, 1, :], start=(oc == 0),
                                              stop=(oc == NOC - 1))],
                   reads=["cast", "consts"], writes=["aux", "bcst"])

        prevc = c_mm(0)
        for oc in range(NOC):
            curc = prevc
            if oc + 1 < NOC:
                prevc = c_mm(oc + 1)
            c_evac(oc, *curc)
        ln_rows("aux", auxP[0:1, :], "bcst", bcst[0:1, :], DM)
        mean_rstd_bcast()
        for oc in range(NOC):
            p.emit("vector",
                   [lambda v, oc=oc: v.tensor_sub(F1[:, oc, :], F1[:, oc, :],
                                                  B2_sb[:, 0, :])],
                   reads=["B2", "F1"], writes=["F1"])
            p.emit("vector",
                   [lambda v, oc=oc: v.tensor_mul(F1[:, oc, :], F1[:, oc, :],
                                                  B2_sb[:, 1, :])],
                   reads=["B2", "F1"], writes=["F1"])
            p.emit("vector",
                   [lambda v, oc=oc: v.tensor_scalar(panB[:, oc, :], F1[:, oc, :],
                                                     parm_sb[:, 0, oc:oc + 1],
                                                     parm_sb[:, 1, oc:oc + 1],
                                                     ALU.mult, ALU.add)],
                   reads=["F1", "params"], writes=["panB"])
        p.dma("sync", y_spill[:, tgo * TG:(tgo + 1) * TG].rearrange(
                  "(c p) t -> p c t", p=128), F1[:, :, :],
              reads=["F1"], writes=["y_spill"])
        p.dma("sync", ag_xn_in[tgo].rearrange("(c p) t -> p c t", p=128),
              panB[:, :, :], reads=["panB"], writes=[f"ag_xn_in{tgo}"])
        p.emit("gpsimd",
               [lambda g, tgo=tgo: g.collective_compute(
                   "AllGather", ALU.bypass, replica_groups=[list(range(8))],
                   ins=[ag_xn_in[tgo]], outs=[ag_xn_out[tgo]])],
               reads=[f"ag_xn_in{tgo}"], writes=[f"ag_xn_out{tgo}"],
               sem=p.chan(f"CCxn{tgo}"), amt=1)


    p.dma("gpsimd", wqkv_sb[:, :, :], A["ffnqkv_s"], writes=["wqkv"])
    attn_phase(is_ffn=True)


    p.dma("gpsimd", wu_sb[0:32, :], A["wuT"], writes=["wu"])
    for tgo in range(OTG):
        dyn_panel_load(tgo, panA, "panA", ag_cx_out, "cx")
        for qq in range(4):
            p.dma("gpsimd", panB[:, qq * 4:(qq + 1) * 4, :],
                  y_spill[qq * 512:(qq + 1) * 512, tgo * TG:(tgo + 1) * TG]
                  .rearrange("(c p) t -> p c t", p=128),
                  reads=["y_spill"], writes=["panB"])
        rbank = [("ctxA", ctxA[0:1, :]), ("ctxB", ctxB[0:1, :]), ("aux", auxP[0:1, :])]
        for rj in range(3):
            res, ap_ = rbank[rj]
            fns = [(lambda t, fc=fc, rj=rj, ap_=ap_:
                    t.matmul(ap_, wr_b[:, fc, rj:rj + 1], panB[:, fc, :],
                             start=(fc == 0), stop=(fc == NOC - 1)))
                   for fc in range(NOC)]
            p.emit("tensor", fns, reads=["wr", "panB"], writes=[res])
        p.emit("vector",
               [lambda v: v.tensor_copy(frow(0), ctxA[0:1, :]),
                lambda v: v.tensor_copy(frow(1), ctxB[0:1, :]),
                lambda v: v.tensor_copy(frow(2), auxP[0:1, :])],
               reads=["ctxA", "ctxB", "aux"], writes=["rows"])
        p.emit("vector",
               [lambda v: v.tensor_tensor(frow(3), frow(0), frow(1), ALU.is_ge),
                lambda v: v.tensor_tensor(frow(4), frow(0), frow(2), ALU.is_ge)],
               reads=["rows"], writes=["rows"])
        p.emit("vector", [lambda v: v.tensor_mul(frow(3), frow(3), frow(4))],
               reads=["rows"], writes=["rows"])
        p.emit("vector",
               [lambda v: v.tensor_tensor(frow(4), frow(1), frow(0), ALU.is_gt),
                lambda v: v.tensor_tensor(frow(5), frow(1), frow(2), ALU.is_ge)],
               reads=["rows"], writes=["rows"])
        p.emit("vector", [lambda v: v.tensor_mul(frow(4), frow(4), frow(5))],
               reads=["rows"], writes=["rows"])
        p.emit("vector",
               [lambda v: v.tensor_copy(brow(0), frow(3)),
                lambda v: v.tensor_copy(brow(1), frow(4))],
               reads=["rows"], writes=["rowb"])
        for mj in range(2):
            p.emit("tensor",
                   [lambda t, mj=mj: t.matmul(bcst[:, :], ones1r_bf[0:1, :],
                                              brow(mj), start=True, stop=True)],
                   reads=["rowb", "consts"], writes=["bcst"])
            p.emit("vector",
                   [lambda v, mj=mj: v.tensor_copy(Mb_sb[:, mj, :], bcst[:, :])],
                   reads=["bcst"], writes=["Mb"])
        fns = [(lambda t, fc=fc: t.matmul(auxP[0:32, :], wd_b[:, fc, :],
                                          panB[:, fc, :], start=(fc == 0),
                                          stop=(fc == NOC - 1)))
               for fc in range(NOC)]
        p.emit("tensor", fns, reads=["wd", "panB"], writes=["aux"])
        p.emit("vector", [lambda v: v.tensor_copy(z_sb[0:32, :], auxP[0:32, :])],
               reads=["aux"], writes=["z"])
        def w_mm(oc):
            wbuf = wtVa if oc % 2 == 0 else wtVb
            wres = "wtVa_" if oc % 2 == 0 else "wtVb_"
            p.dma("sync", wbuf[:, :, :],
                  wo_bf[oc].rearrange("p (c k) -> p c k", c=NOC),
                  reads=["wstaged"], writes=[wres])
            bank, bres = (ctxA, "ctxA") if oc % 2 == 0 else (ctxB, "ctxB")
            fns = [(lambda t, fc=fc, wbuf=wbuf, bank=bank:
                    t.matmul(bank[:, :], wbuf[:, fc, :], panA[:, fc, :],
                             start=(fc == 0), stop=(fc == NOC - 1)))
                   for fc in range(NOC)]
            p.emit("tensor", fns, reads=[wres, "panA"], writes=[bres])
            xb = yoc[:, oc % 2, :]
            xres = "yoc0" if oc % 2 == 0 else "yoc1"
            p.dma("sync", xb, y_spill[oc * 128:(oc + 1) * 128,
                                      tgo * TG:(tgo + 1) * TG],
                  reads=["y_spill"], writes=[xres])
            return bank, bres, xb, xres

        def w_evac(oc, bank, bres, xb, xres):
            p.emit("vector",
                   [lambda v, oc=oc, bank=bank, xb=xb:
                    v.tensor_add(F1[:, oc, :], bank[:, :], xb)],
                   reads=[bres, xres], writes=["F1"])
            p.emit("vector",
                   [lambda v, oc=oc: v.tensor_copy(cast_sb[:, 0, :], F1[:, oc, :])],
                   reads=["F1"], writes=["cast"])
            p.emit("vector",
                   [lambda v: v.tensor_mul(cast_sb[:, 1, :], cast_sb[:, 0, :],
                                           cast_sb[:, 0, :])],
                   reads=["cast"], writes=["cast"])
            p.emit("tensor",
                   [lambda t, oc=oc: t.matmul(pairB[0:1, 512:1024], ones_bf[:, 0:1],
                                              cast_sb[:, 0, :], start=(oc == 0),
                                              stop=(oc == NOC - 1)),
                    lambda t, oc=oc: t.matmul(pairB[0:1, 0:512], ones_bf[:, 0:1],
                                              cast_sb[:, 1, :], start=(oc == 0),
                                              stop=(oc == NOC - 1))],
                   reads=["cast", "consts"], writes=["pairB1", "pairB0"])

        prevw = w_mm(0)
        for oc in range(NOC):
            curw = prevw
            if oc + 1 < NOC:
                prevw = w_mm(oc + 1)
            w_evac(oc, *curw)
        ln_rows("pairB1", pairB[0:1, 512:1024], "pairB0", pairB[0:1, 0:512], DM)
        mean_rstd_bcast()
        for oc in range(NOC):
            p.emit("vector",
                   [lambda v, oc=oc: v.tensor_sub(o_f32[:, 0, :], F1[:, oc, :],
                                                  B2_sb[:, 0, :])],
                   reads=["F1", "B2"], writes=["o32"])
            p.emit("vector",
                   [lambda v: v.tensor_mul(o_f32[:, 0, :], o_f32[:, 0, :],
                                           B2_sb[:, 1, :])],
                   reads=["o32", "B2"], writes=["o32"])
            p.emit("vector",
                   [lambda v, oc=oc: v.tensor_scalar(xn2_sb[:, oc, :], o_f32[:, 0, :],
                                                     parm_sb[:, 2, oc:oc + 1],
                                                     parm_sb[:, 3, oc:oc + 1],
                                                     ALU.mult, ALU.add)],
                   reads=["o32", "params"], writes=["xn2"])
        for hc in range(64):
            wbuf = wtPa if hc % 2 == 0 else wtPb
            wres = "wtPa_" if hc % 2 == 0 else "wtPb_"
            p.dma("sync", wbuf[:, :, :],
                  ff1_bf[hc].rearrange("p (c k) -> p c k", c=NOC),
                  reads=["wstaged"], writes=[wres])
            bank, bres = (ctxA, "ctxA") if hc % 2 == 0 else (ctxB, "ctxB")
            fns = [(lambda t, fc=fc, wbuf=wbuf, bank=bank:
                    t.matmul(bank[:, :], wbuf[:, fc, :], xn2_sb[:, fc, :],
                             start=(fc == 0), stop=(fc == NOC - 1)))
                   for fc in range(NOC)]
            p.emit("tensor", fns, reads=[wres, "xn2"], writes=[bres])
            hdst = h_lo[:, hc, :] if hc < 32 else h_hi[:, hc - 32, :]
            hres = "h_lo" if hc < 32 else "h_hi"
            p.emit("scalar",
                   [lambda s_, hdst=hdst, bank=bank:
                    s_.activation(hdst, bank[:, :], AF.Gelu_apprx_tanh,
                                  bias=czero[:, 0:1], scale=1.0)],
                   reads=[bres, "consts"], writes=[hres])
        def f_mm(oc):
            wbuf2 = w2a if oc % 2 == 0 else w2b
            wres2 = "w2a_" if oc % 2 == 0 else "w2b_"
            p.dma("sync", wbuf2[:, :, :],
                  ff2_bf[oc].rearrange("p (c k) -> p c k", c=64),
                  reads=["wstaged"], writes=[wres2])
            bank_ap = [pairA[:, 0:512], pairA[:, 512:1024], ctxA[:, :], ctxB[:, :]][oc % 4]
            bres = ["pairA0", "pairA1", "ctxA", "ctxB"][oc % 4]
            fns = [(lambda t, hc=hc, wbuf2=wbuf2, bank_ap=bank_ap:
                    t.matmul(bank_ap, wbuf2[:, hc, :],
                             (h_lo[:, hc, :] if hc < 32 else h_hi[:, hc - 32, :]),
                             start=(hc == 0), stop=(hc == 63)))
                   for hc in range(64)]
            p.emit("tensor", fns, reads=[wres2, "h_lo", "h_hi"], writes=[bres])
            up_ap = pairB[:, 0:512] if oc % 2 == 0 else pairB[:, 512:1024]
            upres = "pairB0" if oc % 2 == 0 else "pairB1"
            p.emit("tensor",
                   [lambda t, oc=oc, up_ap=up_ap:
                    t.matmul(up_ap, wu_sb[0:32, oc * 128:(oc + 1) * 128],
                             z_sb[0:32, :], start=True, stop=True)],
                   reads=["wu", "z"], writes=[upres])
            xb = yoc[:, oc % 2, :]
            xres = "yoc0" if oc % 2 == 0 else "yoc1"
            p.dma("sync", xb, y_spill[oc * 128:(oc + 1) * 128,
                                      tgo * TG:(tgo + 1) * TG],
                  reads=["y_spill"], writes=[xres])
            return bank_ap, bres, up_ap, upres, xb, xres

        def f_evac(oc, bank_ap, bres, up_ap, upres, xb, xres):
            p.emit("vector",
                   [lambda v, oc=oc, bank_ap=bank_ap:
                    v.tensor_add(F1[:, oc, :], bank_ap, F1[:, oc, :])],
                   reads=[bres, "F1"], writes=["F1"])
            p.emit("vector",
                   [lambda v, oc=oc, xb=xb: v.scalar_tensor_tensor(
                       ctx1_sb[:, 0, :], xb, parm_sb[:, 4, oc:oc + 1], F1[:, oc, :],
                       ALU.add, ALU.subtract)],
                   reads=[xres, "params", "F1"], writes=["ctx1"])
            p.emit("vector",
                   [lambda v: v.tensor_mul(ctx1_sb[:, 0, :], ctx1_sb[:, 0, :],
                                           Mb_sb[:, 0, :])],
                   reads=["ctx1", "Mb"], writes=["ctx1"])
            p.emit("vector",
                   [lambda v, xb=xb, up_ap=up_ap:
                    v.tensor_add(ctx1_sb[:, 1, :], up_ap, xb)],
                   reads=[upres, xres], writes=["ctx1"])
            p.emit("vector",
                   [lambda v, oc=oc: v.tensor_sub(ctx1_sb[:, 1, :], ctx1_sb[:, 1, :],
                                                  F1[:, oc, :])],
                   reads=["ctx1", "F1"], writes=["ctx1"])
            p.emit("vector",
                   [lambda v: v.tensor_mul(ctx1_sb[:, 1, :], ctx1_sb[:, 1, :],
                                           Mb_sb[:, 1, :])],
                   reads=["ctx1", "Mb"], writes=["ctx1"])
            p.emit("vector",
                   [lambda v, oc=oc, xb=xb: v.tensor_add(F1[:, oc, :], F1[:, oc, :], xb)],
                   reads=["F1", xres], writes=["F1"])
            p.emit("vector",
                   [lambda v: v.tensor_add(ctx1_sb[:, 0, :], ctx1_sb[:, 0, :],
                                           ctx1_sb[:, 1, :])],
                   reads=["ctx1"], writes=["ctx1"])
            p.emit("vector",
                   [lambda v, oc=oc: v.tensor_add(F1[:, oc, :], F1[:, oc, :],
                                                  ctx1_sb[:, 0, :])],
                   reads=["F1", "ctx1"], writes=["F1"])
            p.emit("vector",
                   [lambda v, oc=oc: v.tensor_copy(cast_sb[:, 0, :], F1[:, oc, :])],
                   reads=["F1"], writes=["cast"])
            p.emit("vector",
                   [lambda v: v.tensor_mul(cast_sb[:, 1, :], cast_sb[:, 0, :],
                                           cast_sb[:, 0, :])],
                   reads=["cast"], writes=["cast"])
            p.emit("tensor",
                   [lambda t, oc=oc: t.matmul(auxP[0:1, :], ones_bf[:, 0:1],
                                              cast_sb[:, 0, :], start=(oc == 0),
                                              stop=(oc == NOC - 1)),
                    lambda t, oc=oc: t.matmul(bcst[0:1, :], ones_bf[:, 0:1],
                                              cast_sb[:, 1, :], start=(oc == 0),
                                              stop=(oc == NOC - 1))],
                   reads=["cast", "consts"], writes=["aux", "bcst"])

        prevf = f_mm(0)
        for oc in range(NOC):
            curf = prevf
            if oc + 1 < NOC:
                prevf = f_mm(oc + 1)
            f_evac(oc, *curf)
        ln_rows("aux", auxP[0:1, :], "bcst", bcst[0:1, :], DM)
        mean_rstd_bcast()
        for oc in range(NOC):
            p.emit("vector",
                   [lambda v, oc=oc: v.tensor_sub(F1[:, oc, :], F1[:, oc, :],
                                                  B2_sb[:, 0, :])],
                   reads=["B2", "F1"], writes=["F1"])
            p.emit("vector",
                   [lambda v, oc=oc: v.tensor_mul(F1[:, oc, :], F1[:, oc, :],
                                                  B2_sb[:, 1, :])],
                   reads=["B2", "F1"], writes=["F1"])
        p.dma("sync", out_d[:, tgo * TG:(tgo + 1) * TG].rearrange(
                  "(c p) t -> p c t", p=128), F1[:, :, :],
              reads=["F1"], writes=["outd"])

    if "lg" in dbg_d:
        p.dma("sync", dbg_d["lg"], lam_sb[:, :], reads=["lam"], writes=["dbg"])
    if "ng" in dbg_d:
        p.dma("sync", dbg_d["ng"], negl_pp[:, :], reads=["negl"], writes=["dbg"])
    if "y" in dbg_d:
        p.dma("gpsimd", dbg_d["y"], y_spill, reads=["y_spill"], writes=["dbg"])

    fin = {s: v for s, v in p.cnt.items()
           if (s.startswith("Ds_") or s.startswith("Dg_") or s.startswith("CC")
               or s.startswith("PF")) and v > 0}
    def final(e_):
        for s, v in fin.items():
            if v:
                e_.wait_ge(p.sem[s], v)
    p.q["sync"].append(final)

    with nc.Block() as block:
        @block.tensor
        def _(t):
            for f in p.q["tensor"]:
                f(t)

        @block.scalar
        def _(s_):
            for f in p.q["scalar"]:
                f(s_)

        @block.vector
        def _(v):
            for f in p.q["vector"]:
                f(v)

        @block.sync
        def _(s_):
            for f in p.q["sync"]:
                f(s_)

        @block.gpsimd
        def _(g):
            for f in p.q["gpsimd"]:
                f(g)

    ctx.close()
    return nc


_CACHE = {}


def kernel(**inputs):
    maps = host_in_maps(inputs)
    if "nc" not in _CACHE:
        _CACHE["nc"] = build_nc()
    nc = _CACHE["nc"]
    res = run_bass_kernel_spmd(nc, maps, core_ids=list(range(8)))
    outs = [np.asarray(res.results[i]["out"]) for i in range(8)]
    full = np.concatenate(outs, axis=1)
    return np.ascontiguousarray(full.T).reshape(1, S, DM)


# version 18
# speedup vs baseline: 1.1591x; 1.0406x over previous
import math
import numpy as np
import ml_dtypes
from contextlib import ExitStack

import concourse.bass as bass
import concourse.mybir as mybir
from concourse.bass_utils import run_bass_kernel_spmd

F32 = mybir.dt.float32
BF16 = mybir.dt.bfloat16
AF = mybir.ActivationFunctionType
ALU = mybir.AluOpType
ds = bass.ds

PARANOID = False
S = 8192
DM = 2048
NOC = 16
TG = 512
NTG = S // TG
OWN = 1024
OTG = 2
CSH = 12.0
SCL = 128.0 ** -0.5
LI = 0.8 - 0.6 * math.exp(-0.3)
EPS = 1e-5

ENGS = ("tensor", "scalar", "vector", "sync", "gpsimd")
ESEM = {"tensor": "PE", "scalar": "SC", "vector": "VE", "sync": "dW", "gpsimd": "dG"}


def _f32(a):
    return np.ascontiguousarray(a, dtype=np.float32)


def _tiles(wT, n):
    K = wT.shape[0]
    out = np.empty((n, 128, K // 128, 128), np.float32)
    for j in range(n):
        out[j] = wT[:, j * 128:(j + 1) * 128].reshape(K // 128, 128, 128).transpose(1, 0, 2)
    return _f32(out)


def _swz(a):
    return _f32(np.asarray(a).reshape(NOC, 128, -1).transpose(1, 0, 2))


def host_in_maps(inputs):
    x = np.asarray(inputs["L0_attn_hidden_states"], dtype=np.float32)[0]
    xT = _f32(x.T)
    wqkv = np.asarray(inputs["L0_attn_Wqkv_w"])
    wqkv_b = np.asarray(inputs["L0_attn_Wqkv_b"])
    outw_t = _tiles(_f32(np.asarray(inputs["L0_attn_out_w"]).T), NOC)
    outb = _f32(np.asarray(inputs["L0_attn_out_b"]).reshape(NOC, 128).T)
    sublnw = _f32(np.asarray(inputs["L0_attn_subln_w"]).reshape(2, 128).T)
    lam = {k: _f32(np.asarray(inputs[f"L0_attn_lambda_{k}"]).reshape(128, 1))
           for k in ("q1", "k1", "q2", "k2")}
    wq = np.asarray(inputs["L0_ffn_wq"]); wk = np.asarray(inputs["L0_ffn_wk"])
    wv = np.asarray(inputs["L0_ffn_wv"])
    wo_t = _tiles(_f32(np.asarray(inputs["L0_ffn_wo"]).T), NOC)
    wr_s = _swz(np.asarray(inputs["L0_ffn_w_router"]).T)
    wd_s = _swz(np.asarray(inputs["L0_ffn_w_down"]).T)
    wuT = _f32(np.asarray(inputs["L0_ffn_w_up"]).T)
    ff1_t = _tiles(_f32(np.asarray(inputs["L0_ffn_w_ff1"]).T), 64)
    ff2_t = _tiles(_f32(np.asarray(inputs["L0_ffn_w_ff2"]).T), NOC)
    pv = {k: _f32(np.asarray(inputs["L0_ffn_" + k]).reshape(NOC, 128).T)
          for k in ("ln1_scale", "ln1_bias", "ln2_scale", "ln2_bias", "simple_bias")}

    km = np.arange(128)[:, None]
    tm = np.arange(TG)[None, :]
    masks = np.concatenate(
        [((km + 128 * o) <= tm).astype(np.float32) for o in range(4)],
        axis=1).astype(ml_dtypes.bfloat16)

    maps = []
    for i in range(8):
        h = i; g = h // 2
        rows = np.concatenate([
            np.arange(h * 256, h * 256 + 256),
            np.arange(2048 + g * 256, 2048 + g * 256 + 256),
            np.arange(3072 + g * 256, 3072 + g * 256 + 256)])
        wqkvT = wqkv[rows].T
        bsl = wqkv_b[rows]
        jr = np.arange(2 * i * 128, (2 * i + 2) * 128)
        ffnqkvT = np.concatenate([wq[jr], wk[jr], wv[jr]], axis=0).T
        maps.append({
            "xT": xT, "x_ownT": _f32(xT[:, i * OWN:(i + 1) * OWN]),
            "wqkv_s": _swz(wqkvT), "qkvb": _f32(bsl[:512].reshape(4, 128).T),
            "vbias": _f32(bsl[512:].reshape(1, 256)),
            "outw_t": outw_t, "outb": outb, "sublnw": sublnw,
            "lamq1": lam["q1"], "lamk1": lam["k1"],
            "lamq2": lam["q2"], "lamk2": lam["k2"],
            "ffnqkv_s": _swz(ffnqkvT), "wo_t": wo_t,
            "wr_s": wr_s, "wd_s": wd_s, "wuT": wuT,
            "ff1_t": ff1_t, "ff2_t": ff2_t,
            "ln1s": pv["ln1_scale"], "ln1b": pv["ln1_bias"],
            "ln2s": pv["ln2_scale"], "ln2b": pv["ln2_bias"],
            "sbias": pv["simple_bias"], "masks": np.ascontiguousarray(masks),
        })
    return maps


class Prog:

    def __init__(self, nc, ctx, aliases):
        self.nc = nc
        self.ctx = ctx
        self.q = {e: [] for e in ENGS}
        self.sem = {}
        self.cnt = {}
        for s in ("PE", "SC", "VE", "dW", "dG", "CC"):
            self.sem[s] = ctx.enter_context(nc.semaphore(s))
            self.cnt[s] = 0
        self.writer = {}
        self.readers = {}
        self.waited = {e: {} for e in ENGS}
        self.alias = aliases

    def canon(self, res):
        return self.alias.get(res, (res,))

    def _wait(self, eng, sem, val, skip_eng):
        if skip_eng == eng:
            return
        if self.waited[eng].get(sem, -1) >= val:
            return
        self.waited[eng][sem] = val
        h = self.sem[sem]
        self.q[eng].append(lambda e, h=h, val=val: e.wait_ge(h, val))

    def emit(self, eng, fns, reads=(), writes=(), sem=None, amt=1):
        sem = sem or ESEM[eng]
        if PARANOID:
            for s2, v2 in list(self.cnt.items()):
                if v2 > 0:
                    self._wait(eng, s2, v2, None)
        creads = [c for r in reads for c in self.canon(r)]
        cwrites = [c for r in writes for c in self.canon(r)]
        for r in creads:
            w = self.writer.get(r)
            if w:
                self._wait(eng, w[0], w[1], w[2])
        for r in cwrites:
            w = self.writer.get(r)
            if w:
                self._wait(eng, w[0], w[1], w[2])
            for s2, (v2, sk) in self.readers.get(r, {}).items():
                self._wait(eng, s2, v2, sk)
        self.cnt[sem] += amt
        val = self.cnt[sem]
        h = self.sem[sem]
        assert fns
        for f in fns[:-1]:
            self.q[eng].append(f)
        last = fns[-1]
        self.q[eng].append(lambda e, last=last, h=h, amt=amt: last(e).then_inc(h, amt))
        skip = None
        for r in cwrites:
            self.writer[r] = (sem, val, skip)
            self.readers[r] = {}
        for r in creads:
            d = self.readers.setdefault(r, {})
            old = d.get(sem)
            if old is None or old[0] < val:
                d[sem] = (val, skip)
        return val

    def chan(self, name):
        if name not in self.sem:
            self.sem[name] = self.ctx.enter_context(self.nc.semaphore(name[:31]))
            self.cnt[name] = 0
        return name

    def dma(self, eng, out, in_, reads=(), writes=()):
        key = self.canon(writes[0])[0] if writes else "misc"
        sem = self.chan("D" + eng[0] + "_" + key)
        return self.emit(eng, [lambda e: e.dma_start(out=out, in_=in_)],
                         reads=reads, writes=writes, sem=sem, amt=16)


def build_nc(dbg=()):
    nc = bass.Bass("TRN2", target_bir_lowering=False)
    A = {}
    def din(name, shape, dt=F32):
        A[name] = nc.dram_tensor(name, shape, dt, kind="ExternalInput").ap()
    din("xT", [DM, S]); din("x_ownT", [DM, OWN])
    din("wqkv_s", [128, NOC, 768]); din("qkvb", [128, 4]); din("vbias", [1, 256])
    din("outw_t", [NOC, 128, NOC, 128]); din("outb", [128, NOC]); din("sublnw", [128, 2])
    for k in ("lamq1", "lamk1", "lamq2", "lamk2"):
        din(k, [128, 1])
    din("ffnqkv_s", [128, NOC, 768]); din("wo_t", [NOC, 128, NOC, 128])
    din("wr_s", [128, NOC, 3]); din("wd_s", [128, NOC, 32]); din("wuT", [32, DM])
    din("ff1_t", [64, 128, NOC, 128]); din("ff2_t", [NOC, 128, 64, 128])
    for k in ("ln1s", "ln1b", "ln2s", "ln2b", "sbias"):
        din(k, [128, NOC])
    din("masks", [128, 4 * TG], BF16)
    out_d = nc.dram_tensor("out", [DM, OWN], F32, kind="ExternalOutput").ap()

    ag_o_in = nc.dram_tensor("ag_o_in", [16, 256, TG], BF16, kind="Internal").ap()
    ag_o_out = nc.dram_tensor("ag_o_out", [16, DM, TG], BF16, kind="Internal",
                              addr_space="Shared").ap()
    ag_xn_in = nc.dram_tensor("ag_xn_in", [2, DM, TG], BF16, kind="Internal").ap()
    ag_xn_out = nc.dram_tensor("ag_xn_out", [2, 8 * DM, TG], BF16, kind="Internal",
                               addr_space="Shared").ap()
    ag_cx_in = nc.dram_tensor("ag_cx_in", [16, 256, TG], BF16, kind="Internal").ap()
    ag_cx_out = nc.dram_tensor("ag_cx_out", [16, DM, TG], BF16, kind="Internal",
                               addr_space="Shared").ap()
    y_spill = nc.dram_tensor("y_spill", [DM, OWN], F32, kind="Internal").ap()
    ff1_bf = nc.dram_tensor("ff1_bf", [64, 128, NOC * 128], BF16, kind="Internal").ap()
    ff2_bf = nc.dram_tensor("ff2_bf", [NOC, 128, 64 * 128], BF16, kind="Internal").ap()
    outw_bf = nc.dram_tensor("outw_bf", [NOC, 128, NOC * 128], BF16, kind="Internal").ap()
    wo_bf = nc.dram_tensor("wo_bf", [NOC, 128, NOC * 128], BF16, kind="Internal").ap()

    dbg_d = {}
    for name, shape, dt in (
        ("o", [256, S], BF16), ("y", [DM, OWN], F32), ("xn", [DM, OWN], BF16),
        ("cx", [256, S], BF16), ("k1", [128, 2 * S], BF16),
        ("v1", [128, 64 * 256], BF16), ("q1", [128, 4 * TG], BF16),
        ("ng", [128, 1], F32), ("lg", [128, 4], F32), ("l2", [1, 8], F32),
        ("pt", [128, 2048], BF16), ("c1", [128, 2 * TG], F32),
        ("d1", [1, 2 * TG], F32), ("c2", [128, 2 * TG], F32),
        ("of", [128, 2 * TG], F32), ("os", [128, 2 * TG], BF16),
        ("qg", [128, 4 * TG], BF16),
    ):
        if name in dbg:
            dbg_d[name] = nc.dram_tensor("dbg_" + name, shape, dt,
                                         kind="ExternalOutput").ap()

    ctx = ExitStack()
    e = ctx.enter_context
    R1 = e(nc.sbuf_tensor("R1", [128, 16384], BF16))
    R2 = e(nc.sbuf_tensor("R2", [128, 16384], BF16))
    R3 = e(nc.sbuf_tensor("R3", [128, 16384], BF16))
    R4 = e(nc.sbuf_tensor("R4", [128, 8192], BF16))
    R5 = e(nc.sbuf_tensor("R5", [128, 8192], BF16))
    R7 = e(nc.sbuf_tensor("R7", [128, 1024], BF16))
    R8 = e(nc.sbuf_tensor("R8", [128, 1024], F32))
    R9 = e(nc.sbuf_tensor("R9", [128, 2048], BF16))

    def rv(R, a, b, pat=None, **kw):
        ap = R[:, a:b]
        return ap.rearrange(pat, **kw) if pat else ap

    K_sb = rv(R1, 0, 16384, "p (c s) -> p c s", c=2)
    h_lo = rv(R1, 0, 16384, "p (c t) -> p c t", c=32)
    V_sb = rv(R2, 0, 16384, "p (c s) -> p c s", c=64)
    h_hi = rv(R2, 0, 16384, "p (c t) -> p c t", c=32)
    wtVa = rv(R2, 0, 2048, "p (c k) -> p c k", c=NOC)
    wtVb = rv(R2, 8192, 8192 + 2048, "p (c k) -> p c k", c=NOC)
    wqkv_sb = rv(R3, 0, 12288, "p (c o) -> p c o", c=NOC)
    q_sb = rv(R3, 12288, 14336, "p (c t) -> p c t", c=4)
    PT_sb = rv(R3, 14336, 16384, "p (c t) -> p c t", c=2)
    w2a = rv(R3, 0, 8192, "p (c k) -> p c k", c=64)
    w2b = rv(R3, 8192, 16384, "p (c k) -> p c k", c=64)
    z_sb = R3[0:32, 14336:14848]
    panA = rv(R4, 0, 8192, "p (c t) -> p c t", c=NOC)
    wtPa = rv(R4, 0, 2048, "p (c k) -> p c k", c=NOC)
    wtPb = rv(R4, 4096, 4096 + 2048, "p (c k) -> p c k", c=NOC)
    panB = rv(R5, 0, 8192, "p (c t) -> p c t", c=NOC)
    xn2_sb = rv(R5, 0, 8192, "p (c t) -> p c t", c=NOC)
    ostg = rv(R7, 0, 1024, "p (c t) -> p c t", c=2)
    Mb_sb = rv(R7, 0, 1024, "p (c t) -> p c t", c=2)
    ctx2_sb = rv(R8, 0, 1024, "p (c t) -> p c t", c=2)
    B2_sb = rv(R8, 0, 1024, "p (c t) -> p c t", c=2)
    masks_sb = rv(R9, 0, 2048, "p (c t) -> p c t", c=4)
    wu_sb = R9[0:32, 0:2048]

    F1 = e(nc.sbuf_tensor("F1", [128, NOC, TG], F32))
    F1_bf_view = F1.bitcast(BF16)[:, :, :].rearrange("p c t -> p (c t)")
    ctx1_sb = e(nc.sbuf_tensor("ctx1_sb", [128, 2, TG], F32))
    o_f32 = e(nc.sbuf_tensor("o_f32", [128, 2, TG], F32))
    cast_sb = e(nc.sbuf_tensor("cast_sb", [128, 2, TG], BF16))
    yoc = e(nc.sbuf_tensor("yoc", [128, 2, TG], F32))
    PTf_sb = yoc.bitcast(BF16)
    wr_b = e(nc.sbuf_tensor("wr_b", [128, NOC, 3], BF16))
    wd_b = e(nc.sbuf_tensor("wd_b", [128, NOC, 32], BF16))
    rowf = e(nc.sbuf_tensor("rowf", [1, 8 * TG], F32))
    rowb = e(nc.sbuf_tensor("rowb", [1, 4 * TG], BF16))
    qkvb_sb = e(nc.sbuf_tensor("qkvb_sb", [128, 4], F32))
    vbias_sb = e(nc.sbuf_tensor("vbias_sb", [1, 256], BF16))
    outb_sb = e(nc.sbuf_tensor("outb_sb", [128, NOC], F32))
    parm_sb = e(nc.sbuf_tensor("parm_sb", [128, 5, NOC], F32))
    sublnw2 = e(nc.sbuf_tensor("sublnw2", [128, 2], F32))
    lam_sb = e(nc.sbuf_tensor("lam_sb", [128, 4], F32))
    negl_pp = e(nc.sbuf_tensor("negl_pp", [128, 1], F32))
    ones_bf = e(nc.sbuf_tensor("ones_bf", [128, 1], BF16))
    ones1r_bf = e(nc.sbuf_tensor("ones1r_bf", [1, 128], BF16))
    ones1r_f = e(nc.sbuf_tensor("ones1r_f", [1, 128], F32))
    cexp = e(nc.sbuf_tensor("cexp", [128, 1], F32))
    czero = e(nc.sbuf_tensor("czero", [128, 1], F32))
    zrow = e(nc.sbuf_tensor("zrow", [1, 1], F32))
    epsrow = e(nc.sbuf_tensor("epsrow", [1, 1], F32))

    ALIASES = {
        "V": ("Va", "Vb"),
        "wqkv": ("wqkvA", "wqkvB"),
        "panA": ("panAa", "panAb"),
        "h": ("K", "Va", "Vb"),
        "h_lo": ("K",), "h_hi": ("Va", "Vb"),
        "w2a_": ("wqkvA",), "w2b_": ("wqkvB", "q", "PT0", "PT1"),
        "wtVa_": ("Va",), "wtVb_": ("Vb",),
        "wtPa_": ("panAa",), "wtPb_": ("panAb",),
        "xn2": ("panB",),
        "Mb": ("ostg",),
        "PTf0": ("yoc0",), "PTf1": ("yoc1",),
        "rows": ("rowsA", "rowsB"),
        "rowb": ("rowbA", "rowbB"),
        "B2": ("ctx2",),
        "z": ("PT0",),
        "wu": ("masks",),
    }

    pairA = e(nc.psum_tensor("pairA", [128, 1024], F32))
    pairB = e(nc.psum_tensor("pairB", [128, 1024], F32))
    ctxA = e(nc.psum_tensor("ctxA", [128, 512], F32))
    ctxB = e(nc.psum_tensor("ctxB", [128, 512], F32))
    auxP = e(nc.psum_tensor("auxP", [128, 512], F32))
    bcst = e(nc.psum_tensor("bcst", [128, 512], F32))

    p = Prog(nc, ctx, ALIASES)

    def frow(i):
        return rowf[0:1, i * TG:(i + 1) * TG]
    def brow(i):
        return rowb[0:1, i * TG:(i + 1) * TG]

    p.dma("sync", qkvb_sb[:, :], A["qkvb"], writes=["params"])
    p.dma("gpsimd", vbias_sb[:, :], A["vbias"], writes=["vbias_bf"])
    p.dma("sync", outb_sb[:, :], A["outb"], writes=["params"])
    for j, k in enumerate(("ln1s", "ln1b", "ln2s", "ln2b", "sbias")):
        p.dma("sync", parm_sb[:, j, :], A[k], writes=["params"])
    p.dma("sync", sublnw2[:, :], A["sublnw"], writes=["sublnw_raw"])
    for j, k in enumerate(("lamq1", "lamk1", "lamq2", "lamk2")):
        p.dma("sync", lam_sb[:, j:j + 1], A[k], writes=["lam"])
    p.dma("sync", masks_sb[:, :, :], A["masks"].rearrange("p (o t) -> p o t", o=4),
          writes=["masks"])
    p.dma("gpsimd", wqkv_sb[:, :, :], A["wqkv_s"], writes=["wqkv"])
    p.dma("gpsimd", wr_b[:, :, :], A["wr_s"], writes=["wr"])
    p.dma("gpsimd", wd_b[:, :, :], A["wd_s"], writes=["wd"])

    def memsets(v):
        v.memset(ones_bf[:, :], 1.0)
        v.memset(ones1r_bf[:, :], 1.0)
        v.memset(ones1r_f[:, :], 1.0)
        v.memset(cexp[:, :], -CSH)
        v.memset(czero[:, :], 0.0)
        v.memset(zrow[:, :], 0.0)
        return v.memset(epsrow[:, :], EPS)
    p.emit("vector", [memsets], writes=["consts"])
    p.emit("vector",
           [lambda v: v.tensor_scalar_mul(sublnw2[:, :], sublnw2[:, :], 1.0 - LI)],
           reads=["sublnw_raw"], writes=["sublnw_raw"])

    p.emit("tensor",
           [lambda t: t.matmul(auxP[0:1, 0:1], lam_sb[:, 0:1], lam_sb[:, 1:2],
                               start=True, stop=True)],
           reads=["lam"], writes=["aux"])
    p.emit("tensor",
           [lambda t: t.matmul(bcst[0:1, 0:1], lam_sb[:, 2:3], lam_sb[:, 3:4],
                               start=True, stop=True)],
           reads=["lam"], writes=["bcst"])
    p.emit("scalar",
           [lambda s: s.activation(rowf[0:1, 0:1], auxP[0:1, 0:1], AF.Exp,
                                   bias=zrow[0:1, 0:1], scale=1.0)],
           reads=["aux", "consts"], writes=["rows"])
    p.emit("scalar",
           [lambda s: s.activation(rowf[0:1, 1:2], bcst[0:1, 0:1], AF.Exp,
                                   bias=zrow[0:1, 0:1], scale=1.0)],
           reads=["bcst", "consts"], writes=["rows"])
    p.emit("vector",
           [lambda v: v.tensor_sub(rowf[0:1, 2:3], rowf[0:1, 1:2], rowf[0:1, 0:1])],
           reads=["rows"], writes=["rows"])
    p.emit("vector",
           [lambda v: v.tensor_scalar_add(rowf[0:1, 2:3], rowf[0:1, 2:3], -LI)],
           reads=["rows"], writes=["rows"])
    p.emit("tensor",
           [lambda t: t.matmul(bcst[:, 0:1], ones1r_f[0:1, :], rowf[0:1, 2:3],
                               start=True, stop=True)],
           reads=["rows", "consts"], writes=["bcst"])
    p.emit("vector", [lambda v: v.tensor_copy(negl_pp[:, :], bcst[:, 0:1])],
           reads=["bcst"], writes=["negl"])

    def bcast_row(src_row, hs, lsl, bank_ap, bank_res):
        hi, lo = brow(hs), brow(lsl)
        p.emit("vector", [lambda v: v.tensor_copy(hi, src_row)],
               reads=["rows"], writes=["rowb"])
        p.emit("vector", [lambda v: v.tensor_sub(lo, src_row, hi)],
               reads=["rows", "rowb"], writes=["rowb"])
        p.emit("tensor",
               [lambda t: t.matmul(bank_ap, ones1r_bf[0:1, :], hi, start=True, stop=False),
                lambda t: t.matmul(bank_ap, ones1r_bf[0:1, :], lo, start=False, stop=True)],
               reads=["rowb", "consts"], writes=[bank_res])

    def vres(tile):
        return "Va" if tile < 32 else "Vb"

    stage_jobs = []
    for c in range(16):
        stage_jobs.append((A["ff1_t"].rearrange("n p c k -> n p (c k)")[4 * c:4 * c + 4],
                           ff1_bf[4 * c:4 * c + 4], 4))
    for c in range(NOC):
        stage_jobs.append((A["ff2_t"].rearrange("n p c k -> n p (c k)")[c:c + 1],
                           ff2_bf[c:c + 1], 1))
    for c in range(4):
        stage_jobs.append((A["outw_t"].rearrange("n p c k -> n p (c k)")[4 * c:4 * c + 4],
                           outw_bf[4 * c:4 * c + 4], 4))
        stage_jobs.append((A["wo_t"].rearrange("n p c k -> n p (c k)")[4 * c:4 * c + 4],
                           wo_bf[4 * c:4 * c + 4], 4))
    stage_state = {"i": 0, "par": 0}

    def emit_stage(njobs):
        for _ in range(njobs):
            i = stage_state["i"]
            if i >= len(stage_jobs):
                return
            stage_state["i"] += 1
            srcap, dstap, ntile = stage_jobs[i]
            par = stage_state["par"]; stage_state["par"] ^= 1
            bv = F1_bf_view[:, par * 8192:(par + 1) * 8192]
            p.dma("gpsimd", bv.rearrange("p (n e) -> p n e", n=ntile),
                  srcap.rearrange("n p e -> p n e"),
                  reads=["F1"], writes=["F1", "stgb%d" % par])
            p.dma("sync", dstap.rearrange("n p e -> p n e"),
                  bv.rearrange("p (n e) -> p n e", n=ntile),
                  reads=["F1", "stgb%d" % par], writes=["wstaged"])

    PAIR_RES = [("pairA0", "pairA1"), ("pairB0", "pairB1")]
    PAIR_AP = [pairA, pairB]
    gp = {"par": 0}

    def x_panel_load(tg, buf, bufres):
        for qq in range(4):
            src = A["xT"][qq * 512:(qq + 1) * 512, tg * TG:(tg + 1) * TG]
            p.dma("gpsimd", buf[:, qq * 4:(qq + 1) * 4, :],
                  src.rearrange("(c p) t -> p c t", p=128), writes=[bufres])

    def xn_panel_load(tg, buf, bufres):
        rk, ch = tg // 2, tg % 2
        srcap = ag_xn_out[ch, rk * DM:(rk + 1) * DM, :]
        for qq in range(4):
            p.dma("sync", buf[:, qq * 4:(qq + 1) * 4, :],
                  srcap[qq * 512:(qq + 1) * 512, :].rearrange("(c p) t -> p c t", p=128),
                  reads=[f"ag_xn_out{ch}"], writes=[bufres])

    pidbox = {}

    def dyn_panel_load(tg_own, buf, bufres, src_dram, tag):
        reads = [f"ag_{tag}_out{j}" for j in range(16)]
        for qq in range(4):
            def f(e_, qq=qq, tg_own=tg_own, buf=buf, src_dram=src_dram):
                if "pid" not in pidbox:
                    pidbox["pid"] = e_.partition_id()
                src = src_dram[ds(pidbox["pid"] * 2 + tg_own, 1),
                               qq * 512:(qq + 1) * 512, :]
                return e_.dma_start(out=buf[:, qq * 4:(qq + 1) * 4, :],
                                    in_=src.rearrange("o (c p) t -> p (o c) t", p=128))
            p.emit("gpsimd", [f], reads=reads, writes=[bufres],
                   sem=p.chan("Dg_" + p.canon(bufres)[0]), amt=16)

    def proj_qk(tg, buf, bufres, with_bias):
        banks = [("pairB0", pairB[:, 0:512]), ("pairB1", pairB[:, 512:1024]),
                 ("pairB0", pairB[:, 0:512]), ("pairB1", pairB[:, 512:1024])]
        for j in range(4):
            res, bank = banks[j]
            fns = [(lambda t, fc=fc, j=j, bank=bank, buf=buf:
                    t.matmul(bank, wqkv_sb[:, fc, j * 128:(j + 1) * 128],
                             buf[:, fc, :], start=(fc == 0), stop=(fc == NOC - 1)))
                   for fc in range(NOC)]
            p.emit("tensor", fns, reads=["wqkv", bufres], writes=[res])
            if j < 2:
                dst, dres = q_sb[:, j, :], "q"
            else:
                dst, dres = K_sb[:, j - 2, tg * TG:(tg + 1) * TG], "K"
            if with_bias:
                p.emit("vector",
                       [lambda v, dst=dst, bank=bank, j=j:
                        v.tensor_scalar_add(dst, bank, qkvb_sb[:, j:j + 1])],
                       reads=[res, "params"], writes=[dres])
            else:
                p.emit("vector",
                       [lambda v, dst=dst, bank=bank: v.tensor_copy(dst, bank)],
                       reads=[res], writes=[dres])

    def proj_v(tg, buf, bufres, with_bias):
        for q4 in range(4):
            res = "pairA0" if q4 % 2 == 0 else "pairA1"
            bank_ap = pairA[:, 0:256] if q4 % 2 == 0 else pairA[:, 512:768]
            fns = [(lambda t, fc=fc, q4=q4, bank_ap=bank_ap, buf=buf:
                    t.matmul(bank_ap, buf[:, fc, q4 * 128:(q4 + 1) * 128],
                             wqkv_sb[:, fc, 512:768], start=(fc == 0),
                             stop=(fc == NOC - 1 and not with_bias)))
                   for fc in range(NOC)]
            if with_bias:
                fns.append(lambda t, bank_ap=bank_ap:
                           t.matmul(bank_ap, ones1r_bf[0:1, :], vbias_sb[0:1, :],
                                    start=False, stop=True))
            p.emit("tensor", fns, reads=["wqkv", bufres, "vbias_bf", "consts"],
                   writes=[res])
            tile = tg * 4 + q4
            p.emit("vector",
                   [lambda v, tile=tile, bank_ap=bank_ap:
                    v.tensor_copy(V_sb[:, tile, :], bank_ap)],
                   reads=[res], writes=[vres(tile)])

    def attn_matrix(tg, m, is_ffn):
        npair = (tg + 1) * 2

        def emit_scores(j):
            par = gp["par"]; gp["par"] ^= 1
            prA = PAIR_AP[par]
            r0, r1 = PAIR_RES[par]
            kb_a, kb_b = 2 * j, 2 * j + 1
            p.emit("tensor",
                   [lambda t, kb=kb_a, m=m, prA=prA:
                    t.matmul(prA[:, 0:512], K_sb[:, m, kb * 128:(kb + 1) * 128],
                             q_sb[:, m, :], start=True, stop=True),
                    lambda t, kb=kb_b, m=m, prA=prA:
                    t.matmul(prA[:, 512:1024], K_sb[:, m, kb * 128:(kb + 1) * 128],
                             q_sb[:, m, :], start=True, stop=True)],
                   reads=["K", "q"], writes=[r0, r1])
            ptres = f"PT{par}"
            p.emit("scalar",
                   [lambda s_, par=par, prA=prA:
                    s_.activation(PT_sb[:, par, :], prA[:, :], AF.Exp,
                                  bias=cexp[:, 0:1], scale=SCL)],
                   reads=[r0, r1, "consts"], writes=[ptres])
            if "pt" in dbg_d and not is_ffn and tg == 3 and m == 0 and j < 2:
                p.dma("sync", dbg_d["pt"][:, j * 1024:(j + 1) * 1024],
                      PT_sb[:, par, :], reads=[ptres], writes=["dbg"])
            if j >= npair - 2:
                off = kb_a - tg * 4
                p.emit("vector",
                       [lambda v, par=par, off=off:
                        v.tensor_mul(PT_sb[:, par, 0:512], PT_sb[:, par, 0:512],
                                     masks_sb[:, off, :]),
                        lambda v, par=par, off=off:
                        v.tensor_mul(PT_sb[:, par, 512:1024], PT_sb[:, par, 512:1024],
                                     masks_sb[:, off + 1, :])],
                       reads=[ptres, "masks"], writes=[ptres])
            return par

        def emit_ctx(j, par):
            kb_a, kb_b = 2 * j, 2 * j + 1
            ptres = f"PT{par}"
            first = (j == 0)
            lastp = (j == npair - 1)
            fns = []
            for half, kb in ((0, kb_a), (1, kb_b)):
                st = first and half == 0
                sp = lastp and half == 1
                if is_ffn:
                    fns.append(lambda t, kb=kb, m=m, par=par, half=half, st=st, sp=sp:
                               t.matmul(ctxA[:, :], V_sb[:, kb, m * 128:(m + 1) * 128],
                                        PT_sb[:, par, half * 512:(half + 1) * 512],
                                        start=st, stop=sp))
                else:
                    for ech in range(2):
                        bank = ctxA if ech == 0 else ctxB
                        fns.append(lambda t, kb=kb, ech=ech, par=par, half=half,
                                   st=st, sp=sp, bank=bank:
                                   t.matmul(bank[:, :],
                                            V_sb[:, kb, ech * 128:(ech + 1) * 128],
                                            PT_sb[:, par, half * 512:(half + 1) * 512],
                                            start=st, stop=sp))
            fns.append(lambda t, par=par, first=first:
                       t.matmul(auxP[0:1, :], ones_bf[:, 0:1],
                                PT_sb[:, par, 0:512],
                                start=first, stop=False))
            fns.append(lambda t, par=par, lastp=lastp:
                       t.matmul(auxP[0:1, :], ones_bf[:, 0:1],
                                PT_sb[:, par, 512:1024],
                                start=False, stop=lastp))
            wr = (["ctxA", "aux"] if is_ffn else ["ctxA", "ctxB", "aux"])
            rd = [ptres, "consts", vres(kb_a), vres(kb_b)]
            p.emit("tensor", fns, reads=rd, writes=wr)

        return emit_scores, emit_ctx, npair

    def fr(P, k):
        return rowf[0:1, (4 * P + k) * TG:(4 * P + k + 1) * TG]

    def bw(P, k):
        return rowb[0:1, (2 * P + k) * TG:(2 * P + k + 1) * TG]

    def rres(P):
        return "rowsA" if P == 0 else "rowsB"

    def bres_(P):
        return "rowbA" if P == 0 else "rowbB"

    def seg_rden(P, slots):
        def go():
            p.emit("scalar",
                   [(lambda s_, d=d, r=r: s_.activation(fr(P, r), fr(P, d), AF.Ln,
                                                        bias=zrow[0:1, 0:1], scale=1.0))
                    for d, r in slots],
                   reads=[rres(P), "consts"], writes=[rres(P)])
            p.emit("scalar",
                   [(lambda s_, r=r: s_.activation(fr(P, r), fr(P, r), AF.Exp,
                                                   bias=zrow[0:1, 0:1], scale=-1.0))
                    for d, r in slots],
                   reads=[rres(P), "consts"], writes=[rres(P)])
        return go

    def seg_bcast(P, src_slot):
        def go():
            p.emit("vector", [lambda v: v.tensor_copy(bw(P, 0), fr(P, src_slot))],
                   reads=[rres(P)], writes=[bres_(P)])
            p.emit("vector", [lambda v: v.tensor_sub(bw(P, 1), fr(P, src_slot), bw(P, 0))],
                   reads=[rres(P), bres_(P)], writes=[bres_(P)])
            p.emit("tensor",
                   [lambda t: t.matmul(bcst[:, :], ones1r_bf[0:1, :], bw(P, 0),
                                       start=True, stop=False),
                    lambda t: t.matmul(bcst[:, :], ones1r_bf[0:1, :], bw(P, 1),
                                       start=False, stop=True)],
                   reads=[bres_(P), "consts"], writes=["bcst"])
        return go

    def seg_row_bcast_b(P, bslot):
        def go():
            p.emit("tensor",
                   [lambda t: t.matmul(bcst[:, :], ones1r_bf[0:1, :], bw(P, bslot),
                                       start=True, stop=True)],
                   reads=[bres_(P), "consts"], writes=["bcst"])
        return go

    def evac_ffn_head(tg, m):
        P = tg % 2
        p.emit("vector",
               [lambda v, m=m: v.tensor_copy(o_f32[:, m, :], ctxA[:, :])],
               reads=["ctxA"], writes=["o32"])
        p.emit("vector",
               [lambda v, P=P, m=m: v.tensor_copy(fr(P, 2 * m), auxP[0:1, :])],
               reads=["aux"], writes=[rres(P)])

    def ffn_tail_segs(tg, m, dstD, dstO, tag):
        P = tg % 2
        segs = [seg_rden(P, [(2 * m, 2 * m + 1)]),
                seg_bcast(P, 2 * m + 1)]

        def s_mult():
            p.emit("vector",
                   [lambda v, m=m: v.tensor_mul(ostg[:, m, :], o_f32[:, m, :],
                                                bcst[:, :])],
                   reads=["o32", "bcst"], writes=["ostg"])
        segs.append(s_mult)
        if m == 1:
            def s_store():
                p.dma("sync", dstD[tg].rearrange("(c p) t -> p c t", p=128),
                      ostg[:, :, :], reads=["ostg"], writes=[f"ag_{tag}_in{tg}"])
                p.emit("gpsimd",
                       [lambda g: g.collective_compute(
                           "AllGather", ALU.bypass,
                           replica_groups=[list(range(8))],
                           ins=[dstD[tg]], outs=[dstO[tg]])],
                       reads=[f"ag_{tag}_in{tg}"], writes=[f"ag_{tag}_out{tg}"],
                       sem=p.chan(f"CC{tag}{tg}"), amt=1)
            segs.append(s_store)
        return segs

    def evac_attn1_p1(tg):
        P = tg % 2
        p.emit("vector",
               [lambda v: v.tensor_copy(ctx1_sb[:, 0, :], ctxA[:, :]),
                lambda v: v.tensor_copy(ctx1_sb[:, 1, :], ctxB[:, :])],
               reads=["ctxA", "ctxB"], writes=["ctx1"])
        p.emit("vector",
               [lambda v, P=P: v.tensor_copy(fr(P, 0), auxP[0:1, :])],
               reads=["aux"], writes=[rres(P)])

    def evac_attn1_p2_head(tg):
        P = tg % 2
        p.emit("vector",
               [lambda v: v.tensor_copy(ctx2_sb[:, 0, :], ctxA[:, :]),
                lambda v: v.tensor_copy(ctx2_sb[:, 1, :], ctxB[:, :])],
               reads=["ctxA", "ctxB"], writes=["ctx2"])
        p.emit("vector",
               [lambda v, P=P: v.tensor_copy(fr(P, 1), auxP[0:1, :])],
               reads=["aux"], writes=[rres(P)])

    def attn1_tail_segs(tg):
        P = tg % 2
        segs = [seg_rden(P, [(0, 2), (1, 3)]),
                seg_bcast(P, 2)]

        def s_t1():
            for ech in range(2):
                p.emit("vector",
                       [lambda v, ech=ech: v.tensor_mul(o_f32[:, ech, :],
                                                        ctx1_sb[:, ech, :], bcst[:, :])],
                       reads=["ctx1", "bcst"], writes=["o32"])
        segs.append(s_t1)
        segs.append(seg_bcast(P, 3))

        def s_u2():
            for ech in range(2):
                p.emit("vector",
                       [lambda v, ech=ech: v.tensor_mul(ctx2_sb[:, ech, :],
                                                        ctx2_sb[:, ech, :], bcst[:, :])],
                       reads=["ctx2", "bcst"], writes=["ctx2"])
                p.emit("vector",
                       [lambda v, ech=ech: v.scalar_tensor_tensor(
                           o_f32[:, ech, :], ctx2_sb[:, ech, :], negl_pp[:, 0:1],
                           o_f32[:, ech, :], ALU.mult, ALU.add)],
                       reads=["ctx2", "negl", "o32"], writes=["o32"])
        segs.append(s_u2)

        def s_sq():
            p.emit("vector",
                   [lambda v: v.tensor_copy(cast_sb[:, 0, :], o_f32[:, 0, :]),
                    lambda v: v.tensor_copy(cast_sb[:, 1, :], o_f32[:, 1, :])],
                   reads=["o32"], writes=["cast"])
            p.emit("vector",
                   [lambda v: v.tensor_mul(cast_sb[:, 0, :], cast_sb[:, 0, :],
                                           cast_sb[:, 0, :]),
                    lambda v: v.tensor_mul(cast_sb[:, 1, :], cast_sb[:, 1, :],
                                           cast_sb[:, 1, :])],
                   reads=["cast"], writes=["cast"])
        segs.append(s_sq)

        def s_ssq():
            p.emit("tensor",
                   [lambda t: t.matmul(bcst[0:1, :], ones_bf[:, 0:1], cast_sb[:, 0, :],
                                       start=True, stop=False),
                    lambda t: t.matmul(bcst[0:1, :], ones_bf[:, 0:1], cast_sb[:, 1, :],
                                       start=False, stop=True)],
                   reads=["cast", "consts"], writes=["bcst"])
        segs.append(s_ssq)

        def s_rstd():
            p.emit("scalar",
                   [lambda s_, P=P: s_.activation(fr(P, 2), bcst[0:1, :], AF.Ln,
                                                  bias=epsrow[0:1, 0:1],
                                                  scale=1.0 / 256.0)],
                   reads=["bcst", "consts"], writes=[rres(P)])
            p.emit("scalar",
                   [lambda s_, P=P: s_.activation(fr(P, 2), fr(P, 2), AF.Exp,
                                                  bias=zrow[0:1, 0:1], scale=-0.5)],
                   reads=[rres(P), "consts"], writes=[rres(P)])
        segs.append(s_rstd)
        segs.append(seg_bcast(P, 2))

        def s_subln():
            p.emit("vector",
                   [(lambda v, ech=ech: v.scalar_tensor_tensor(
                       ostg[:, ech, :], o_f32[:, ech, :], sublnw2[:, ech:ech + 1],
                       bcst[:, :], ALU.mult, ALU.mult)) for ech in range(2)],
                   reads=["o32", "bcst", "sublnw_raw"], writes=["ostg"])
        segs.append(s_subln)

        def s_store():
            p.dma("sync", ag_o_in[tg].rearrange("(c p) t -> p c t", p=128),
                  ostg[:, :, :], reads=["ostg"], writes=[f"ag_o_in{tg}"])
            p.emit("gpsimd",
                   [lambda g: g.collective_compute(
                       "AllGather", ALU.bypass,
                       replica_groups=[list(range(8))],
                       ins=[ag_o_in[tg]], outs=[ag_o_out[tg]])],
                   reads=[f"ag_o_in{tg}"], writes=[f"ag_o_out{tg}"],
                   sem=p.chan(f"CCo{tg}"), amt=1)
        segs.append(s_store)
        return segs

    def attn_phase(is_ffn):
        from collections import deque
        pend = deque()

        def hook():
            if pend:
                pend.popleft()()

        for tg in range(2):
            buf, bufres = (panA, "panA") if tg % 2 == 0 else (panB, "panB")
            if is_ffn:
                xn_panel_load(tg, buf, bufres)
            else:
                x_panel_load(tg, buf, bufres)
        for tg in range(NTG):
            buf, bufres = (panA, "panA") if tg % 2 == 0 else (panB, "panB")
            proj_qk(tg, buf, bufres, with_bias=not is_ffn)
            proj_v(tg, buf, bufres, with_bias=not is_ffn)
            if tg + 2 < NTG:
                nbuf, nbres = (panA, "panA") if tg % 2 == 0 else (panB, "panB")
                if is_ffn:
                    xn_panel_load(tg + 2, nbuf, nbres)
                else:
                    x_panel_load(tg + 2, nbuf, nbres)
            if not is_ffn:
                emit_stage(2 if tg < 4 else 5)
            es0, ec0, np0 = attn_matrix(tg, 0, is_ffn)
            es1, ec1, np1 = attn_matrix(tg, 1, is_ffn)
            stream = [(es0, ec0, j) for j in range(np0)] + \
                     [(es1, ec1, j) for j in range(np1)]
            par_prev = stream[0][0](stream[0][2])
            for idx in range(len(stream)):
                par_j = par_prev
                if idx + 1 < len(stream):
                    par_prev = stream[idx + 1][0](stream[idx + 1][2])
                stream[idx][1](stream[idx][2], par_j)
                hook()
                if idx == np0 - 1:
                    while pend:
                        pend.popleft()()
                    if is_ffn:
                        evac_ffn_head(tg, 0)
                        pend.extend(ffn_tail_segs(tg, 0, ag_cx_in, ag_cx_out, "cx"))
                    else:
                        evac_attn1_p1(tg)
            while pend:
                pend.popleft()()
            if is_ffn:
                evac_ffn_head(tg, 1)
                pend.extend(ffn_tail_segs(tg, 1, ag_cx_in, ag_cx_out, "cx"))
            else:
                evac_attn1_p2_head(tg)
                pend.extend(attn1_tail_segs(tg))
        while pend:
            pend.popleft()()

    attn_phase(is_ffn=False)
    emit_stage(len(stage_jobs))

    if "k1" in dbg_d:
        p.dma("gpsimd", dbg_d["k1"], R1[:, :], reads=["K"], writes=["dbg"])
    if "v1" in dbg_d:
        p.dma("gpsimd", dbg_d["v1"], R2[:, :], reads=["V"], writes=["dbg"])
    if "q1" in dbg_d:
        p.dma("gpsimd", dbg_d["q1"], R3[:, 12288:14336], reads=["q"], writes=["dbg"])

    def ln_rows(sum_res, sum_ap, ssq_res, ssq_ap, n):
        p.emit("vector",
               [lambda v: v.tensor_scalar_mul(frow(0), sum_ap, 1.0 / n)],
               reads=[sum_res], writes=["rows"])
        p.emit("vector",
               [lambda v: v.tensor_mul(frow(1), frow(0), frow(0))],
               reads=["rows"], writes=["rows"])
        p.emit("vector",
               [lambda v: v.scalar_tensor_tensor(frow(1), ssq_ap, 1.0 / n, frow(1),
                                                 ALU.mult, ALU.subtract)],
               reads=[ssq_res, "rows"], writes=["rows"])
        p.emit("scalar",
               [lambda s_: s_.activation(frow(2), frow(1), AF.Ln,
                                         bias=epsrow[0:1, 0:1], scale=1.0)],
               reads=["rows", "consts"], writes=["rows"])
        p.emit("scalar",
               [lambda s_: s_.activation(frow(2), frow(2), AF.Exp,
                                         bias=zrow[0:1, 0:1], scale=-0.5)],
               reads=["rows", "consts"], writes=["rows"])

    def mean_rstd_bcast():
        bcast_row(frow(0), 0, 1, bcst[:, :], "bcst")
        p.emit("vector", [lambda v: v.tensor_copy(B2_sb[:, 0, :], bcst[:, :])],
               reads=["bcst"], writes=["B2"])
        bcast_row(frow(2), 2, 3, bcst[:, :], "bcst")
        p.emit("vector", [lambda v: v.tensor_copy(B2_sb[:, 1, :], bcst[:, :])],
               reads=["bcst"], writes=["B2"])

    for tgo in range(OTG):
        dyn_panel_load(tgo, panA, "panA", ag_o_out, "o")
        def c_mm(oc):
            wbuf = wtVa if oc % 2 == 0 else wtVb
            wres = "wtVa_" if oc % 2 == 0 else "wtVb_"
            p.dma("sync", wbuf[:, :, :],
                  outw_bf[oc].rearrange("p (c k) -> p c k", c=NOC),
                  reads=["wstaged"], writes=[wres])
            bank, bres = (ctxA, "ctxA") if oc % 2 == 0 else (ctxB, "ctxB")
            fns = [(lambda t, fc=fc, wbuf=wbuf, bank=bank:
                    t.matmul(bank[:, :], wbuf[:, fc, :], panA[:, fc, :],
                             start=(fc == 0), stop=(fc == NOC - 1)))
                   for fc in range(NOC)]
            p.emit("tensor", fns, reads=[wres, "panA"], writes=[bres])
            xb = yoc[:, oc % 2, :]
            xres = "yoc0" if oc % 2 == 0 else "yoc1"
            p.dma("sync", xb, A["x_ownT"][oc * 128:(oc + 1) * 128,
                                          tgo * TG:(tgo + 1) * TG], writes=[xres])
            return bank, bres, xb, xres

        def c_evac(oc, bank, bres, xb, xres):
            p.emit("vector",
                   [lambda v, oc=oc, bank=bank, xb=xb:
                    v.scalar_tensor_tensor(F1[:, oc, :], bank[:, :],
                                           outb_sb[:, oc:oc + 1], xb,
                                           ALU.add, ALU.add)],
                   reads=[bres, xres, "params"], writes=["F1"])
            p.emit("vector",
                   [lambda v, oc=oc: v.tensor_copy(cast_sb[:, 0, :], F1[:, oc, :])],
                   reads=["F1"], writes=["cast"])
            p.emit("vector",
                   [lambda v: v.tensor_mul(cast_sb[:, 1, :], cast_sb[:, 0, :],
                                           cast_sb[:, 0, :])],
                   reads=["cast"], writes=["cast"])
            p.emit("tensor",
                   [lambda t, oc=oc: t.matmul(auxP[0:1, :], ones_bf[:, 0:1],
                                              cast_sb[:, 0, :], start=(oc == 0),
                                              stop=(oc == NOC - 1)),
                    lambda t, oc=oc: t.matmul(bcst[0:1, :], ones_bf[:, 0:1],
                                              cast_sb[:, 1, :], start=(oc == 0),
                                              stop=(oc == NOC - 1))],
                   reads=["cast", "consts"], writes=["aux", "bcst"])

        prevc = c_mm(0)
        for oc in range(NOC):
            curc = prevc
            if oc + 1 < NOC:
                prevc = c_mm(oc + 1)
            c_evac(oc, *curc)
        ln_rows("aux", auxP[0:1, :], "bcst", bcst[0:1, :], DM)
        mean_rstd_bcast()
        for oc in range(NOC):
            p.emit("vector",
                   [lambda v, oc=oc: v.tensor_sub(F1[:, oc, :], F1[:, oc, :],
                                                  B2_sb[:, 0, :])],
                   reads=["B2", "F1"], writes=["F1"])
            p.emit("vector",
                   [lambda v, oc=oc: v.tensor_mul(F1[:, oc, :], F1[:, oc, :],
                                                  B2_sb[:, 1, :])],
                   reads=["B2", "F1"], writes=["F1"])
            p.emit("vector",
                   [lambda v, oc=oc: v.tensor_scalar(panB[:, oc, :], F1[:, oc, :],
                                                     parm_sb[:, 0, oc:oc + 1],
                                                     parm_sb[:, 1, oc:oc + 1],
                                                     ALU.mult, ALU.add)],
                   reads=["F1", "params"], writes=["panB"])
        p.dma("sync", y_spill[:, tgo * TG:(tgo + 1) * TG].rearrange(
                  "(c p) t -> p c t", p=128), F1[:, :, :],
              reads=["F1"], writes=["y_spill"])
        p.dma("sync", ag_xn_in[tgo].rearrange("(c p) t -> p c t", p=128),
              panB[:, :, :], reads=["panB"], writes=[f"ag_xn_in{tgo}"])
        p.emit("gpsimd",
               [lambda g, tgo=tgo: g.collective_compute(
                   "AllGather", ALU.bypass, replica_groups=[list(range(8))],
                   ins=[ag_xn_in[tgo]], outs=[ag_xn_out[tgo]])],
               reads=[f"ag_xn_in{tgo}"], writes=[f"ag_xn_out{tgo}"],
               sem=p.chan(f"CCxn{tgo}"), amt=1)


    p.dma("gpsimd", wqkv_sb[:, :, :], A["ffnqkv_s"], writes=["wqkv"])
    attn_phase(is_ffn=True)


    p.dma("gpsimd", wu_sb[0:32, :], A["wuT"], writes=["wu"])
    for tgo in range(OTG):
        dyn_panel_load(tgo, panA, "panA", ag_cx_out, "cx")
        for qq in range(4):
            p.dma("gpsimd", panB[:, qq * 4:(qq + 1) * 4, :],
                  y_spill[qq * 512:(qq + 1) * 512, tgo * TG:(tgo + 1) * TG]
                  .rearrange("(c p) t -> p c t", p=128),
                  reads=["y_spill"], writes=["panB"])
        rbank = [("ctxA", ctxA[0:1, :]), ("ctxB", ctxB[0:1, :]), ("aux", auxP[0:1, :])]
        for rj in range(3):
            res, ap_ = rbank[rj]
            fns = [(lambda t, fc=fc, rj=rj, ap_=ap_:
                    t.matmul(ap_, wr_b[:, fc, rj:rj + 1], panB[:, fc, :],
                             start=(fc == 0), stop=(fc == NOC - 1)))
                   for fc in range(NOC)]
            p.emit("tensor", fns, reads=["wr", "panB"], writes=[res])
        p.emit("vector",
               [lambda v: v.tensor_copy(frow(0), ctxA[0:1, :]),
                lambda v: v.tensor_copy(frow(1), ctxB[0:1, :]),
                lambda v: v.tensor_copy(frow(2), auxP[0:1, :])],
               reads=["ctxA", "ctxB", "aux"], writes=["rows"])
        p.emit("vector",
               [lambda v: v.tensor_tensor(frow(3), frow(0), frow(1), ALU.is_ge),
                lambda v: v.tensor_tensor(frow(4), frow(0), frow(2), ALU.is_ge)],
               reads=["rows"], writes=["rows"])
        p.emit("vector", [lambda v: v.tensor_mul(frow(3), frow(3), frow(4))],
               reads=["rows"], writes=["rows"])
        p.emit("vector",
               [lambda v: v.tensor_tensor(frow(4), frow(1), frow(0), ALU.is_gt),
                lambda v: v.tensor_tensor(frow(5), frow(1), frow(2), ALU.is_ge)],
               reads=["rows"], writes=["rows"])
        p.emit("vector", [lambda v: v.tensor_mul(frow(4), frow(4), frow(5))],
               reads=["rows"], writes=["rows"])
        p.emit("vector",
               [lambda v: v.tensor_copy(brow(0), frow(3)),
                lambda v: v.tensor_copy(brow(1), frow(4))],
               reads=["rows"], writes=["rowb"])
        for mj in range(2):
            p.emit("tensor",
                   [lambda t, mj=mj: t.matmul(bcst[:, :], ones1r_bf[0:1, :],
                                              brow(mj), start=True, stop=True)],
                   reads=["rowb", "consts"], writes=["bcst"])
            p.emit("vector",
                   [lambda v, mj=mj: v.tensor_copy(Mb_sb[:, mj, :], bcst[:, :])],
                   reads=["bcst"], writes=["Mb"])
        fns = [(lambda t, fc=fc: t.matmul(auxP[0:32, :], wd_b[:, fc, :],
                                          panB[:, fc, :], start=(fc == 0),
                                          stop=(fc == NOC - 1)))
               for fc in range(NOC)]
        p.emit("tensor", fns, reads=["wd", "panB"], writes=["aux"])
        p.emit("vector", [lambda v: v.tensor_copy(z_sb[0:32, :], auxP[0:32, :])],
               reads=["aux"], writes=["z"])
        def w_mm(oc):
            wbuf = wtVa if oc % 2 == 0 else wtVb
            wres = "wtVa_" if oc % 2 == 0 else "wtVb_"
            p.dma("sync", wbuf[:, :, :],
                  wo_bf[oc].rearrange("p (c k) -> p c k", c=NOC),
                  reads=["wstaged"], writes=[wres])
            bank, bres = (ctxA, "ctxA") if oc % 2 == 0 else (ctxB, "ctxB")
            fns = [(lambda t, fc=fc, wbuf=wbuf, bank=bank:
                    t.matmul(bank[:, :], wbuf[:, fc, :], panA[:, fc, :],
                             start=(fc == 0), stop=(fc == NOC - 1)))
                   for fc in range(NOC)]
            p.emit("tensor", fns, reads=[wres, "panA"], writes=[bres])
            xb = yoc[:, oc % 2, :]
            xres = "yoc0" if oc % 2 == 0 else "yoc1"
            p.dma("sync", xb, y_spill[oc * 128:(oc + 1) * 128,
                                      tgo * TG:(tgo + 1) * TG],
                  reads=["y_spill"], writes=[xres])
            return bank, bres, xb, xres

        def w_evac(oc, bank, bres, xb, xres):
            p.emit("vector",
                   [lambda v, oc=oc, bank=bank, xb=xb:
                    v.tensor_add(F1[:, oc, :], bank[:, :], xb)],
                   reads=[bres, xres], writes=["F1"])
            p.emit("vector",
                   [lambda v, oc=oc: v.tensor_copy(cast_sb[:, 0, :], F1[:, oc, :])],
                   reads=["F1"], writes=["cast"])
            p.emit("vector",
                   [lambda v: v.tensor_mul(cast_sb[:, 1, :], cast_sb[:, 0, :],
                                           cast_sb[:, 0, :])],
                   reads=["cast"], writes=["cast"])
            p.emit("tensor",
                   [lambda t, oc=oc: t.matmul(pairB[0:1, 512:1024], ones_bf[:, 0:1],
                                              cast_sb[:, 0, :], start=(oc == 0),
                                              stop=(oc == NOC - 1)),
                    lambda t, oc=oc: t.matmul(pairB[0:1, 0:512], ones_bf[:, 0:1],
                                              cast_sb[:, 1, :], start=(oc == 0),
                                              stop=(oc == NOC - 1))],
                   reads=["cast", "consts"], writes=["pairB1", "pairB0"])

        prevw = w_mm(0)
        for oc in range(NOC):
            curw = prevw
            if oc + 1 < NOC:
                prevw = w_mm(oc + 1)
            w_evac(oc, *curw)
        ln_rows("pairB1", pairB[0:1, 512:1024], "pairB0", pairB[0:1, 0:512], DM)
        mean_rstd_bcast()
        for oc in range(NOC):
            p.emit("vector",
                   [lambda v, oc=oc: v.tensor_sub(o_f32[:, 0, :], F1[:, oc, :],
                                                  B2_sb[:, 0, :])],
                   reads=["F1", "B2"], writes=["o32"])
            p.emit("vector",
                   [lambda v: v.tensor_mul(o_f32[:, 0, :], o_f32[:, 0, :],
                                           B2_sb[:, 1, :])],
                   reads=["o32", "B2"], writes=["o32"])
            p.emit("vector",
                   [lambda v, oc=oc: v.tensor_scalar(xn2_sb[:, oc, :], o_f32[:, 0, :],
                                                     parm_sb[:, 2, oc:oc + 1],
                                                     parm_sb[:, 3, oc:oc + 1],
                                                     ALU.mult, ALU.add)],
                   reads=["o32", "params"], writes=["xn2"])
        for hc in range(64):
            wbuf = wtPa if hc % 2 == 0 else wtPb
            wres = "wtPa_" if hc % 2 == 0 else "wtPb_"
            p.dma("sync", wbuf[:, :, :],
                  ff1_bf[hc].rearrange("p (c k) -> p c k", c=NOC),
                  reads=["wstaged"], writes=[wres])
            bank, bres = (ctxA, "ctxA") if hc % 2 == 0 else (ctxB, "ctxB")
            fns = [(lambda t, fc=fc, wbuf=wbuf, bank=bank:
                    t.matmul(bank[:, :], wbuf[:, fc, :], xn2_sb[:, fc, :],
                             start=(fc == 0), stop=(fc == NOC - 1)))
                   for fc in range(NOC)]
            p.emit("tensor", fns, reads=[wres, "xn2"], writes=[bres])
            hdst = h_lo[:, hc, :] if hc < 32 else h_hi[:, hc - 32, :]
            hres = "h_lo" if hc < 32 else "h_hi"
            p.emit("scalar",
                   [lambda s_, hdst=hdst, bank=bank:
                    s_.activation(hdst, bank[:, :], AF.Gelu_apprx_tanh,
                                  bias=czero[:, 0:1], scale=1.0)],
                   reads=[bres, "consts"], writes=[hres])
        def f_mm(oc):
            wbuf2 = w2a if oc % 2 == 0 else w2b
            wres2 = "w2a_" if oc % 2 == 0 else "w2b_"
            p.dma("sync", wbuf2[:, :, :],
                  ff2_bf[oc].rearrange("p (c k) -> p c k", c=64),
                  reads=["wstaged"], writes=[wres2])
            bank_ap = [pairA[:, 0:512], pairA[:, 512:1024], ctxA[:, :], ctxB[:, :]][oc % 4]
            bres = ["pairA0", "pairA1", "ctxA", "ctxB"][oc % 4]
            fns = [(lambda t, hc=hc, wbuf2=wbuf2, bank_ap=bank_ap:
                    t.matmul(bank_ap, wbuf2[:, hc, :],
                             (h_lo[:, hc, :] if hc < 32 else h_hi[:, hc - 32, :]),
                             start=(hc == 0), stop=(hc == 63)))
                   for hc in range(64)]
            p.emit("tensor", fns, reads=[wres2, "h_lo", "h_hi"], writes=[bres])
            up_ap = pairB[:, 0:512] if oc % 2 == 0 else pairB[:, 512:1024]
            upres = "pairB0" if oc % 2 == 0 else "pairB1"
            p.emit("tensor",
                   [lambda t, oc=oc, up_ap=up_ap:
                    t.matmul(up_ap, wu_sb[0:32, oc * 128:(oc + 1) * 128],
                             z_sb[0:32, :], start=True, stop=True)],
                   reads=["wu", "z"], writes=[upres])
            xb = yoc[:, oc % 2, :]
            xres = "yoc0" if oc % 2 == 0 else "yoc1"
            p.dma("sync", xb, y_spill[oc * 128:(oc + 1) * 128,
                                      tgo * TG:(tgo + 1) * TG],
                  reads=["y_spill"], writes=[xres])
            return bank_ap, bres, up_ap, upres, xb, xres

        def f_evac(oc, bank_ap, bres, up_ap, upres, xb, xres):
            p.emit("vector",
                   [lambda v, oc=oc, bank_ap=bank_ap:
                    v.tensor_add(F1[:, oc, :], bank_ap, F1[:, oc, :])],
                   reads=[bres, "F1"], writes=["F1"])
            p.emit("vector",
                   [lambda v, oc=oc, xb=xb: v.scalar_tensor_tensor(
                       ctx1_sb[:, 0, :], xb, parm_sb[:, 4, oc:oc + 1], F1[:, oc, :],
                       ALU.add, ALU.subtract)],
                   reads=[xres, "params", "F1"], writes=["ctx1"])
            p.emit("vector",
                   [lambda v: v.tensor_mul(ctx1_sb[:, 0, :], ctx1_sb[:, 0, :],
                                           Mb_sb[:, 0, :])],
                   reads=["ctx1", "Mb"], writes=["ctx1"])
            p.emit("vector",
                   [lambda v, xb=xb, up_ap=up_ap:
                    v.tensor_add(ctx1_sb[:, 1, :], up_ap, xb)],
                   reads=[upres, xres], writes=["ctx1"])
            p.emit("vector",
                   [lambda v, oc=oc: v.tensor_sub(ctx1_sb[:, 1, :], ctx1_sb[:, 1, :],
                                                  F1[:, oc, :])],
                   reads=["ctx1", "F1"], writes=["ctx1"])
            p.emit("vector",
                   [lambda v: v.tensor_mul(ctx1_sb[:, 1, :], ctx1_sb[:, 1, :],
                                           Mb_sb[:, 1, :])],
                   reads=["ctx1", "Mb"], writes=["ctx1"])
            p.emit("vector",
                   [lambda v, oc=oc, xb=xb: v.tensor_add(F1[:, oc, :], F1[:, oc, :], xb)],
                   reads=["F1", xres], writes=["F1"])
            p.emit("vector",
                   [lambda v: v.tensor_add(ctx1_sb[:, 0, :], ctx1_sb[:, 0, :],
                                           ctx1_sb[:, 1, :])],
                   reads=["ctx1"], writes=["ctx1"])
            p.emit("vector",
                   [lambda v, oc=oc: v.tensor_add(F1[:, oc, :], F1[:, oc, :],
                                                  ctx1_sb[:, 0, :])],
                   reads=["F1", "ctx1"], writes=["F1"])
            p.emit("vector",
                   [lambda v, oc=oc: v.tensor_copy(cast_sb[:, 0, :], F1[:, oc, :])],
                   reads=["F1"], writes=["cast"])
            p.emit("vector",
                   [lambda v: v.tensor_mul(cast_sb[:, 1, :], cast_sb[:, 0, :],
                                           cast_sb[:, 0, :])],
                   reads=["cast"], writes=["cast"])
            p.emit("tensor",
                   [lambda t, oc=oc: t.matmul(auxP[0:1, :], ones_bf[:, 0:1],
                                              cast_sb[:, 0, :], start=(oc == 0),
                                              stop=(oc == NOC - 1)),
                    lambda t, oc=oc: t.matmul(bcst[0:1, :], ones_bf[:, 0:1],
                                              cast_sb[:, 1, :], start=(oc == 0),
                                              stop=(oc == NOC - 1))],
                   reads=["cast", "consts"], writes=["aux", "bcst"])

        prevf = f_mm(0)
        for oc in range(NOC):
            curf = prevf
            if oc + 1 < NOC:
                prevf = f_mm(oc + 1)
            f_evac(oc, *curf)
        ln_rows("aux", auxP[0:1, :], "bcst", bcst[0:1, :], DM)
        mean_rstd_bcast()
        for oc in range(NOC):
            p.emit("vector",
                   [lambda v, oc=oc: v.tensor_sub(F1[:, oc, :], F1[:, oc, :],
                                                  B2_sb[:, 0, :])],
                   reads=["B2", "F1"], writes=["F1"])
            p.emit("vector",
                   [lambda v, oc=oc: v.tensor_mul(F1[:, oc, :], F1[:, oc, :],
                                                  B2_sb[:, 1, :])],
                   reads=["B2", "F1"], writes=["F1"])
        p.dma("sync", out_d[:, tgo * TG:(tgo + 1) * TG].rearrange(
                  "(c p) t -> p c t", p=128), F1[:, :, :],
              reads=["F1"], writes=["outd"])

    if "lg" in dbg_d:
        p.dma("sync", dbg_d["lg"], lam_sb[:, :], reads=["lam"], writes=["dbg"])
    if "ng" in dbg_d:
        p.dma("sync", dbg_d["ng"], negl_pp[:, :], reads=["negl"], writes=["dbg"])
    if "y" in dbg_d:
        p.dma("gpsimd", dbg_d["y"], y_spill, reads=["y_spill"], writes=["dbg"])

    fin = {s: v for s, v in p.cnt.items()
           if (s.startswith("Ds_") or s.startswith("Dg_") or s.startswith("CC")
               or s.startswith("PF")) and v > 0}
    def final(e_):
        for s, v in fin.items():
            if v:
                e_.wait_ge(p.sem[s], v)
    p.q["sync"].append(final)

    with nc.Block() as block:
        @block.tensor
        def _(t):
            for f in p.q["tensor"]:
                f(t)

        @block.scalar
        def _(s_):
            for f in p.q["scalar"]:
                f(s_)

        @block.vector
        def _(v):
            for f in p.q["vector"]:
                f(v)

        @block.sync
        def _(s_):
            for f in p.q["sync"]:
                f(s_)

        @block.gpsimd
        def _(g):
            for f in p.q["gpsimd"]:
                f(g)

    ctx.close()
    return nc


_CACHE = {}


def kernel(**inputs):
    maps = host_in_maps(inputs)
    if "nc" not in _CACHE:
        _CACHE["nc"] = build_nc()
    nc = _CACHE["nc"]
    res = run_bass_kernel_spmd(nc, maps, core_ids=list(range(8)))
    outs = [np.asarray(res.results[i]["out"]) for i in range(8)]
    full = np.concatenate(outs, axis=1)
    return np.ascontiguousarray(full.T).reshape(1, S, DM)


# version 20
# speedup vs baseline: 1.1595x; 1.0003x over previous
import math
import numpy as np
import ml_dtypes
from contextlib import ExitStack

import concourse.bass as bass
import concourse.mybir as mybir
from concourse.bass_utils import run_bass_kernel_spmd

F32 = mybir.dt.float32
BF16 = mybir.dt.bfloat16
AF = mybir.ActivationFunctionType
ALU = mybir.AluOpType
ds = bass.ds

PARANOID = False
S = 8192
DM = 2048
NOC = 16
TG = 512
NTG = S // TG
OWN = 1024
OTG = 2
CSH = 12.0
SCL = 128.0 ** -0.5
LI = 0.8 - 0.6 * math.exp(-0.3)
EPS = 1e-5

ENGS = ("tensor", "scalar", "vector", "sync", "gpsimd")
ESEM = {"tensor": "PE", "scalar": "SC", "vector": "VE", "sync": "dW", "gpsimd": "dG"}


def _f32(a):
    return np.ascontiguousarray(a, dtype=np.float32)


def _tiles(wT, n):
    K = wT.shape[0]
    out = np.empty((n, 128, K // 128, 128), np.float32)
    for j in range(n):
        out[j] = wT[:, j * 128:(j + 1) * 128].reshape(K // 128, 128, 128).transpose(1, 0, 2)
    return _f32(out)


def _swz(a):
    return _f32(np.asarray(a).reshape(NOC, 128, -1).transpose(1, 0, 2))


def host_in_maps(inputs):
    x = np.asarray(inputs["L0_attn_hidden_states"], dtype=np.float32)[0]
    xT = _f32(x.T)
    wqkv = np.asarray(inputs["L0_attn_Wqkv_w"])
    wqkv_b = np.asarray(inputs["L0_attn_Wqkv_b"])
    outw_t = _tiles(_f32(np.asarray(inputs["L0_attn_out_w"]).T), NOC)
    outb = _f32(np.asarray(inputs["L0_attn_out_b"]).reshape(NOC, 128).T)
    sublnw = _f32(np.asarray(inputs["L0_attn_subln_w"]).reshape(2, 128).T)
    lam = {k: _f32(np.asarray(inputs[f"L0_attn_lambda_{k}"]).reshape(128, 1))
           for k in ("q1", "k1", "q2", "k2")}
    wq = np.asarray(inputs["L0_ffn_wq"]); wk = np.asarray(inputs["L0_ffn_wk"])
    wv = np.asarray(inputs["L0_ffn_wv"])
    wo_t = _tiles(_f32(np.asarray(inputs["L0_ffn_wo"]).T), NOC)
    wr_s = _swz(np.asarray(inputs["L0_ffn_w_router"]).T)
    wd_s = _swz(np.asarray(inputs["L0_ffn_w_down"]).T)
    wuT = _f32(np.asarray(inputs["L0_ffn_w_up"]).T)
    ff1_t = _tiles(_f32(np.asarray(inputs["L0_ffn_w_ff1"]).T), 64)
    ff2_t = _tiles(_f32(np.asarray(inputs["L0_ffn_w_ff2"]).T), NOC)
    pv = {k: _f32(np.asarray(inputs["L0_ffn_" + k]).reshape(NOC, 128).T)
          for k in ("ln1_scale", "ln1_bias", "ln2_scale", "ln2_bias", "simple_bias")}

    km = np.arange(128)[:, None]
    tm = np.arange(TG)[None, :]
    masks = np.concatenate(
        [((km + 128 * o) <= tm).astype(np.float32) for o in range(4)],
        axis=1).astype(ml_dtypes.bfloat16)

    maps = []
    for i in range(8):
        h = i; g = h // 2
        rows = np.concatenate([
            np.arange(h * 256, h * 256 + 256),
            np.arange(2048 + g * 256, 2048 + g * 256 + 256),
            np.arange(3072 + g * 256, 3072 + g * 256 + 256)])
        wqkvT = wqkv[rows].T
        bsl = wqkv_b[rows]
        jr = np.arange(2 * i * 128, (2 * i + 2) * 128)
        ffnqkvT = np.concatenate([wq[jr], wk[jr], wv[jr]], axis=0).T
        maps.append({
            "xT": xT, "x_ownT": _f32(xT[:, i * OWN:(i + 1) * OWN]),
            "wqkv_s": _swz(wqkvT), "qkvb": _f32(bsl[:512].reshape(4, 128).T),
            "vbias": _f32(bsl[512:].reshape(1, 256)),
            "outw_t": outw_t, "outb": outb, "sublnw": sublnw,
            "lamq1": lam["q1"], "lamk1": lam["k1"],
            "lamq2": lam["q2"], "lamk2": lam["k2"],
            "ffnqkv_s": _swz(ffnqkvT), "wo_t": wo_t,
            "wr_s": wr_s, "wd_s": wd_s, "wuT": wuT,
            "ff1_t": ff1_t, "ff2_t": ff2_t,
            "ln1s": pv["ln1_scale"], "ln1b": pv["ln1_bias"],
            "ln2s": pv["ln2_scale"], "ln2b": pv["ln2_bias"],
            "sbias": pv["simple_bias"], "masks": np.ascontiguousarray(masks),
        })
    return maps


class Prog:

    def __init__(self, nc, ctx, aliases):
        self.nc = nc
        self.ctx = ctx
        self.q = {e: [] for e in ENGS}
        self.sem = {}
        self.cnt = {}
        for s in ("PE", "SC", "VE", "dW", "dG", "CC"):
            self.sem[s] = ctx.enter_context(nc.semaphore(s))
            self.cnt[s] = 0
        self.writer = {}
        self.readers = {}
        self.waited = {e: {} for e in ENGS}
        self.alias = aliases

    def canon(self, res):
        return self.alias.get(res, (res,))

    def _wait(self, eng, sem, val, skip_eng):
        if skip_eng == eng:
            return
        if self.waited[eng].get(sem, -1) >= val:
            return
        self.waited[eng][sem] = val
        h = self.sem[sem]
        self.q[eng].append(lambda e, h=h, val=val: e.wait_ge(h, val))

    def emit(self, eng, fns, reads=(), writes=(), sem=None, amt=1):
        sem = sem or ESEM[eng]
        if PARANOID:
            for s2, v2 in list(self.cnt.items()):
                if v2 > 0:
                    self._wait(eng, s2, v2, None)
        creads = [c for r in reads for c in self.canon(r)]
        cwrites = [c for r in writes for c in self.canon(r)]
        for r in creads:
            w = self.writer.get(r)
            if w:
                self._wait(eng, w[0], w[1], w[2])
        for r in cwrites:
            w = self.writer.get(r)
            if w:
                self._wait(eng, w[0], w[1], w[2])
            for s2, (v2, sk) in self.readers.get(r, {}).items():
                self._wait(eng, s2, v2, sk)
        self.cnt[sem] += amt
        val = self.cnt[sem]
        h = self.sem[sem]
        assert fns
        for f in fns[:-1]:
            self.q[eng].append(f)
        last = fns[-1]
        self.q[eng].append(lambda e, last=last, h=h, amt=amt: last(e).then_inc(h, amt))
        skip = None
        for r in cwrites:
            self.writer[r] = (sem, val, skip)
            self.readers[r] = {}
        for r in creads:
            d = self.readers.setdefault(r, {})
            old = d.get(sem)
            if old is None or old[0] < val:
                d[sem] = (val, skip)
        return val

    def chan(self, name):
        if name not in self.sem:
            self.sem[name] = self.ctx.enter_context(self.nc.semaphore(name[:31]))
            self.cnt[name] = 0
        return name

    def dma(self, eng, out, in_, reads=(), writes=()):
        key = self.canon(writes[0])[0] if writes else "misc"
        sem = self.chan("D" + eng[0] + "_" + key)
        return self.emit(eng, [lambda e: e.dma_start(out=out, in_=in_)],
                         reads=reads, writes=writes, sem=sem, amt=16)


def build_nc(dbg=()):
    nc = bass.Bass("TRN2", target_bir_lowering=False)
    A = {}
    def din(name, shape, dt=F32):
        A[name] = nc.dram_tensor(name, shape, dt, kind="ExternalInput").ap()
    din("xT", [DM, S]); din("x_ownT", [DM, OWN])
    din("wqkv_s", [128, NOC, 768]); din("qkvb", [128, 4]); din("vbias", [1, 256])
    din("outw_t", [NOC, 128, NOC, 128]); din("outb", [128, NOC]); din("sublnw", [128, 2])
    for k in ("lamq1", "lamk1", "lamq2", "lamk2"):
        din(k, [128, 1])
    din("ffnqkv_s", [128, NOC, 768]); din("wo_t", [NOC, 128, NOC, 128])
    din("wr_s", [128, NOC, 3]); din("wd_s", [128, NOC, 32]); din("wuT", [32, DM])
    din("ff1_t", [64, 128, NOC, 128]); din("ff2_t", [NOC, 128, 64, 128])
    for k in ("ln1s", "ln1b", "ln2s", "ln2b", "sbias"):
        din(k, [128, NOC])
    din("masks", [128, 4 * TG], BF16)
    out_d = nc.dram_tensor("out", [DM, OWN], F32, kind="ExternalOutput").ap()

    ag_o_in = nc.dram_tensor("ag_o_in", [16, 256, TG], BF16, kind="Internal").ap()
    ag_o_out = nc.dram_tensor("ag_o_out", [16, DM, TG], BF16, kind="Internal",
                              addr_space="Shared").ap()
    ag_xn_in = nc.dram_tensor("ag_xn_in", [2, DM, TG], BF16, kind="Internal").ap()
    ag_xn_out = nc.dram_tensor("ag_xn_out", [2, 8 * DM, TG], BF16, kind="Internal",
                               addr_space="Shared").ap()
    ag_cx_in = nc.dram_tensor("ag_cx_in", [16, 256, TG], BF16, kind="Internal").ap()
    ag_cx_out = nc.dram_tensor("ag_cx_out", [16, DM, TG], BF16, kind="Internal",
                               addr_space="Shared").ap()
    y_spill = nc.dram_tensor("y_spill", [DM, OWN], F32, kind="Internal").ap()
    ff1_bf = nc.dram_tensor("ff1_bf", [64, 128, NOC * 128], BF16, kind="Internal").ap()
    ff2_bf = nc.dram_tensor("ff2_bf", [NOC, 128, 64 * 128], BF16, kind="Internal").ap()
    outw_bf = nc.dram_tensor("outw_bf", [NOC, 128, NOC * 128], BF16, kind="Internal").ap()
    wo_bf = nc.dram_tensor("wo_bf", [NOC, 128, NOC * 128], BF16, kind="Internal").ap()

    dbg_d = {}
    for name, shape, dt in (
        ("o", [256, S], BF16), ("y", [DM, OWN], F32), ("xn", [DM, OWN], BF16),
        ("cx", [256, S], BF16), ("k1", [128, 2 * S], BF16),
        ("v1", [128, 64 * 256], BF16), ("q1", [128, 4 * TG], BF16),
        ("ng", [128, 1], F32), ("lg", [128, 4], F32), ("l2", [1, 8], F32),
        ("pt", [128, 2048], BF16), ("c1", [128, 2 * TG], F32),
        ("d1", [1, 2 * TG], F32), ("c2", [128, 2 * TG], F32),
        ("of", [128, 2 * TG], F32), ("os", [128, 2 * TG], BF16),
        ("qg", [128, 4 * TG], BF16),
    ):
        if name in dbg:
            dbg_d[name] = nc.dram_tensor("dbg_" + name, shape, dt,
                                         kind="ExternalOutput").ap()

    ctx = ExitStack()
    e = ctx.enter_context
    R1 = e(nc.sbuf_tensor("R1", [128, 16384], BF16))
    R2 = e(nc.sbuf_tensor("R2", [128, 16384], BF16))
    R3 = e(nc.sbuf_tensor("R3", [128, 16384], BF16))
    R4 = e(nc.sbuf_tensor("R4", [128, 8192], BF16))
    R5 = e(nc.sbuf_tensor("R5", [128, 8192], BF16))
    R7 = e(nc.sbuf_tensor("R7", [128, 1024], BF16))
    R8 = e(nc.sbuf_tensor("R8", [128, 1024], F32))
    R9 = e(nc.sbuf_tensor("R9", [128, 2048], BF16))

    def rv(R, a, b, pat=None, **kw):
        ap = R[:, a:b]
        return ap.rearrange(pat, **kw) if pat else ap

    K_sb = rv(R1, 0, 16384, "p (c s) -> p c s", c=2)
    h_lo = rv(R1, 0, 16384, "p (c t) -> p c t", c=32)
    V_sb = rv(R2, 0, 16384, "p (c s) -> p c s", c=64)
    h_hi = rv(R2, 0, 16384, "p (c t) -> p c t", c=32)
    wtVa = rv(R2, 0, 2048, "p (c k) -> p c k", c=NOC)
    wtVb = rv(R2, 8192, 8192 + 2048, "p (c k) -> p c k", c=NOC)
    wqkv_sb = rv(R3, 0, 12288, "p (c o) -> p c o", c=NOC)
    q_sb = rv(R3, 12288, 14336, "p (c t) -> p c t", c=4)
    PT_sb = rv(R3, 14336, 16384, "p (c t) -> p c t", c=2)
    w2a = rv(R3, 0, 8192, "p (c k) -> p c k", c=64)
    w2b = rv(R3, 8192, 16384, "p (c k) -> p c k", c=64)
    z_sb = R3[0:32, 14336:14848]
    panA = rv(R4, 0, 8192, "p (c t) -> p c t", c=NOC)
    wtPa = rv(R4, 0, 2048, "p (c k) -> p c k", c=NOC)
    wtPb = rv(R4, 4096, 4096 + 2048, "p (c k) -> p c k", c=NOC)
    panB = rv(R5, 0, 8192, "p (c t) -> p c t", c=NOC)
    xn2_sb = rv(R5, 0, 8192, "p (c t) -> p c t", c=NOC)
    ostg = rv(R7, 0, 1024, "p (c t) -> p c t", c=2)
    Mb_sb = rv(R7, 0, 1024, "p (c t) -> p c t", c=2)
    ctx2_sb = rv(R8, 0, 1024, "p (c t) -> p c t", c=2)
    B2_sb = rv(R8, 0, 1024, "p (c t) -> p c t", c=2)
    masks_sb = rv(R9, 0, 2048, "p (c t) -> p c t", c=4)
    wu_sb = R9[0:32, 0:2048]

    F1 = e(nc.sbuf_tensor("F1", [128, NOC, TG], F32))
    F1_bf_view = F1.bitcast(BF16)[:, :, :].rearrange("p c t -> p (c t)")
    ctx1_sb = e(nc.sbuf_tensor("ctx1_sb", [128, 2, TG], F32))
    o_f32 = e(nc.sbuf_tensor("o_f32", [128, 2, TG], F32))
    cast_sb = e(nc.sbuf_tensor("cast_sb", [128, 2, TG], BF16))
    yoc = e(nc.sbuf_tensor("yoc", [128, 2, TG], F32))
    PTf_sb = yoc.bitcast(BF16)
    wr_b = e(nc.sbuf_tensor("wr_b", [128, NOC, 3], BF16))
    wd_b = e(nc.sbuf_tensor("wd_b", [128, NOC, 32], BF16))
    rowf = e(nc.sbuf_tensor("rowf", [1, 8 * TG], F32))
    rowb = e(nc.sbuf_tensor("rowb", [1, 4 * TG], BF16))
    qkvb_sb = e(nc.sbuf_tensor("qkvb_sb", [128, 4], F32))
    vbias_sb = e(nc.sbuf_tensor("vbias_sb", [1, 256], BF16))
    outb_sb = e(nc.sbuf_tensor("outb_sb", [128, NOC], F32))
    parm_sb = e(nc.sbuf_tensor("parm_sb", [128, 5, NOC], F32))
    sublnw2 = e(nc.sbuf_tensor("sublnw2", [128, 2], F32))
    lam_sb = e(nc.sbuf_tensor("lam_sb", [128, 4], F32))
    negl_pp = e(nc.sbuf_tensor("negl_pp", [128, 1], F32))
    ones_bf = e(nc.sbuf_tensor("ones_bf", [128, 1], BF16))
    ones1r_bf = e(nc.sbuf_tensor("ones1r_bf", [1, 128], BF16))
    ones1r_f = e(nc.sbuf_tensor("ones1r_f", [1, 128], F32))
    cexp = e(nc.sbuf_tensor("cexp", [128, 1], F32))
    czero = e(nc.sbuf_tensor("czero", [128, 1], F32))
    zrow = e(nc.sbuf_tensor("zrow", [1, 1], F32))
    epsrow = e(nc.sbuf_tensor("epsrow", [1, 1], F32))

    ALIASES = {
        "V": ("Va", "Vb"),
        "wqkv": ("wqkvA", "wqkvB"),
        "panA": ("panAa", "panAb"),
        "h": ("K", "Va", "Vb"),
        "h_lo": ("K",), "h_hi": ("Va", "Vb"),
        "w2a_": ("wqkvA",), "w2b_": ("wqkvB", "q", "PT0", "PT1"),
        "wtVa_": ("Va",), "wtVb_": ("Vb",),
        "wtPa_": ("panAa",), "wtPb_": ("panAb",),
        "xn2": ("panB",),
        "Mb": ("ostg",),
        "PTf0": ("yoc0",), "PTf1": ("yoc1",),
        "rows": ("rowsA", "rowsB"),
        "rowb": ("rowbA", "rowbB"),
        "B2": ("ctx2",),
        "z": ("PT0",),
        "wu": ("masks",),
    }

    pairA = e(nc.psum_tensor("pairA", [128, 1024], F32))
    pairB = e(nc.psum_tensor("pairB", [128, 1024], F32))
    ctxA = e(nc.psum_tensor("ctxA", [128, 512], F32))
    ctxB = e(nc.psum_tensor("ctxB", [128, 512], F32))
    auxP = e(nc.psum_tensor("auxP", [128, 512], F32))
    bcst = e(nc.psum_tensor("bcst", [128, 512], F32))

    p = Prog(nc, ctx, ALIASES)

    def frow(i):
        return rowf[0:1, i * TG:(i + 1) * TG]
    def brow(i):
        return rowb[0:1, i * TG:(i + 1) * TG]

    p.dma("sync", qkvb_sb[:, :], A["qkvb"], writes=["params"])
    p.dma("gpsimd", vbias_sb[:, :], A["vbias"], writes=["vbias_bf"])
    p.dma("sync", outb_sb[:, :], A["outb"], writes=["params"])
    for j, k in enumerate(("ln1s", "ln1b", "ln2s", "ln2b", "sbias")):
        p.dma("sync", parm_sb[:, j, :], A[k], writes=["params"])
    p.dma("sync", sublnw2[:, :], A["sublnw"], writes=["sublnw_raw"])
    for j, k in enumerate(("lamq1", "lamk1", "lamq2", "lamk2")):
        p.dma("sync", lam_sb[:, j:j + 1], A[k], writes=["lam"])
    p.dma("sync", masks_sb[:, :, :], A["masks"].rearrange("p (o t) -> p o t", o=4),
          writes=["masks"])
    p.dma("gpsimd", wqkv_sb[:, :, :], A["wqkv_s"], writes=["wqkv"])
    p.dma("gpsimd", wr_b[:, :, :], A["wr_s"], writes=["wr"])
    p.dma("gpsimd", wd_b[:, :, :], A["wd_s"], writes=["wd"])

    def memsets(v):
        v.memset(ones_bf[:, :], 1.0)
        v.memset(ones1r_bf[:, :], 1.0)
        v.memset(ones1r_f[:, :], 1.0)
        v.memset(cexp[:, :], -CSH)
        v.memset(czero[:, :], 0.0)
        v.memset(zrow[:, :], 0.0)
        return v.memset(epsrow[:, :], EPS)
    p.emit("vector", [memsets], writes=["consts"])
    p.emit("vector",
           [lambda v: v.tensor_scalar_mul(sublnw2[:, :], sublnw2[:, :], 1.0 - LI)],
           reads=["sublnw_raw"], writes=["sublnw_raw"])

    p.emit("tensor",
           [lambda t: t.matmul(auxP[0:1, 0:1], lam_sb[:, 0:1], lam_sb[:, 1:2],
                               start=True, stop=True)],
           reads=["lam"], writes=["aux"])
    p.emit("tensor",
           [lambda t: t.matmul(bcst[0:1, 0:1], lam_sb[:, 2:3], lam_sb[:, 3:4],
                               start=True, stop=True)],
           reads=["lam"], writes=["bcst"])
    p.emit("scalar",
           [lambda s: s.activation(rowf[0:1, 0:1], auxP[0:1, 0:1], AF.Exp,
                                   bias=zrow[0:1, 0:1], scale=1.0)],
           reads=["aux", "consts"], writes=["rows"])
    p.emit("scalar",
           [lambda s: s.activation(rowf[0:1, 1:2], bcst[0:1, 0:1], AF.Exp,
                                   bias=zrow[0:1, 0:1], scale=1.0)],
           reads=["bcst", "consts"], writes=["rows"])
    p.emit("vector",
           [lambda v: v.tensor_sub(rowf[0:1, 2:3], rowf[0:1, 1:2], rowf[0:1, 0:1])],
           reads=["rows"], writes=["rows"])
    p.emit("vector",
           [lambda v: v.tensor_scalar_add(rowf[0:1, 2:3], rowf[0:1, 2:3], -LI)],
           reads=["rows"], writes=["rows"])
    p.emit("tensor",
           [lambda t: t.matmul(bcst[:, 0:1], ones1r_f[0:1, :], rowf[0:1, 2:3],
                               start=True, stop=True)],
           reads=["rows", "consts"], writes=["bcst"])
    p.emit("vector", [lambda v: v.tensor_copy(negl_pp[:, :], bcst[:, 0:1])],
           reads=["bcst"], writes=["negl"])

    def bcast_row(src_row, hs, lsl, bank_ap, bank_res):
        hi, lo = brow(hs), brow(lsl)
        p.emit("vector", [lambda v: v.tensor_copy(hi, src_row)],
               reads=["rows"], writes=["rowb"])
        p.emit("vector", [lambda v: v.tensor_sub(lo, src_row, hi)],
               reads=["rows", "rowb"], writes=["rowb"])
        p.emit("tensor",
               [lambda t: t.matmul(bank_ap, ones1r_bf[0:1, :], hi, start=True, stop=False),
                lambda t: t.matmul(bank_ap, ones1r_bf[0:1, :], lo, start=False, stop=True)],
               reads=["rowb", "consts"], writes=[bank_res])

    def vres(tile):
        return "Va" if tile < 32 else "Vb"

    stage_jobs = []
    for c in range(16):
        stage_jobs.append((A["ff1_t"].rearrange("n p c k -> n p (c k)")[4 * c:4 * c + 4],
                           ff1_bf[4 * c:4 * c + 4], 4))
    for c in range(NOC):
        stage_jobs.append((A["ff2_t"].rearrange("n p c k -> n p (c k)")[c:c + 1],
                           ff2_bf[c:c + 1], 1))
    for c in range(4):
        stage_jobs.append((A["outw_t"].rearrange("n p c k -> n p (c k)")[4 * c:4 * c + 4],
                           outw_bf[4 * c:4 * c + 4], 4))
        stage_jobs.append((A["wo_t"].rearrange("n p c k -> n p (c k)")[4 * c:4 * c + 4],
                           wo_bf[4 * c:4 * c + 4], 4))
    stage_state = {"i": 0, "par": 0}

    def emit_stage(njobs):
        for _ in range(njobs):
            i = stage_state["i"]
            if i >= len(stage_jobs):
                return
            stage_state["i"] += 1
            srcap, dstap, ntile = stage_jobs[i]
            par = stage_state["par"]; stage_state["par"] ^= 1
            bv = F1_bf_view[:, par * 8192:(par + 1) * 8192]
            p.dma("gpsimd", bv.rearrange("p (n e) -> p n e", n=ntile),
                  srcap.rearrange("n p e -> p n e"),
                  reads=["F1"], writes=["F1", "stgb%d" % par])
            p.dma("sync", dstap.rearrange("n p e -> p n e"),
                  bv.rearrange("p (n e) -> p n e", n=ntile),
                  reads=["F1", "stgb%d" % par], writes=["wstaged"])

    PAIR_RES = [("pairA0", "pairA1"), ("pairB0", "pairB1")]
    PAIR_AP = [pairA, pairB]
    gp = {"par": 0}

    def x_panel_load(tg, buf, bufres):
        for qq in range(4):
            src = A["xT"][qq * 512:(qq + 1) * 512, tg * TG:(tg + 1) * TG]
            p.dma("gpsimd", buf[:, qq * 4:(qq + 1) * 4, :],
                  src.rearrange("(c p) t -> p c t", p=128), writes=[bufres])

    def xn_panel_load(tg, buf, bufres):
        rk, ch = tg // 2, tg % 2
        srcap = ag_xn_out[ch, rk * DM:(rk + 1) * DM, :]
        for qq in range(4):
            p.dma("sync", buf[:, qq * 4:(qq + 1) * 4, :],
                  srcap[qq * 512:(qq + 1) * 512, :].rearrange("(c p) t -> p c t", p=128),
                  reads=[f"ag_xn_out{ch}"], writes=[bufres])

    pidbox = {}

    def dyn_panel_load(tg_own, buf, bufres, src_dram, tag):
        reads = [f"ag_{tag}_out{j}" for j in range(16)]
        for qq in range(4):
            def f(e_, qq=qq, tg_own=tg_own, buf=buf, src_dram=src_dram):
                if "pid" not in pidbox:
                    pidbox["pid"] = e_.partition_id()
                src = src_dram[ds(pidbox["pid"] * 2 + tg_own, 1),
                               qq * 512:(qq + 1) * 512, :]
                return e_.dma_start(out=buf[:, qq * 4:(qq + 1) * 4, :],
                                    in_=src.rearrange("o (c p) t -> p (o c) t", p=128))
            p.emit("gpsimd", [f], reads=reads, writes=[bufres],
                   sem=p.chan("Dg_" + p.canon(bufres)[0]), amt=16)

    def proj_qk(tg, buf, bufres, with_bias):
        banks = [("pairB0", pairB[:, 0:512]), ("pairB1", pairB[:, 512:1024]),
                 ("pairB0", pairB[:, 0:512]), ("pairB1", pairB[:, 512:1024])]
        for j in range(4):
            res, bank = banks[j]
            fns = [(lambda t, fc=fc, j=j, bank=bank, buf=buf:
                    t.matmul(bank, wqkv_sb[:, fc, j * 128:(j + 1) * 128],
                             buf[:, fc, :], start=(fc == 0), stop=(fc == NOC - 1)))
                   for fc in range(NOC)]
            p.emit("tensor", fns, reads=["wqkv", bufres], writes=[res])
            if j < 2:
                dst, dres = q_sb[:, j, :], "q"
            else:
                dst, dres = K_sb[:, j - 2, tg * TG:(tg + 1) * TG], "K"
            if with_bias:
                p.emit("vector",
                       [lambda v, dst=dst, bank=bank, j=j:
                        v.tensor_scalar_add(dst, bank, qkvb_sb[:, j:j + 1])],
                       reads=[res, "params"], writes=[dres])
            else:
                p.emit("vector",
                       [lambda v, dst=dst, bank=bank: v.tensor_copy(dst, bank)],
                       reads=[res], writes=[dres])

    def proj_v(tg, buf, bufres, with_bias):
        for q4 in range(4):
            res = "pairA0" if q4 % 2 == 0 else "pairA1"
            bank_ap = pairA[:, 0:256] if q4 % 2 == 0 else pairA[:, 512:768]
            fns = [(lambda t, fc=fc, q4=q4, bank_ap=bank_ap, buf=buf:
                    t.matmul(bank_ap, buf[:, fc, q4 * 128:(q4 + 1) * 128],
                             wqkv_sb[:, fc, 512:768], start=(fc == 0),
                             stop=(fc == NOC - 1 and not with_bias)))
                   for fc in range(NOC)]
            if with_bias:
                fns.append(lambda t, bank_ap=bank_ap:
                           t.matmul(bank_ap, ones1r_bf[0:1, :], vbias_sb[0:1, :],
                                    start=False, stop=True))
            p.emit("tensor", fns, reads=["wqkv", bufres, "vbias_bf", "consts"],
                   writes=[res])
            tile = tg * 4 + q4
            p.emit("vector",
                   [lambda v, tile=tile, bank_ap=bank_ap:
                    v.tensor_copy(V_sb[:, tile, :], bank_ap)],
                   reads=[res], writes=[vres(tile)])

    def attn_matrix(tg, m, is_ffn):
        npair = (tg + 1) * 2

        def emit_scores(j):
            par = gp["par"]; gp["par"] ^= 1
            prA = PAIR_AP[par]
            r0, r1 = PAIR_RES[par]
            kb_a, kb_b = 2 * j, 2 * j + 1
            p.emit("tensor",
                   [lambda t, kb=kb_a, m=m, prA=prA:
                    t.matmul(prA[:, 0:512], K_sb[:, m, kb * 128:(kb + 1) * 128],
                             q_sb[:, m, :], start=True, stop=True),
                    lambda t, kb=kb_b, m=m, prA=prA:
                    t.matmul(prA[:, 512:1024], K_sb[:, m, kb * 128:(kb + 1) * 128],
                             q_sb[:, m, :], start=True, stop=True)],
                   reads=["K", "q"], writes=[r0, r1])
            ptres = f"PT{par}"
            p.emit("scalar",
                   [lambda s_, par=par, prA=prA:
                    s_.activation(PT_sb[:, par, :], prA[:, :], AF.Exp,
                                  bias=cexp[:, 0:1], scale=SCL)],
                   reads=[r0, r1, "consts"], writes=[ptres])
            if "pt" in dbg_d and not is_ffn and tg == 3 and m == 0 and j < 2:
                p.dma("sync", dbg_d["pt"][:, j * 1024:(j + 1) * 1024],
                      PT_sb[:, par, :], reads=[ptres], writes=["dbg"])
            if j >= npair - 2:
                off = kb_a - tg * 4
                p.emit("vector",
                       [lambda v, par=par, off=off:
                        v.tensor_mul(PT_sb[:, par, 0:512], PT_sb[:, par, 0:512],
                                     masks_sb[:, off, :]),
                        lambda v, par=par, off=off:
                        v.tensor_mul(PT_sb[:, par, 512:1024], PT_sb[:, par, 512:1024],
                                     masks_sb[:, off + 1, :])],
                       reads=[ptres, "masks"], writes=[ptres])
            return par

        def emit_ctx(j, par, first, lastp):
            kb_a, kb_b = 2 * j, 2 * j + 1
            ptres = f"PT{par}"
            fns = []
            for half, kb in ((0, kb_a), (1, kb_b)):
                st = first and half == 0
                sp = lastp and half == 1
                if is_ffn:
                    fns.append(lambda t, kb=kb, m=m, par=par, half=half, st=st, sp=sp:
                               t.matmul(ctxA[:, :], V_sb[:, kb, m * 128:(m + 1) * 128],
                                        PT_sb[:, par, half * 512:(half + 1) * 512],
                                        start=st, stop=sp))
                else:
                    for ech in range(2):
                        bank = ctxA if ech == 0 else ctxB
                        fns.append(lambda t, kb=kb, ech=ech, par=par, half=half,
                                   st=st, sp=sp, bank=bank:
                                   t.matmul(bank[:, :],
                                            V_sb[:, kb, ech * 128:(ech + 1) * 128],
                                            PT_sb[:, par, half * 512:(half + 1) * 512],
                                            start=st, stop=sp))
            fns.append(lambda t, par=par, first=first:
                       t.matmul(auxP[0:1, :], ones_bf[:, 0:1],
                                PT_sb[:, par, 0:512],
                                start=first, stop=False))
            fns.append(lambda t, par=par, lastp=lastp:
                       t.matmul(auxP[0:1, :], ones_bf[:, 0:1],
                                PT_sb[:, par, 512:1024],
                                start=False, stop=lastp))
            wr = (["ctxA", "aux"] if is_ffn else ["ctxA", "ctxB", "aux"])
            rd = [ptres, "consts", vres(kb_a), vres(kb_b)]
            p.emit("tensor", fns, reads=rd, writes=wr)

        return emit_scores, emit_ctx, npair

    def fr(P, k):
        return rowf[0:1, (4 * P + k) * TG:(4 * P + k + 1) * TG]

    def bw(P, k):
        return rowb[0:1, (2 * P + k) * TG:(2 * P + k + 1) * TG]

    def rres(P):
        return "rowsA" if P == 0 else "rowsB"

    def bres_(P):
        return "rowbA" if P == 0 else "rowbB"

    def seg_rden(P, slots):
        def go():
            p.emit("scalar",
                   [(lambda s_, d=d, r=r: s_.activation(fr(P, r), fr(P, d), AF.Ln,
                                                        bias=zrow[0:1, 0:1], scale=1.0))
                    for d, r in slots],
                   reads=[rres(P), "consts"], writes=[rres(P)])
            p.emit("scalar",
                   [(lambda s_, r=r: s_.activation(fr(P, r), fr(P, r), AF.Exp,
                                                   bias=zrow[0:1, 0:1], scale=-1.0))
                    for d, r in slots],
                   reads=[rres(P), "consts"], writes=[rres(P)])
        return go

    def seg_bcast(P, src_slot):
        def go():
            p.emit("vector", [lambda v: v.tensor_copy(bw(P, 0), fr(P, src_slot))],
                   reads=[rres(P)], writes=[bres_(P)])
            p.emit("vector", [lambda v: v.tensor_sub(bw(P, 1), fr(P, src_slot), bw(P, 0))],
                   reads=[rres(P), bres_(P)], writes=[bres_(P)])
            p.emit("tensor",
                   [lambda t: t.matmul(bcst[:, :], ones1r_bf[0:1, :], bw(P, 0),
                                       start=True, stop=False),
                    lambda t: t.matmul(bcst[:, :], ones1r_bf[0:1, :], bw(P, 1),
                                       start=False, stop=True)],
                   reads=[bres_(P), "consts"], writes=["bcst"])
        return go

    def seg_row_bcast_b(P, bslot):
        def go():
            p.emit("tensor",
                   [lambda t: t.matmul(bcst[:, :], ones1r_bf[0:1, :], bw(P, bslot),
                                       start=True, stop=True)],
                   reads=[bres_(P), "consts"], writes=["bcst"])
        return go

    def evac_ffn_head(tg, m):
        P = tg % 2
        p.emit("vector",
               [lambda v, m=m: v.tensor_copy(o_f32[:, m, :], ctxA[:, :])],
               reads=["ctxA"], writes=["o32"])
        p.emit("vector",
               [lambda v, P=P, m=m: v.tensor_copy(fr(P, 2 * m), auxP[0:1, :])],
               reads=["aux"], writes=[rres(P)])

    def ffn_tail_segs(tg, m, dstD, dstO, tag):
        P = tg % 2
        segs = [seg_rden(P, [(2 * m, 2 * m + 1)]),
                seg_bcast(P, 2 * m + 1)]

        def s_mult():
            p.emit("vector",
                   [lambda v, m=m: v.tensor_mul(ostg[:, m, :], o_f32[:, m, :],
                                                bcst[:, :])],
                   reads=["o32", "bcst"], writes=["ostg"])
        segs.append(s_mult)
        if m == 1:
            def s_store():
                p.dma("sync", dstD[tg].rearrange("(c p) t -> p c t", p=128),
                      ostg[:, :, :], reads=["ostg"], writes=[f"ag_{tag}_in{tg}"])
                p.emit("gpsimd",
                       [lambda g: g.collective_compute(
                           "AllGather", ALU.bypass,
                           replica_groups=[list(range(8))],
                           ins=[dstD[tg]], outs=[dstO[tg]])],
                       reads=[f"ag_{tag}_in{tg}"], writes=[f"ag_{tag}_out{tg}"],
                       sem=p.chan(f"CC{tag}{tg}"), amt=1)
            segs.append(s_store)
        return segs

    def evac_attn1_p1(tg):
        P = tg % 2
        p.emit("vector",
               [lambda v: v.tensor_copy(ctx1_sb[:, 0, :], ctxA[:, :]),
                lambda v: v.tensor_copy(ctx1_sb[:, 1, :], ctxB[:, :])],
               reads=["ctxA", "ctxB"], writes=["ctx1"])
        p.emit("vector",
               [lambda v, P=P: v.tensor_copy(fr(P, 0), auxP[0:1, :])],
               reads=["aux"], writes=[rres(P)])

    def evac_attn1_p2_head(tg):
        P = tg % 2
        p.emit("vector",
               [lambda v: v.tensor_copy(ctx2_sb[:, 0, :], ctxA[:, :]),
                lambda v: v.tensor_copy(ctx2_sb[:, 1, :], ctxB[:, :])],
               reads=["ctxA", "ctxB"], writes=["ctx2"])
        p.emit("vector",
               [lambda v, P=P: v.tensor_copy(fr(P, 1), auxP[0:1, :])],
               reads=["aux"], writes=[rres(P)])

    def attn1_tail_segs(tg):
        P = tg % 2
        segs = [seg_rden(P, [(0, 2), (1, 3)]),
                seg_bcast(P, 2)]

        def s_t1():
            for ech in range(2):
                p.emit("vector",
                       [lambda v, ech=ech: v.tensor_mul(o_f32[:, ech, :],
                                                        ctx1_sb[:, ech, :], bcst[:, :])],
                       reads=["ctx1", "bcst"], writes=["o32"])
        segs.append(s_t1)
        segs.append(seg_bcast(P, 3))

        def s_u2():
            for ech in range(2):
                p.emit("vector",
                       [lambda v, ech=ech: v.tensor_mul(ctx2_sb[:, ech, :],
                                                        ctx2_sb[:, ech, :], bcst[:, :])],
                       reads=["ctx2", "bcst"], writes=["ctx2"])
                p.emit("vector",
                       [lambda v, ech=ech: v.scalar_tensor_tensor(
                           o_f32[:, ech, :], ctx2_sb[:, ech, :], negl_pp[:, 0:1],
                           o_f32[:, ech, :], ALU.mult, ALU.add)],
                       reads=["ctx2", "negl", "o32"], writes=["o32"])
        segs.append(s_u2)

        def s_sq():
            p.emit("vector",
                   [lambda v: v.tensor_copy(cast_sb[:, 0, :], o_f32[:, 0, :]),
                    lambda v: v.tensor_copy(cast_sb[:, 1, :], o_f32[:, 1, :])],
                   reads=["o32"], writes=["cast"])
            p.emit("vector",
                   [lambda v: v.tensor_mul(cast_sb[:, 0, :], cast_sb[:, 0, :],
                                           cast_sb[:, 0, :]),
                    lambda v: v.tensor_mul(cast_sb[:, 1, :], cast_sb[:, 1, :],
                                           cast_sb[:, 1, :])],
                   reads=["cast"], writes=["cast"])
        segs.append(s_sq)

        def s_ssq():
            p.emit("tensor",
                   [lambda t: t.matmul(bcst[0:1, :], ones_bf[:, 0:1], cast_sb[:, 0, :],
                                       start=True, stop=False),
                    lambda t: t.matmul(bcst[0:1, :], ones_bf[:, 0:1], cast_sb[:, 1, :],
                                       start=False, stop=True)],
                   reads=["cast", "consts"], writes=["bcst"])
        segs.append(s_ssq)

        def s_rstd():
            p.emit("scalar",
                   [lambda s_, P=P: s_.activation(fr(P, 2), bcst[0:1, :], AF.Ln,
                                                  bias=epsrow[0:1, 0:1],
                                                  scale=1.0 / 256.0)],
                   reads=["bcst", "consts"], writes=[rres(P)])
            p.emit("scalar",
                   [lambda s_, P=P: s_.activation(fr(P, 2), fr(P, 2), AF.Exp,
                                                  bias=zrow[0:1, 0:1], scale=-0.5)],
                   reads=[rres(P), "consts"], writes=[rres(P)])
        segs.append(s_rstd)
        segs.append(seg_bcast(P, 2))

        def s_subln():
            p.emit("vector",
                   [(lambda v, ech=ech: v.scalar_tensor_tensor(
                       ostg[:, ech, :], o_f32[:, ech, :], sublnw2[:, ech:ech + 1],
                       bcst[:, :], ALU.mult, ALU.mult)) for ech in range(2)],
                   reads=["o32", "bcst", "sublnw_raw"], writes=["ostg"])
        segs.append(s_subln)

        def s_store():
            p.dma("sync", ag_o_in[tg].rearrange("(c p) t -> p c t", p=128),
                  ostg[:, :, :], reads=["ostg"], writes=[f"ag_o_in{tg}"])
            p.emit("gpsimd",
                   [lambda g: g.collective_compute(
                       "AllGather", ALU.bypass,
                       replica_groups=[list(range(8))],
                       ins=[ag_o_in[tg]], outs=[ag_o_out[tg]])],
                   reads=[f"ag_o_in{tg}"], writes=[f"ag_o_out{tg}"],
                   sem=p.chan(f"CCo{tg}"), amt=1)
        segs.append(s_store)
        return segs

    def attn_phase(is_ffn):
        from collections import deque
        pend = deque()

        def hook():
            if pend:
                pend.popleft()()

        for tg in range(2):
            buf, bufres = (panA, "panA") if tg % 2 == 0 else (panB, "panB")
            if is_ffn:
                xn_panel_load(tg, buf, bufres)
            else:
                x_panel_load(tg, buf, bufres)
        for tg in range(NTG):
            buf, bufres = (panA, "panA") if tg % 2 == 0 else (panB, "panB")
            proj_qk(tg, buf, bufres, with_bias=not is_ffn)
            proj_v(tg, buf, bufres, with_bias=not is_ffn)
            if tg + 2 < NTG:
                nbuf, nbres = (panA, "panA") if tg % 2 == 0 else (panB, "panB")
                if is_ffn:
                    xn_panel_load(tg + 2, nbuf, nbres)
                else:
                    x_panel_load(tg + 2, nbuf, nbres)
            if not is_ffn:
                emit_stage(2 if tg < 4 else 5)
            es0, ec0, np0 = attn_matrix(tg, 0, is_ffn)
            es1, ec1, np1 = attn_matrix(tg, 1, is_ffn)

            def order(npx):
                if npx <= 2:
                    return list(range(npx))
                return [npx - 2, npx - 1] + list(range(npx - 2))
            stream = [(es0, ec0, j, pos == 0, pos == np0 - 1)
                      for pos, j in enumerate(order(np0))] + \
                     [(es1, ec1, j, pos == 0, pos == np1 - 1)
                      for pos, j in enumerate(order(np1))]
            par_prev = stream[0][0](stream[0][2])
            for idx in range(len(stream)):
                par_j = par_prev
                if idx + 1 < len(stream):
                    par_prev = stream[idx + 1][0](stream[idx + 1][2])
                stream[idx][1](stream[idx][2], par_j, stream[idx][3], stream[idx][4])
                hook()
                if idx == np0 - 1:
                    while pend:
                        pend.popleft()()
                    if is_ffn:
                        evac_ffn_head(tg, 0)
                        pend.extend(ffn_tail_segs(tg, 0, ag_cx_in, ag_cx_out, "cx"))
                    else:
                        evac_attn1_p1(tg)
            while pend:
                pend.popleft()()
            if is_ffn:
                evac_ffn_head(tg, 1)
                pend.extend(ffn_tail_segs(tg, 1, ag_cx_in, ag_cx_out, "cx"))
            else:
                evac_attn1_p2_head(tg)
                pend.extend(attn1_tail_segs(tg))
        while pend:
            pend.popleft()()

    attn_phase(is_ffn=False)
    emit_stage(len(stage_jobs))

    if "k1" in dbg_d:
        p.dma("gpsimd", dbg_d["k1"], R1[:, :], reads=["K"], writes=["dbg"])
    if "v1" in dbg_d:
        p.dma("gpsimd", dbg_d["v1"], R2[:, :], reads=["V"], writes=["dbg"])
    if "q1" in dbg_d:
        p.dma("gpsimd", dbg_d["q1"], R3[:, 12288:14336], reads=["q"], writes=["dbg"])

    def ln_rows(sum_res, sum_ap, ssq_res, ssq_ap, n):
        p.emit("vector",
               [lambda v: v.tensor_scalar_mul(frow(0), sum_ap, 1.0 / n)],
               reads=[sum_res], writes=["rows"])
        p.emit("vector",
               [lambda v: v.tensor_mul(frow(1), frow(0), frow(0))],
               reads=["rows"], writes=["rows"])
        p.emit("vector",
               [lambda v: v.scalar_tensor_tensor(frow(1), ssq_ap, 1.0 / n, frow(1),
                                                 ALU.mult, ALU.subtract)],
               reads=[ssq_res, "rows"], writes=["rows"])
        p.emit("scalar",
               [lambda s_: s_.activation(frow(2), frow(1), AF.Ln,
                                         bias=epsrow[0:1, 0:1], scale=1.0)],
               reads=["rows", "consts"], writes=["rows"])
        p.emit("scalar",
               [lambda s_: s_.activation(frow(2), frow(2), AF.Exp,
                                         bias=zrow[0:1, 0:1], scale=-0.5)],
               reads=["rows", "consts"], writes=["rows"])

    def mean_rstd_bcast():
        bcast_row(frow(0), 0, 1, bcst[:, :], "bcst")
        p.emit("vector", [lambda v: v.tensor_copy(B2_sb[:, 0, :], bcst[:, :])],
               reads=["bcst"], writes=["B2"])
        bcast_row(frow(2), 2, 3, bcst[:, :], "bcst")
        p.emit("vector", [lambda v: v.tensor_copy(B2_sb[:, 1, :], bcst[:, :])],
               reads=["bcst"], writes=["B2"])

    for tgo in range(OTG):
        dyn_panel_load(tgo, panA, "panA", ag_o_out, "o")
        def c_mm(oc):
            wbuf = wtVa if oc % 2 == 0 else wtVb
            wres = "wtVa_" if oc % 2 == 0 else "wtVb_"
            p.dma("sync", wbuf[:, :, :],
                  outw_bf[oc].rearrange("p (c k) -> p c k", c=NOC),
                  reads=["wstaged"], writes=[wres])
            bank, bres = (ctxA, "ctxA") if oc % 2 == 0 else (ctxB, "ctxB")
            fns = [(lambda t, fc=fc, wbuf=wbuf, bank=bank:
                    t.matmul(bank[:, :], wbuf[:, fc, :], panA[:, fc, :],
                             start=(fc == 0), stop=(fc == NOC - 1)))
                   for fc in range(NOC)]
            p.emit("tensor", fns, reads=[wres, "panA"], writes=[bres])
            xb = yoc[:, oc % 2, :]
            xres = "yoc0" if oc % 2 == 0 else "yoc1"
            p.dma("sync", xb, A["x_ownT"][oc * 128:(oc + 1) * 128,
                                          tgo * TG:(tgo + 1) * TG], writes=[xres])
            return bank, bres, xb, xres

        def c_evac(oc, bank, bres, xb, xres):
            p.emit("vector",
                   [lambda v, oc=oc, bank=bank, xb=xb:
                    v.scalar_tensor_tensor(F1[:, oc, :], bank[:, :],
                                           outb_sb[:, oc:oc + 1], xb,
                                           ALU.add, ALU.add)],
                   reads=[bres, xres, "params"], writes=["F1"])
            p.emit("vector",
                   [lambda v, oc=oc: v.tensor_copy(cast_sb[:, 0, :], F1[:, oc, :])],
                   reads=["F1"], writes=["cast"])
            p.emit("vector",
                   [lambda v: v.tensor_mul(cast_sb[:, 1, :], cast_sb[:, 0, :],
                                           cast_sb[:, 0, :])],
                   reads=["cast"], writes=["cast"])
            p.emit("tensor",
                   [lambda t, oc=oc: t.matmul(auxP[0:1, :], ones_bf[:, 0:1],
                                              cast_sb[:, 0, :], start=(oc == 0),
                                              stop=(oc == NOC - 1)),
                    lambda t, oc=oc: t.matmul(bcst[0:1, :], ones_bf[:, 0:1],
                                              cast_sb[:, 1, :], start=(oc == 0),
                                              stop=(oc == NOC - 1))],
                   reads=["cast", "consts"], writes=["aux", "bcst"])

        prevc = c_mm(0)
        for oc in range(NOC):
            curc = prevc
            if oc + 1 < NOC:
                prevc = c_mm(oc + 1)
            c_evac(oc, *curc)
        ln_rows("aux", auxP[0:1, :], "bcst", bcst[0:1, :], DM)
        mean_rstd_bcast()
        for oc in range(NOC):
            p.emit("vector",
                   [lambda v, oc=oc: v.tensor_sub(F1[:, oc, :], F1[:, oc, :],
                                                  B2_sb[:, 0, :])],
                   reads=["B2", "F1"], writes=["F1"])
            p.emit("vector",
                   [lambda v, oc=oc: v.tensor_mul(F1[:, oc, :], F1[:, oc, :],
                                                  B2_sb[:, 1, :])],
                   reads=["B2", "F1"], writes=["F1"])
            p.emit("vector",
                   [lambda v, oc=oc: v.tensor_scalar(panB[:, oc, :], F1[:, oc, :],
                                                     parm_sb[:, 0, oc:oc + 1],
                                                     parm_sb[:, 1, oc:oc + 1],
                                                     ALU.mult, ALU.add)],
                   reads=["F1", "params"], writes=["panB"])
        p.dma("sync", y_spill[:, tgo * TG:(tgo + 1) * TG].rearrange(
                  "(c p) t -> p c t", p=128), F1[:, :, :],
              reads=["F1"], writes=["y_spill"])
        p.dma("sync", ag_xn_in[tgo].rearrange("(c p) t -> p c t", p=128),
              panB[:, :, :], reads=["panB"], writes=[f"ag_xn_in{tgo}"])
        p.emit("gpsimd",
               [lambda g, tgo=tgo: g.collective_compute(
                   "AllGather", ALU.bypass, replica_groups=[list(range(8))],
                   ins=[ag_xn_in[tgo]], outs=[ag_xn_out[tgo]])],
               reads=[f"ag_xn_in{tgo}"], writes=[f"ag_xn_out{tgo}"],
               sem=p.chan(f"CCxn{tgo}"), amt=1)


    p.dma("gpsimd", wqkv_sb[:, :, :], A["ffnqkv_s"], writes=["wqkv"])
    attn_phase(is_ffn=True)


    p.dma("gpsimd", wu_sb[0:32, :], A["wuT"], writes=["wu"])
    for tgo in range(OTG):
        dyn_panel_load(tgo, panA, "panA", ag_cx_out, "cx")
        for qq in range(4):
            p.dma("gpsimd", panB[:, qq * 4:(qq + 1) * 4, :],
                  y_spill[qq * 512:(qq + 1) * 512, tgo * TG:(tgo + 1) * TG]
                  .rearrange("(c p) t -> p c t", p=128),
                  reads=["y_spill"], writes=["panB"])
        rbank = [("ctxA", ctxA[0:1, :]), ("ctxB", ctxB[0:1, :]), ("aux", auxP[0:1, :])]
        for rj in range(3):
            res, ap_ = rbank[rj]
            fns = [(lambda t, fc=fc, rj=rj, ap_=ap_:
                    t.matmul(ap_, wr_b[:, fc, rj:rj + 1], panB[:, fc, :],
                             start=(fc == 0), stop=(fc == NOC - 1)))
                   for fc in range(NOC)]
            p.emit("tensor", fns, reads=["wr", "panB"], writes=[res])
        p.emit("vector",
               [lambda v: v.tensor_copy(frow(0), ctxA[0:1, :]),
                lambda v: v.tensor_copy(frow(1), ctxB[0:1, :]),
                lambda v: v.tensor_copy(frow(2), auxP[0:1, :])],
               reads=["ctxA", "ctxB", "aux"], writes=["rows"])
        p.emit("vector",
               [lambda v: v.tensor_tensor(frow(3), frow(0), frow(1), ALU.is_ge),
                lambda v: v.tensor_tensor(frow(4), frow(0), frow(2), ALU.is_ge)],
               reads=["rows"], writes=["rows"])
        p.emit("vector", [lambda v: v.tensor_mul(frow(3), frow(3), frow(4))],
               reads=["rows"], writes=["rows"])
        p.emit("vector",
               [lambda v: v.tensor_tensor(frow(4), frow(1), frow(0), ALU.is_gt),
                lambda v: v.tensor_tensor(frow(5), frow(1), frow(2), ALU.is_ge)],
               reads=["rows"], writes=["rows"])
        p.emit("vector", [lambda v: v.tensor_mul(frow(4), frow(4), frow(5))],
               reads=["rows"], writes=["rows"])
        p.emit("vector",
               [lambda v: v.tensor_copy(brow(0), frow(3)),
                lambda v: v.tensor_copy(brow(1), frow(4))],
               reads=["rows"], writes=["rowb"])
        for mj in range(2):
            p.emit("tensor",
                   [lambda t, mj=mj: t.matmul(bcst[:, :], ones1r_bf[0:1, :],
                                              brow(mj), start=True, stop=True)],
                   reads=["rowb", "consts"], writes=["bcst"])
            p.emit("vector",
                   [lambda v, mj=mj: v.tensor_copy(Mb_sb[:, mj, :], bcst[:, :])],
                   reads=["bcst"], writes=["Mb"])
        fns = [(lambda t, fc=fc: t.matmul(auxP[0:32, :], wd_b[:, fc, :],
                                          panB[:, fc, :], start=(fc == 0),
                                          stop=(fc == NOC - 1)))
               for fc in range(NOC)]
        p.emit("tensor", fns, reads=["wd", "panB"], writes=["aux"])
        p.emit("vector", [lambda v: v.tensor_copy(z_sb[0:32, :], auxP[0:32, :])],
               reads=["aux"], writes=["z"])
        def w_mm(oc):
            wbuf = wtVa if oc % 2 == 0 else wtVb
            wres = "wtVa_" if oc % 2 == 0 else "wtVb_"
            p.dma("sync", wbuf[:, :, :],
                  wo_bf[oc].rearrange("p (c k) -> p c k", c=NOC),
                  reads=["wstaged"], writes=[wres])
            bank, bres = (ctxA, "ctxA") if oc % 2 == 0 else (ctxB, "ctxB")
            fns = [(lambda t, fc=fc, wbuf=wbuf, bank=bank:
                    t.matmul(bank[:, :], wbuf[:, fc, :], panA[:, fc, :],
                             start=(fc == 0), stop=(fc == NOC - 1)))
                   for fc in range(NOC)]
            p.emit("tensor", fns, reads=[wres, "panA"], writes=[bres])
            xb = yoc[:, oc % 2, :]
            xres = "yoc0" if oc % 2 == 0 else "yoc1"
            p.dma("scalar", xb, y_spill[oc * 128:(oc + 1) * 128,
                                      tgo * TG:(tgo + 1) * TG],
                  reads=["y_spill"], writes=[xres])
            return bank, bres, xb, xres

        def w_evac(oc, bank, bres, xb, xres):
            p.emit("vector",
                   [lambda v, oc=oc, bank=bank, xb=xb:
                    v.tensor_add(F1[:, oc, :], bank[:, :], xb)],
                   reads=[bres, xres], writes=["F1"])
            p.emit("vector",
                   [lambda v, oc=oc: v.tensor_copy(cast_sb[:, 0, :], F1[:, oc, :])],
                   reads=["F1"], writes=["cast"])
            p.emit("vector",
                   [lambda v: v.tensor_mul(cast_sb[:, 1, :], cast_sb[:, 0, :],
                                           cast_sb[:, 0, :])],
                   reads=["cast"], writes=["cast"])
            p.emit("tensor",
                   [lambda t, oc=oc: t.matmul(pairB[0:1, 512:1024], ones_bf[:, 0:1],
                                              cast_sb[:, 0, :], start=(oc == 0),
                                              stop=(oc == NOC - 1)),
                    lambda t, oc=oc: t.matmul(pairB[0:1, 0:512], ones_bf[:, 0:1],
                                              cast_sb[:, 1, :], start=(oc == 0),
                                              stop=(oc == NOC - 1))],
                   reads=["cast", "consts"], writes=["pairB1", "pairB0"])

        prevw = w_mm(0)
        for oc in range(NOC):
            curw = prevw
            if oc + 1 < NOC:
                prevw = w_mm(oc + 1)
            w_evac(oc, *curw)
        ln_rows("pairB1", pairB[0:1, 512:1024], "pairB0", pairB[0:1, 0:512], DM)
        mean_rstd_bcast()
        for oc in range(NOC):
            p.emit("vector",
                   [lambda v, oc=oc: v.tensor_sub(o_f32[:, 0, :], F1[:, oc, :],
                                                  B2_sb[:, 0, :])],
                   reads=["F1", "B2"], writes=["o32"])
            p.emit("vector",
                   [lambda v: v.tensor_mul(o_f32[:, 0, :], o_f32[:, 0, :],
                                           B2_sb[:, 1, :])],
                   reads=["o32", "B2"], writes=["o32"])
            p.emit("vector",
                   [lambda v, oc=oc: v.tensor_scalar(xn2_sb[:, oc, :], o_f32[:, 0, :],
                                                     parm_sb[:, 2, oc:oc + 1],
                                                     parm_sb[:, 3, oc:oc + 1],
                                                     ALU.mult, ALU.add)],
                   reads=["o32", "params"], writes=["xn2"])
        for hc in range(64):
            wbuf = wtPa if hc % 2 == 0 else wtPb
            wres = "wtPa_" if hc % 2 == 0 else "wtPb_"
            p.dma("sync", wbuf[:, :, :],
                  ff1_bf[hc].rearrange("p (c k) -> p c k", c=NOC),
                  reads=["wstaged"], writes=[wres])
            bank, bres = (ctxA, "ctxA") if hc % 2 == 0 else (ctxB, "ctxB")
            fns = [(lambda t, fc=fc, wbuf=wbuf, bank=bank:
                    t.matmul(bank[:, :], wbuf[:, fc, :], xn2_sb[:, fc, :],
                             start=(fc == 0), stop=(fc == NOC - 1)))
                   for fc in range(NOC)]
            p.emit("tensor", fns, reads=[wres, "xn2"], writes=[bres])
            hdst = h_lo[:, hc, :] if hc < 32 else h_hi[:, hc - 32, :]
            hres = "h_lo" if hc < 32 else "h_hi"
            p.emit("scalar",
                   [lambda s_, hdst=hdst, bank=bank:
                    s_.activation(hdst, bank[:, :], AF.Gelu_apprx_tanh,
                                  bias=czero[:, 0:1], scale=1.0)],
                   reads=[bres, "consts"], writes=[hres])
        def f_mm(oc):
            wbuf2 = w2a if oc % 2 == 0 else w2b
            wres2 = "w2a_" if oc % 2 == 0 else "w2b_"
            p.dma("sync", wbuf2[:, :, :],
                  ff2_bf[oc].rearrange("p (c k) -> p c k", c=64),
                  reads=["wstaged"], writes=[wres2])
            bank_ap = [pairA[:, 0:512], pairA[:, 512:1024], ctxA[:, :], ctxB[:, :]][oc % 4]
            bres = ["pairA0", "pairA1", "ctxA", "ctxB"][oc % 4]
            fns = [(lambda t, hc=hc, wbuf2=wbuf2, bank_ap=bank_ap:
                    t.matmul(bank_ap, wbuf2[:, hc, :],
                             (h_lo[:, hc, :] if hc < 32 else h_hi[:, hc - 32, :]),
                             start=(hc == 0), stop=(hc == 63)))
                   for hc in range(64)]
            p.emit("tensor", fns, reads=[wres2, "h_lo", "h_hi"], writes=[bres])
            up_ap = pairB[:, 0:512] if oc % 2 == 0 else pairB[:, 512:1024]
            upres = "pairB0" if oc % 2 == 0 else "pairB1"
            p.emit("tensor",
                   [lambda t, oc=oc, up_ap=up_ap:
                    t.matmul(up_ap, wu_sb[0:32, oc * 128:(oc + 1) * 128],
                             z_sb[0:32, :], start=True, stop=True)],
                   reads=["wu", "z"], writes=[upres])
            xb = yoc[:, oc % 2, :]
            xres = "yoc0" if oc % 2 == 0 else "yoc1"
            p.dma("scalar", xb, y_spill[oc * 128:(oc + 1) * 128,
                                      tgo * TG:(tgo + 1) * TG],
                  reads=["y_spill"], writes=[xres])
            return bank_ap, bres, up_ap, upres, xb, xres

        def f_evac(oc, bank_ap, bres, up_ap, upres, xb, xres):
            p.emit("vector",
                   [lambda v, oc=oc, bank_ap=bank_ap:
                    v.tensor_add(F1[:, oc, :], bank_ap, F1[:, oc, :])],
                   reads=[bres, "F1"], writes=["F1"])
            p.emit("vector",
                   [lambda v, oc=oc, xb=xb: v.scalar_tensor_tensor(
                       ctx1_sb[:, 0, :], xb, parm_sb[:, 4, oc:oc + 1], F1[:, oc, :],
                       ALU.add, ALU.subtract)],
                   reads=[xres, "params", "F1"], writes=["ctx1"])
            p.emit("vector",
                   [lambda v: v.tensor_mul(ctx1_sb[:, 0, :], ctx1_sb[:, 0, :],
                                           Mb_sb[:, 0, :])],
                   reads=["ctx1", "Mb"], writes=["ctx1"])
            p.emit("vector",
                   [lambda v, xb=xb, up_ap=up_ap:
                    v.tensor_add(ctx1_sb[:, 1, :], up_ap, xb)],
                   reads=[upres, xres], writes=["ctx1"])
            p.emit("vector",
                   [lambda v, oc=oc: v.tensor_sub(ctx1_sb[:, 1, :], ctx1_sb[:, 1, :],
                                                  F1[:, oc, :])],
                   reads=["ctx1", "F1"], writes=["ctx1"])
            p.emit("vector",
                   [lambda v: v.tensor_mul(ctx1_sb[:, 1, :], ctx1_sb[:, 1, :],
                                           Mb_sb[:, 1, :])],
                   reads=["ctx1", "Mb"], writes=["ctx1"])
            p.emit("vector",
                   [lambda v, oc=oc, xb=xb: v.tensor_add(F1[:, oc, :], F1[:, oc, :], xb)],
                   reads=["F1", xres], writes=["F1"])
            p.emit("vector",
                   [lambda v: v.tensor_add(ctx1_sb[:, 0, :], ctx1_sb[:, 0, :],
                                           ctx1_sb[:, 1, :])],
                   reads=["ctx1"], writes=["ctx1"])
            p.emit("vector",
                   [lambda v, oc=oc: v.tensor_add(F1[:, oc, :], F1[:, oc, :],
                                                  ctx1_sb[:, 0, :])],
                   reads=["F1", "ctx1"], writes=["F1"])
            p.emit("vector",
                   [lambda v, oc=oc: v.tensor_copy(cast_sb[:, 0, :], F1[:, oc, :])],
                   reads=["F1"], writes=["cast"])
            p.emit("vector",
                   [lambda v: v.tensor_mul(cast_sb[:, 1, :], cast_sb[:, 0, :],
                                           cast_sb[:, 0, :])],
                   reads=["cast"], writes=["cast"])
            p.emit("tensor",
                   [lambda t, oc=oc: t.matmul(auxP[0:1, :], ones_bf[:, 0:1],
                                              cast_sb[:, 0, :], start=(oc == 0),
                                              stop=(oc == NOC - 1)),
                    lambda t, oc=oc: t.matmul(bcst[0:1, :], ones_bf[:, 0:1],
                                              cast_sb[:, 1, :], start=(oc == 0),
                                              stop=(oc == NOC - 1))],
                   reads=["cast", "consts"], writes=["aux", "bcst"])

        prevf = f_mm(0)
        for oc in range(NOC):
            curf = prevf
            if oc + 1 < NOC:
                prevf = f_mm(oc + 1)
            f_evac(oc, *curf)
        ln_rows("aux", auxP[0:1, :], "bcst", bcst[0:1, :], DM)
        mean_rstd_bcast()
        for oc in range(NOC):
            p.emit("vector",
                   [lambda v, oc=oc: v.tensor_sub(F1[:, oc, :], F1[:, oc, :],
                                                  B2_sb[:, 0, :])],
                   reads=["B2", "F1"], writes=["F1"])
            p.emit("vector",
                   [lambda v, oc=oc: v.tensor_mul(F1[:, oc, :], F1[:, oc, :],
                                                  B2_sb[:, 1, :])],
                   reads=["B2", "F1"], writes=["F1"])
        p.dma("sync", out_d[:, tgo * TG:(tgo + 1) * TG].rearrange(
                  "(c p) t -> p c t", p=128), F1[:, :, :],
              reads=["F1"], writes=["outd"])

    if "lg" in dbg_d:
        p.dma("sync", dbg_d["lg"], lam_sb[:, :], reads=["lam"], writes=["dbg"])
    if "ng" in dbg_d:
        p.dma("sync", dbg_d["ng"], negl_pp[:, :], reads=["negl"], writes=["dbg"])
    if "y" in dbg_d:
        p.dma("gpsimd", dbg_d["y"], y_spill, reads=["y_spill"], writes=["dbg"])

    fin = {s: v for s, v in p.cnt.items()
           if (s.startswith("Ds_") or s.startswith("Dg_") or s.startswith("CC")
               or s.startswith("PF")) and v > 0}
    def final(e_):
        for s, v in fin.items():
            if v:
                e_.wait_ge(p.sem[s], v)
    p.q["sync"].append(final)

    with nc.Block() as block:
        @block.tensor
        def _(t):
            for f in p.q["tensor"]:
                f(t)

        @block.scalar
        def _(s_):
            for f in p.q["scalar"]:
                f(s_)

        @block.vector
        def _(v):
            for f in p.q["vector"]:
                f(v)

        @block.sync
        def _(s_):
            for f in p.q["sync"]:
                f(s_)

        @block.gpsimd
        def _(g):
            for f in p.q["gpsimd"]:
                f(g)

    ctx.close()
    return nc


_CACHE = {}


def kernel(**inputs):
    maps = host_in_maps(inputs)
    if "nc" not in _CACHE:
        _CACHE["nc"] = build_nc()
    nc = _CACHE["nc"]
    res = run_bass_kernel_spmd(nc, maps, core_ids=list(range(8)))
    outs = [np.asarray(res.results[i]["out"]) for i in range(8)]
    full = np.concatenate(outs, axis=1)
    return np.ascontiguousarray(full.T).reshape(1, S, DM)
